# version 03
# baseline (speedup 1.0000x reference)
import numpy as np
from contextlib import ExitStack

import concourse.bass as bass
import concourse.mybir as mybir
from concourse.bass_utils import run_bass_kernel_spmd

F32 = mybir.dt.float32
BF16 = mybir.dt.bfloat16
AF = mybir.ActivationFunctionType
ALU = mybir.AluOpType
AX = mybir.AxisListType

T = 8192
D = 1024
NCH = D // 128
NEG = -60000.0
H = 16
HD = 64
DEPTH = 6
DI = 2730
NCORES = 8
FVALS = (1e-5, 1.0 / 32.0, 8.0, 0.125)

ENGINES = ("sync", "gpsimd", "vector", "scalar", "tensor")


def bc_mid(ap2, n):
    assert len(ap2.ap) == 2
    return bass.AP(ap2.tensor, ap2.offset, [list(ap2.ap[0]), [0, n], list(ap2.ap[1])])


class Plan:
    def __init__(self):
        self.ops = {e: [] for e in ENGINES}
        self.cnt = {}
        self.semnames = []

    def sem(self, name):
        if name not in self.cnt:
            self.cnt[name] = 0
            self.semnames.append(name)
        return name

    def emit(self, eng, fn, waits=(), inc=None, incn=1):
        if inc is not None:
            self.sem(inc)
            self.cnt[inc] += incn
        self.ops[eng].append((tuple(waits), fn, inc, incn))
        return (inc, self.cnt[inc]) if inc is not None else None

    def chain(self, eng, fn, waits=()):
        s = self.sem(eng + "_self")
        w = list(waits)
        if self.cnt[s] > 0:
            w.append((s, self.cnt[s]))
        return self.emit(eng, fn, w, inc=s)

    def mark(self, eng):
        s = self.sem(eng + "_self")
        return (s, self.cnt[s])

    def dma(self, eng, dst, src, sem, waits=()):
        self.sem(sem)
        fn = lambda e, dst=dst, src=src: e.dma_start(out=dst, in_=src)
        return self.emit(eng, fn, waits, inc=sem, incn=16)

    def wait_only(self, eng, waits):
        fn = lambda e: e.nop()
        self.ops[eng].append((tuple(waits), fn, None, 0))

    def fence(self):
        waits = [(s, self.cnt[s]) for s in self.semnames if self.cnt[s] > 0]
        for e in ENGINES:
            self.wait_only(e, waits)

    def build(self, nc):
        es = ExitStack()
        SEM = {n: es.enter_context(nc.semaphore(n)) for n in self.semnames}
        blk = es.enter_context(nc.Block())
        for eng_name in ENGINES:
            def body(e, eng_name=eng_name):
                wm = {}
                for waits, fn, inc, incn in self.ops[eng_name]:
                    for (s, v) in waits:
                        if wm.get(s, 0) >= v:
                            continue
                        e.wait_ge(SEM[s], v)
                        wm[s] = v
                    ins = fn(e)
                    if inc is not None:
                        ins.then_inc(SEM[inc], incn)
            getattr(blk, eng_name)(body)
        es.close()


class Builder:
    def __init__(self, nc):
        self.nc = nc
        self.P = Plan()
        self.es = ExitStack()
        self.uid = 0

    def dram(self, name, shape, dt, kind="Internal"):
        return self.nc.dram_tensor(name, shape, dt, kind=kind).ap()

    def sb(self, name, shape, dt=F32):
        return self.es.enter_context(self.nc.sbuf_tensor(name, shape, dt))

    def unique(self, prefix):
        self.uid += 1
        return f"{prefix}{self.uid}"

    def copy(self, eng, dst, src, waits=()):
        return self.P.chain(eng, lambda e, d=dst, s=src: e.tensor_copy(d, s), waits)

    def act(self, dst, src, func, bias=0.0, scale=1.0, accum=None, waits=()):
        fn = lambda e, d=dst, s=src, f=func, b=bias, sc=scale, a=accum: e.activation(
            d, s, f, bias=b, scale=sc, accum_out=a)
        return self.P.chain("scalar", fn, waits)

    def scopy(self, dst, src, waits=()):
        return self.P.chain("scalar", lambda e, d=dst, s=src: e.copy(d, s), waits)

    def tt(self, eng, dst, a, b, op, waits=()):
        return self.P.chain(eng, lambda e, d=dst, x=a, y=b, o=op: e.tensor_tensor(d, x, y, o), waits)

    def ts(self, eng, dst, a, s1, op, s2=None, op2=None, accum=None, waits=()):
        def fn(e, d=dst, x=a, s1=s1, s2=s2, op=op, op2=op2, acc=accum):
            if s2 is None and acc is None:
                return e.tensor_single_scalar(d, x, s1, op)
            return e.tensor_scalar(d, x, s1, s2, op, op2 if op2 is not None else ALU.bypass,
                                   accum_out=acc)
        return self.P.chain(eng, fn, waits)

    def stt(self, eng, dst, in0, scalar, in1, op0, op1, waits=()):
        fn = lambda e, d=dst, a=in0, s=scalar, b=in1, o0=op0, o1=op1: e.scalar_tensor_tensor(
            d, a, s, b, o0, o1)
        return self.P.chain(eng, fn, waits)

    def red(self, dst, src, op, negate=False, waits=()):
        fn = lambda e, d=dst, s=src, o=op, n=negate: e.tensor_reduce(d, s, AX.X, o, negate=n)
        return self.P.chain("vector", fn, waits)

    def recip(self, dst, src, waits=()):
        return self.P.chain("vector", lambda e, d=dst, s=src: e.reciprocal(d, s), waits)

    def memset(self, eng, dst, val, waits=()):
        return self.P.chain(eng, lambda e, d=dst, v=val: e.memset(d, v), waits)

    def scan(self, dst, d0, d1, init, op0, op1, waits=()):
        fn = lambda e, d=dst, a=d0, b=d1, i=init, o0=op0, o1=op1: e.tensor_tensor_scan(
            d, a, b, i, o0, o1)
        return self.P.chain("vector", fn, waits)

    def mm(self, out, lhsT, rhs, start, stop, waits=(), inc=None):
        fn = lambda e, o=out, l=lhsT, r=rhs, st=start, sp=stop: e.matmul(
            o, l, r, start=st, stop=sp)
        return self.P.emit("tensor", fn, waits, inc=inc)

    def tr(self, out, src, ident, waits=(), inc=None):
        fn = lambda e, o=out, s=src, i=ident: e.transpose(o, s, i)
        return self.P.emit("tensor", fn, waits, inc=inc)



def build_program(stop_after="all", debug_taps=()):
    nc = bass.Bass("TRN2", target_bir_lowering=False, debug=False)
    B = Builder(nc)
    P = B.P
    P.sem("pe")

    xin = B.dram("xin", [D, T], F32, kind="ExternalInput")
    aQw = B.dram("aQw", [D, D], F32, kind="ExternalInput")
    aKw = B.dram("aKw", [D, D], F32, kind="ExternalInput")
    aVw = B.dram("aVw", [D, D], F32, kind="ExternalInput")
    bWqT = B.dram("bWqT", [D, D], F32, kind="ExternalInput")
    bWkT = B.dram("bWkT", [D, D], F32, kind="ExternalInput")
    bWvT = B.dram("bWvT", [D, D], F32, kind="ExternalInput")
    bWoT = B.dram("bWoT", [D, D], F32, kind="ExternalInput")
    bWk_orig = B.dram("bWk_orig", [D, D], F32, kind="ExternalInput")
    bwsblk = B.dram("bwsblk", [D, H], F32, kind="ExternalInput")
    fWqkv = B.dram("fWqkv", [DEPTH, D, 3 * D], F32, kind="ExternalInput")
    fWo = B.dram("fWo", [DEPTH, D, D], F32, kind="ExternalInput")
    fW1 = B.dram("fW1", [DEPTH, D, 2 * DI], F32, kind="ExternalInput")
    fW2 = B.dram("fW2", [DEPTH, DI, D], F32, kind="ExternalInput")
    fb1 = B.dram("fb1", [DEPTH, 128, 44], F32, kind="ExternalInput")
    fb2 = B.dram("fb2", [DEPTH, 128, NCH], F32, kind="ExternalInput")
    flng = B.dram("flng", [DEPTH, 128, NCH], F32, kind="ExternalInput")
    flnb = B.dram("flnb", [DEPTH, 128, NCH], F32, kind="ExternalInput")
    fqs = B.dram("fqs", [DEPTH, 128, 2], F32, kind="ExternalInput")
    fks = B.dram("fks", [DEPTH, 128, 2], F32, kind="ExternalInput")
    tbl_cs = B.dram("tbl_cs", [HD, T], F32, kind="ExternalInput")
    tbl_ss = B.dram("tbl_ss", [HD, T], F32, kind="ExternalInput")
    tbl_ci = B.dram("tbl_ci", [HD, T], F32, kind="ExternalInput")
    tbl_si = B.dram("tbl_si", [HD, T], F32, kind="ExternalInput")
    c_ident = B.dram("c_ident", [128, 128], BF16, kind="ExternalInput")
    c_identf = B.dram("c_identf", [128, 128], F32, kind="ExternalInput")
    c_stair = B.dram("c_stair", [128, 1024], BF16, kind="ExternalInput")
    c_band = B.dram("c_band", [3, 128, 384], BF16, kind="ExternalInput")
    c_prot = B.dram("c_prot", [128, 128], BF16, kind="ExternalInput")
    c_ones1 = B.dram("c_ones1", [128, 1], BF16, kind="ExternalInput")
    c_onesr = B.dram("c_onesr", [1, 128], F32, kind="ExternalInput")
    c_onescf = B.dram("c_onescf", [128, 1], F32, kind="ExternalInput")
    c_iota64 = B.dram("c_iota64", [128, HD], F32, kind="ExternalInput")
    c_hsel = B.dram("c_hsel", [NCH, 128, H], BF16, kind="ExternalInput")
    c_hsel2 = B.dram("c_hsel2", [128, 2], BF16, kind="ExternalInput")
    c_pairsel = B.dram("c_pairsel", [2, 128], F32, kind="ExternalInput")
    c_hselT = B.dram("c_hselT", [H, NCH, 128], F32, kind="ExternalInput")
    c_gsel = B.dram("c_gsel", [128, H], BF16, kind="ExternalInput")
    c_gselT = B.dram("c_gselT", [H, 128], F32, kind="ExternalInput")
    c_fvals = B.dram("c_fvals", [128, 8], F32, kind="ExternalInput")
    c_onesrb = B.dram("c_onesrb", [1, 512], BF16, kind="ExternalInput")
    c_fbr = B.dram("c_fbr", [DEPTH, 2, 30, 128], BF16, kind="ExternalInput")

    out = B.dram("xout", [D, T], F32, kind="ExternalOutput")

    S_a = B.dram("S_a", [D, T], F32)
    S_b = B.dram("S_b", [D, T], F32)
    S0 = B.dram("S0", [D, T], F32)
    S1 = B.dram("S1", [D, T], F32)
    XNB = B.dram("XNB", [D, T], BF16)
    QS = B.dram("QS", [D, T], BF16)
    KS = B.dram("KS", [D, T], BF16)
    KTM = B.dram("KTM", [T, D], BF16)
    VTM = B.dram("VTM", [T, D], BF16)
    RANKB = B.dram("RANKB", [T, H], F32)
    IDXB = B.dram("IDXB", [128, 8 * 1024], F32)

    dbg = {}
    for tap in debug_taps:
        if tap == "xnb":
            dbg["dbg_xnb"] = B.dram("dbg_xnb", [D, T], BF16, kind="ExternalOutput")
        if tap == "qkv":
            dbg["dbg_q"] = B.dram("dbg_q", [D, T], BF16, kind="ExternalOutput")
            dbg["dbg_k"] = B.dram("dbg_k", [D, T], BF16, kind="ExternalOutput")
            dbg["dbg_v"] = B.dram("dbg_v", [T, D], BF16, kind="ExternalOutput")

    def fm(apx):
        return apx.rearrange("(c p) t -> p c t", p=128)

    def tm(apx):
        return apx.rearrange("(s p) d -> p s d", p=128)

    sb_ident = B.sb("sb_ident", [128, 128], BF16)
    sb_identf = B.sb("sb_identf", [128, 128], F32)
    sb_stair = B.sb("sb_stair", [128, 1024], BF16)
    sb_band = B.sb("sb_band", [128, 3, 384], BF16)
    sb_prot = B.sb("sb_prot", [128, 128], BF16)
    sb_ones1 = B.sb("sb_ones1", [128, 1], BF16)
    sb_onesr = B.sb("sb_onesr", [1, 128], F32)
    sb_onescf = B.sb("sb_onescf", [128, 1], F32)
    sb_iota64 = B.sb("sb_iota64", [128, HD], F32)
    sb_hsel = B.sb("sb_hsel", [128, NCH, H], BF16)
    sb_hsel2 = B.sb("sb_hsel2", [128, 2], BF16)
    sb_pairsel = B.sb("sb_pairsel", [2, 128], F32)
    sb_gsel = B.sb("sb_gsel", [128, H], BF16)
    sb_gselT = B.sb("sb_gselT", [H, 128], F32)
    sb_fb1 = B.sb("sb_fb1", [128, DEPTH, 44], F32)
    sb_fb2 = B.sb("sb_fb2", [128, DEPTH, NCH], F32)
    sb_flng = B.sb("sb_flng", [128, DEPTH, NCH], F32)
    sb_flnb = B.sb("sb_flnb", [128, DEPTH, NCH], F32)
    sb_fqs = B.sb("sb_fqs", [128, DEPTH, 2], F32)
    sb_fks = B.sb("sb_fks", [128, DEPTH, 2], F32)
    sb_fvals = B.sb("sb_fvals", [128, 8], F32)
    sb_onesrb = B.sb("sb_onesrb", [65, 512], BF16)
    for _i, _v in enumerate(FVALS):
        nc.const_aps.register(F32, _v, sb_fvals[:, _i:_i + 1])

    P.sem("ldc")
    for dst, src in [
        (sb_ident[:], c_ident[:, :]), (sb_identf[:], c_identf[:, :]),
        (sb_stair[:], c_stair[:, :]),
        (sb_band[:], c_band.rearrange("m p n -> p m n")), (sb_prot[:], c_prot[:, :]),
        (sb_ones1[:], c_ones1[:, :]), (sb_onesr[:], c_onesr[:, :]),
        (sb_onescf[:], c_onescf[:, :]),
        (sb_iota64[:], c_iota64[:, :]), (sb_hsel[:], c_hsel.rearrange("c p h -> p c h")),
        (sb_hsel2[:], c_hsel2[:, :]), (sb_pairsel[:], c_pairsel[:, :]),
        (sb_gsel[:], c_gsel[:, :]), (sb_gselT[:], c_gselT[:, :]),
        (sb_fb1[:], fb1.rearrange("l p n -> p l n")), (sb_fb2[:], fb2.rearrange("l p n -> p l n")),
        (sb_flng[:], flng.rearrange("l p n -> p l n")), (sb_flnb[:], flnb.rearrange("l p n -> p l n")),
        (sb_fqs[:], fqs.rearrange("l p n -> p l n")), (sb_fks[:], fks.rearrange("l p n -> p l n")),
        (sb_fvals[:], c_fvals[:, :]), (sb_onesrb[0:1, :], c_onesrb[:, :]),
        (sb_onesrb[64:65, :], c_onesrb[:, :]),
    ]:
        P.dma("sync", dst, src, "ldc")
    LDC = ("ldc", P.cnt["ldc"])
    for e in ENGINES:
        P.wait_only(e, [LDC])

    TTL = 512
    NTL = T // TTL

    def ln_pass(name, src, src2=None, dst_bf=None, dst_f32=None, dst_sum=None,
                no_norm=False, idx_hook=None):
        with ExitStack() as ph:
            sbp = lambda nm, shp, dt=F32: ph.enter_context(
                nc.sbuf_tensor(B.unique(nm), shp, dt))
            ln_x = [sbp("lnx", [128, NCH, TTL]) for _ in range(2)]
            ln_x2 = [sbp("lnx2", [128, NCH, TTL]) for _ in range(2)] if src2 is not None else None
            ln_xb = [sbp("lnxb", [128, NCH, TTL], BF16) for _ in range(2)]
            ln_ob = [sbp("lnob", [128, NCH, TTL], BF16) for _ in range(2)] if dst_bf is not None else None
            ln_of = [sbp("lnof", [128, NCH, TTL]) for _ in range(2)]
            ln_xq = sbp("lnxq", [128, NCH, TTL], BF16)
            R = sbp("lnrows", [1, 3 * 512])
            ln_mb = sbp("lnmb", [128, TTL])
            ln_rb = sbp("lnrb", [128, TTL])
            ps_s = ph.enter_context(nc.psum_tensor(B.unique("lnS"), [1, TTL], F32))
            ps_q = ph.enter_context(nc.psum_tensor(B.unique("lnQ"), [1, TTL], F32))
            ps_m = ph.enter_context(nc.psum_tensor(B.unique("lnM"), [128, TTL], F32))
            ps_r = ph.enter_context(nc.psum_tensor(B.unique("lnR"), [128, TTL], F32))
            ld = P.sem(name + "_ld")
            st = P.sem(name + "_st")
            slot_rel = [None, None]
            srcv = fm(src)
            src2v = fm(src2) if src2 is not None else None
            dbv = fm(dst_bf) if dst_bf is not None else None
            dfv = fm(dst_f32) if dst_f32 is not None else None
            dsv = fm(dst_sum) if dst_sum is not None else None
            for i in range(NTL):
                sl = i % 2
                t0 = i * TTL
                w = list(slot_rel[sl]) if slot_rel[sl] else []
                P.dma("sync", ln_x[sl][:], srcv[:, :, t0:t0 + TTL], ld, waits=w)
                if src2 is not None:
                    P.dma("sync", ln_x2[sl][:], src2v[:, :, t0:t0 + TTL], ld, waits=w)
                ldw = [(ld, P.cnt[ld])]
                if src2 is not None:
                    B.tt("vector", ln_x[sl][:], ln_x[sl][:], ln_x2[sl][:], ALU.add, waits=ldw)
                    ldw = [P.mark("vector")]
                    if dst_sum is not None:
                        P.dma("sync", dsv[:, :, t0:t0 + TTL], ln_x[sl][:], st, waits=ldw)
                if no_norm:
                    slot_rel[sl] = ldw + [(st, P.cnt[st])]
                    continue
                B.copy("vector", ln_xb[sl][:], ln_x[sl][:], waits=ldw)
                m_xb = P.mark("vector")
                B.act(ln_xq[:], ln_xb[sl][:], AF.Square, waits=[m_xb])
                m_xq = P.mark("scalar")
                for c in range(NCH):
                    B.mm(ps_s[:], sb_ones1[:], ln_xb[sl][:, c, :], c == 0, c == NCH - 1,
                         waits=[m_xb, P.mark("scalar")] if c == 0 else (),
                         inc="pe" if c == NCH - 1 else None)
                pe_s = ("pe", P.cnt["pe"])
                for c in range(NCH):
                    B.mm(ps_q[:], sb_ones1[:], ln_xq[:, c, :], c == 0, c == NCH - 1,
                         waits=[m_xq] if c == 0 else (),
                         inc="pe" if c == NCH - 1 else None)
                pe_q = ("pe", P.cnt["pe"])
                r0 = R[0:1, 0:TTL]
                r1 = R[0:1, 512:512 + TTL]
                r2 = R[0:1, 1024:1024 + TTL]
                B.scopy(r0, ps_s[:], waits=[pe_s, P.mark("vector")])
                B.scopy(r1, ps_q[:], waits=[pe_q])
                m_rows = P.mark("scalar")
                B.ts("vector", r0, r0, 1.0 / D, ALU.mult, waits=[m_rows])
                B.ts("vector", r1, r1, 1.0 / D, ALU.mult)
                B.tt("vector", r2, r0, r0, ALU.mult)
                B.tt("vector", r1, r1, r2, ALU.subtract)
                m_var = P.mark("vector")
                B.act(r2, r1, AF.Ln, bias=1e-5, waits=[m_var])
                B.act(r2, r2, AF.Exp, scale=-0.5)
                m_rstd = P.mark("scalar")
                B.mm(ps_m[:], sb_onesr[:], r0, True, True,
                     waits=[m_var, m_rstd, P.mark("scalar")], inc="pe")
                pe_m = ("pe", P.cnt["pe"])
                B.mm(ps_r[:], sb_onesr[:], r2, True, True, inc="pe")
                pe_r = ("pe", P.cnt["pe"])
                B.scopy(ln_mb[:], ps_m[:], waits=[pe_m, P.mark("vector")])
                B.scopy(ln_rb[:], ps_r[:], waits=[pe_r])
                m_bc = P.mark("scalar")
                B.tt("vector", ln_of[sl][:], ln_x[sl][:], bc_mid(ln_mb[:], NCH),
                     ALU.subtract, waits=[m_bc])
                if dst_f32 is not None or idx_hook is not None:
                    B.tt("vector", ln_of[sl][:], ln_of[sl][:], bc_mid(ln_rb[:], NCH), ALU.mult)
                    m_nf = P.mark("vector")
                    if dst_bf is not None:
                        B.copy("vector", ln_ob[sl][:], ln_of[sl][:])
                    if dst_f32 is not None:
                        P.dma("sync", dfv[:, :, t0:t0 + TTL], ln_of[sl][:], st, waits=[m_nf])
                elif dst_bf is not None:
                    B.tt("vector", ln_ob[sl][:], ln_of[sl][:], bc_mid(ln_rb[:], NCH), ALU.mult)
                m_out = P.mark("vector")
                if dst_bf is not None:
                    P.dma("sync", dbv[:, :, t0:t0 + TTL], ln_ob[sl][:], st, waits=[m_out])
                if idx_hook is not None:
                    idx_hook(i, t0, ln_of[sl], m_out)
                    slot_rel[sl] = [m_out, (st, P.cnt[st]), ("pe", P.cnt["pe"])]
                else:
                    slot_rel[sl] = [m_out, (st, P.cnt[st])]
            P.fence()

    def proj_pass(name, wlist, src_b, outs, tile_cols=512):
        with ExitStack() as ph:
            lw = P.sem(name + "_lw")
            sbw = []
            for wi, (wd, nco) in enumerate(wlist):
                wt = ph.enter_context(nc.sbuf_tensor(B.unique(name + "w"),
                                                     [128, NCH, nco * 128], BF16))
                P.dma("gpsimd", wt[:], wd.rearrange("(c p) o -> p c o", p=128), lw)
                sbw.append(wt)
            LW = (lw, P.cnt[lw])
            nfm = sum(1 for k, _ in outs if k == "fm")
            xb = [ph.enter_context(nc.sbuf_tensor(B.unique(name + "x"),
                                                  [128, NCH, tile_cols], BF16)) for _ in range(2)]
            obs = {}
            for oi, (kind, od) in enumerate(outs):
                if kind == "fm":
                    obs[oi] = [ph.enter_context(nc.sbuf_tensor(
                        B.unique(name + "o"), [128, NCH, tile_cols], BF16)) for _ in range(2)]
                else:
                    obs[oi] = [ph.enter_context(nc.sbuf_tensor(
                        B.unique(name + "ot"), [128, tile_cols // 128, D], BF16)) for _ in range(2)]
            pss = [ph.enter_context(nc.psum_tensor(B.unique(name + "p"), [128, tile_cols], F32))
                   for _ in range(4)]
            bank_rel = [None] * 4
            ld = P.sem(name + "_ld")
            st = P.sem(name + "_st")
            srcv3 = fm(src_b)
            slot_rel = [None, None]
            ntiles = T // tile_cols
            pb = 0
            for i in range(ntiles):
                sl = i % 2
                t0 = i * tile_cols
                w = list(slot_rel[sl]) if slot_rel[sl] else [LW]
                P.dma("sync", xb[sl][:], srcv3[:, :, t0:t0 + tile_cols], ld, waits=w)
                ldw = [(ld, P.cnt[ld])]
                rel_marks = []
                for oi, (kind, od) in enumerate(outs):
                    wt = sbw[oi]
                    ob = obs[oi][sl]
                    if kind == "fm":
                        for co in range(NCH):
                            bk = pb % 4
                            ps = pss[bk]
                            pb += 1
                            w0 = ldw + [LW] + ([bank_rel[bk]] if bank_rel[bk] else [])
                            for ci in range(NCH):
                                B.mm(ps[:], wt[:, ci, co * 128:(co + 1) * 128],
                                     xb[sl][:, ci, :], ci == 0, ci == NCH - 1,
                                     waits=w0 if ci == 0 else (),
                                     inc="pe" if ci == NCH - 1 else None)
                            B.act(ob[:, co, :], ps[:], AF.Copy, waits=[("pe", P.cnt["pe"])])
                            bank_rel[bk] = P.mark("scalar")
                        m_o = P.mark("scalar")
                        P.dma("sync", fm(od)[:, :, t0:t0 + tile_cols], ob[:], st, waits=[m_o])
                        rel_marks.append(m_o)
                    else:
                        nts = tile_cols // 128
                        for s4 in range(nts):
                            for eh in range(2):
                                bk = pb % 4
                                ps = pss[bk]
                                pb += 1
                                w0 = ldw + [LW] + ([bank_rel[bk]] if bank_rel[bk] else [])
                                for ci in range(NCH):
                                    B.mm(ps[:], xb[sl][:, ci, s4 * 128:(s4 + 1) * 128],
                                         wt[:, ci, eh * 512:(eh + 1) * 512],
                                         ci == 0, ci == NCH - 1,
                                         waits=w0 if ci == 0 else (),
                                         inc="pe" if ci == NCH - 1 else None)
                                B.act(ob[:, s4, eh * 512:(eh + 1) * 512], ps[:], AF.Copy,
                                      waits=[("pe", P.cnt["pe"])])
                                bank_rel[bk] = P.mark("scalar")
                        m_o = P.mark("scalar")
                        P.dma("sync", tm(od)[:, i * nts:(i + 1) * nts, :], ob[:], st,
                              waits=[m_o])
                        rel_marks.append(m_o)
                slot_rel[sl] = rel_marks + [(st, P.cnt[st])]
            P.fence()

    def attn_a_flash(x_res_src, x_dst):
        QSUP = 1024
        KSUP = 1024
        NQT = QSUP // 128
        KCH = KSUP // 128
        with ExitStack() as ph:
            sbp = lambda nm, shp, dt: ph.enter_context(nc.sbuf_tensor(B.unique(nm), shp, dt))
            sb_qt = sbp("faq", [128, NCH, QSUP], BF16)
            sb_kt = sbp("fak", [128, NCH, KSUP], BF16)
            sb_v = sbp("fav", [128, KCH, D], BF16)
            sb_o = sbp("fao", [128, NCH, QSUP], F32)
            sb_pr = sbp("fapr", [128, KCH, QSUP], BF16)
            sb_ml = sbp("faml", [128, 64], F32)
            sb_row = sbp("farow", [1, 2 * QSUP], F32)
            sb_ab = sbp("faab", [128, QSUP], F32)
            sb_xr = sbp("faxr", [128, NCH, QSUP], F32)
            psp = lambda nm, shp, dt: ph.enter_context(nc.psum_tensor(B.unique(nm), shp, dt))
            ps_sim = [psp("fas", [128, 512], F32) for _ in range(2)]
            ps_tr = [psp("fat", [128, 128], BF16) for _ in range(2)]
            ps_pv = [psp("fap", [128, 512], F32) for _ in range(2)]
            ps_row = psp("far", [1, 512], F32)
            ps_ab = psp("faa", [128, 512], F32)
            lq = P.sem("fa_lq")
            lk = P.sem("fa_lk")
            lx = P.sem("fa_lx")
            stx = P.sem("fa_st")
            qsv, ksv = fm(QS), fm(KS)
            xrv, xdv = fm(x_res_src), fm(x_dst)
            vv = tm(VTM)
            n_qs = T // QSUP
            prev_qs_rel = None
            for qs in range(n_qs):
                q0 = qs * QSUP
                n_ks = ((qs + 1) * QSUP + KSUP - 1) // KSUP
                w = list(prev_qs_rel) if prev_qs_rel else []
                P.dma("sync", sb_qt[:], qsv[:, :, q0:q0 + QSUP], lq, waits=w)
                P.dma("sync", sb_xr[:], xrv[:, :, q0:q0 + QSUP], lx, waits=w)
                mq = [(lq, P.cnt[lq])]
                mx = [(lx, P.cnt[lx])]
                B.memset("vector", sb_ml[:, 0:8], -3.0e38, waits=w)
                B.memset("vector", sb_ml[:, 8:16], 0.0)
                for c in range(NCH):
                    B.memset("vector", sb_o[:, c, :], 0.0)
                m_init = P.mark("vector")
                last_mg_m = None
                for ks in range(n_ks):
                    k0 = ks * KSUP
                    wk = [m_init] if ks == 0 else list(last_mg_m)
                    P.dma("sync", sb_kt[:], ksv[:, :, k0:k0 + KSUP], lk, waits=wk)
                    P.dma("sync", sb_v[:], vv[:, k0 // 128:(k0 + KSUP) // 128, :], lk,
                          waits=wk)
                    mk = [(lk, P.cnt[lk])]
                    tile_cov = []
                    m_exp_last = None
                    for j in range(NQT):
                        jq = qs * NQT + j
                        qend = (jq + 1) * 128
                        ncov = max(0, min(KSUP, qend - k0))
                        nsub = (ncov + 511) // 512
                        tile_cov.append((ncov + 127) // 128)
                        if nsub == 0:
                            continue
                        for u in range(nsub):
                            ps = ps_sim[u % 2]
                            k0s = k0 + u * 512
                            diag = (k0s <= jq * 128) and (jq * 128 < k0s + 512)
                            bw = mq + mk + [P.mark("scalar")]
                            first = True
                            if diag:
                                pvar = (jq * 128 - k0s) // 128
                                B.mm(ps[:], sb_ident[:],
                                     sb_stair[:, 384 - 128 * pvar:896 - 128 * pvar],
                                     True, False, waits=bw)
                                first = False
                            for ci in range(NCH):
                                B.mm(ps[:], sb_qt[:, ci, j * 128:(j + 1) * 128],
                                     sb_kt[:, ci, u * 512:(u + 1) * 512],
                                     first and ci == 0, ci == NCH - 1,
                                     waits=bw if (first and ci == 0) else (),
                                     inc="pe" if ci == NCH - 1 else None)
                            pes = ("pe", P.cnt["pe"])
                            B.red(sb_ml[:, 16 + u:17 + u], ps[:], ALU.max, waits=[pes])
                        B.red(sb_ml[:, 32:33], sb_ml[:, 16:16 + nsub], ALU.max)
                        B.tt("vector", sb_ml[:, 32:33], sb_ml[:, 0 + j:1 + j],
                             sb_ml[:, 32:33], ALU.max)
                        B.tt("vector", sb_ml[:, 33:34], sb_ml[:, 0 + j:1 + j],
                             sb_ml[:, 32:33], ALU.subtract)
                        B.copy("vector", sb_ml[:, 0 + j:1 + j], sb_ml[:, 32:33])
                        B.ts("vector", sb_ml[:, 34:35], sb_ml[:, 32:33], -1.0 / 32.0, ALU.mult)
                        m_v = P.mark("vector")
                        B.act(sb_ml[:, 24 + j:25 + j], sb_ml[:, 33:34], AF.Exp,
                              scale=1.0 / 32.0, waits=[m_v])
                        for u in range(nsub):
                            ps = ps_sim[u % 2]
                            B.act(sb_pr[:, 4 * u:4 * (u + 1), j * 128:(j + 1) * 128],
                                  ps[:], AF.Exp, scale=1.0 / 32.0, bias=sb_ml[:, 34:35],
                                  accum=sb_ml[:, 20 + u:21 + u])
                        m_exp_last = P.mark("scalar")
                        B.red(sb_ml[:, 33:34], sb_ml[:, 20:20 + nsub], ALU.add,
                              waits=[m_exp_last])
                        B.tt("vector", sb_ml[:, 8 + j:9 + j], sb_ml[:, 8 + j:9 + j],
                             sb_ml[:, 24 + j:25 + j], ALU.mult)
                        B.tt("vector", sb_ml[:, 8 + j:9 + j], sb_ml[:, 8 + j:9 + j],
                             sb_ml[:, 33:34], ALU.add)
                    for j in range(NQT):
                        cov = tile_cov[j]
                        if cov < KCH:
                            B.memset("vector", sb_pr[:, cov:KCH, j * 128:(j + 1) * 128], 0.0)
                    m_rag = P.mark("vector")
                    for hf in range(2):
                        for j4 in range(4):
                            j = 4 * hf + j4
                            jq = qs * NQT + j
                            covered = (jq + 1) * 128 > k0
                            src_col = sb_ml[:, 24 + j:25 + j] if covered else sb_onescf[:]
                            B.mm(ps_row[0:1, j4 * 128:(j4 + 1) * 128], src_col, sb_identf[:],
                                 True, True,
                                 waits=[m_exp_last, P.mark("scalar")] if j4 == 0 else (),
                                 inc="pe" if j4 == 3 else None)
                        B.scopy(sb_row[0:1, hf * 512:(hf + 1) * 512], ps_row[0:1, :],
                                waits=[("pe", P.cnt["pe"])])
                        m_ar = P.mark("scalar")
                        B.mm(ps_ab[:], sb_onesr[:], sb_row[0:1, hf * 512:(hf + 1) * 512],
                             True, True, waits=[m_ar, P.mark("scalar")], inc="pe")
                        B.scopy(sb_ab[:, hf * 512:(hf + 1) * 512], ps_ab[:],
                                waits=[("pe", P.cnt["pe"])])
                    m_ab = P.mark("scalar")
                    maxcov = max(tile_cov)
                    tb = 0
                    tr_rel = [None, None]
                    for j in range(NQT):
                        for kc in range(tile_cov[j]):
                            bk = tb % 2
                            tb += 1
                            w0 = [m_exp_last, m_rag]
                            if tr_rel[bk]:
                                w0.append(tr_rel[bk])
                            B.tr(ps_tr[bk][:], sb_pr[:, kc, j * 128:(j + 1) * 128],
                                 sb_ident[:], waits=w0, inc="pe")
                            B.copy("vector", sb_pr[:, kc, j * 128:(j + 1) * 128],
                                   ps_tr[bk][:], waits=[("pe", P.cnt["pe"])])
                            tr_rel[bk] = P.mark("vector")
                    m_trv = P.mark("vector")
                    for dc in range(NCH):
                        for half in range(2):
                            ps = ps_pv[half]
                            for kc in range(maxcov):
                                B.mm(ps[:], sb_v[:, kc, dc * 128:(dc + 1) * 128],
                                     sb_pr[:, kc, half * 512:(half + 1) * 512],
                                     kc == 0, kc == maxcov - 1,
                                     waits=[m_trv, P.mark("vector")] if kc == 0 else (),
                                     inc="pe" if kc == maxcov - 1 else None)
                        pe_pv = ("pe", P.cnt["pe"])
                        B.tt("vector", sb_o[:, dc, :], sb_o[:, dc, :], sb_ab[:],
                             ALU.mult, waits=[pe_pv, m_ab])
                        B.tt("vector", sb_o[:, dc, 0:512], sb_o[:, dc, 0:512],
                             ps_pv[0][:], ALU.add)
                        B.tt("vector", sb_o[:, dc, 512:1024], sb_o[:, dc, 512:1024],
                             ps_pv[1][:], ALU.add)
                    last_mg_m = [P.mark("vector")]
                B.recip(sb_ml[:, 40:48], sb_ml[:, 8:16], waits=last_mg_m)
                m_li = P.mark("vector")
                for hf in range(2):
                    for j4 in range(4):
                        j = 4 * hf + j4
                        B.mm(ps_row[0:1, j4 * 128:(j4 + 1) * 128], sb_ml[:, 40 + j:41 + j],
                             sb_identf[:], True, True,
                             waits=[m_li, P.mark("scalar")] if j4 == 0 else (),
                             inc="pe" if j4 == 3 else None)
                    B.scopy(sb_row[0:1, QSUP + hf * 512:QSUP + (hf + 1) * 512],
                            ps_row[0:1, :], waits=[("pe", P.cnt["pe"])])
                    m_lr = P.mark("scalar")
                    B.mm(ps_ab[:], sb_onesr[:],
                         sb_row[0:1, QSUP + hf * 512:QSUP + (hf + 1) * 512],
                         True, True, waits=[m_lr, P.mark("scalar")], inc="pe")
                    B.scopy(sb_ab[:, hf * 512:(hf + 1) * 512], ps_ab[:],
                            waits=[("pe", P.cnt["pe"])])
                m_lb = P.mark("scalar")
                for dc in range(NCH):
                    B.tt("vector", sb_o[:, dc, :], sb_o[:, dc, :], sb_ab[:], ALU.mult,
                         waits=[m_lb])
                    B.tt("vector", sb_o[:, dc, :], sb_o[:, dc, :], sb_xr[:, dc, :],
                         ALU.add, waits=mx)
                m_fin = P.mark("vector")
                P.dma("sync", xdv[:, :, q0:q0 + QSUP], sb_o[:], stx, waits=[m_fin])
                prev_qs_rel = [m_fin, (stx, P.cnt[stx])]
            P.fence()

    def inner_attn(l, ysrc, ydst):
        TTA = 256
        NT = T // TTA
        with ExitStack() as ph:
            sbp = lambda nm, shp, dt: ph.enter_context(nc.sbuf_tensor(B.unique(nm), shp, dt))
            psp = lambda nm, shp, dt=F32: ph.enter_context(nc.psum_tensor(B.unique(nm), shp, dt))
            lw = P.sem("ia_lw")
            w_qkv = sbp("iawq", [128, NCH, 3 * D], BF16)
            w_o = sbp("iawo", [128, NCH, D], BF16)
            hselT = sbp("iaht", [H, NCH, 128], F32)
            P.dma("gpsimd", w_qkv[:],
                  fWqkv.rearrange("l (c p) n -> p l c n", p=128)[:, l, :, :], lw)
            P.dma("gpsimd", w_o[:],
                  fWo.rearrange("l (c p) n -> p l c n", p=128)[:, l, :, :], lw)
            P.dma("sync", hselT[:], c_hselT[:, :, :], lw)
            LW = (lw, P.cnt[lw])
            ysh = [sbp("iay", [128, NCH, TTA + 1], F32) for _ in range(2)]
            yshb = sbp("iayb", [128, NCH, TTA], BF16)
            qkf = sbp("iaqk", [128, NCH, TTA], F32)
            qkq = sbp("iaqq", [128, NCH, TTA], BF16)
            qpb = sbp("iaqp", [128, NCH, TTA], BF16)
            kwin = sbp("iakw", [128, NCH, 2 * TTA], BF16)
            vwin = sbp("iavw", [128, 4, D], BF16)
            tb4 = sbp("iatb", [128, 4, TTA], F32)
            ab4 = sbp("iaab", [128, 4, TTA], F32)
            rn2 = sbp("iarn", [16, TTA], F32)
            probs_all = sbp("iapr", [128, 32, 384], BF16)
            prT = [sbp("iapt", [128, 3 * 128], BF16) for _ in range(2)]
            tmpf = sbp("iatm", [128, TTA], F32)
            sml = sbp("iasm", [128, 128], F32)
            aout = sbp("iaao", [128, NCH, TTA], BF16)
            yout = [sbp("iayo", [128, NCH, TTA], F32) for _ in range(2)]
            ps_mm = [psp("iam", [128, 512]) for _ in range(2)]
            ps_nq = psp("ian", [16, TTA])
            ps_rb = psp("iab", [128, TTA])
            ps_rot = psp("iar", [128, TTA])
            ps_sim = [psp("ias", [128, 384]) for _ in range(2)]
            ps_pvo = psp("iap", [128, 128])
            rot_bf = ps_rot[:].bitcast(BF16)
            ld = P.sem("ia_ld")
            st = P.sem("ia_st")
            tbs = P.sem("ia_tb")
            srcv, dstv = fm(ysrc), fm(ydst)
            slot_rel = [None, None]
            tb_rel = None
            kv_rel = None
            mm_rel = [None, None]
            nq_rel = None
            rb_rel = None
            rot_rel = None
            sim_rel = [None, None]
            prT_rel = [None, None]
            pvo_rel = None
            for i in range(NT):
                sl = i % 2
                t0 = i * TTA
                w = list(slot_rel[sl]) if slot_rel[sl] else [LW]
                if i == 0:
                    P.dma("sync", ysh[sl][:, :, 1:TTA + 1], srcv[:, :, 0:TTA], ld, waits=w)
                    B.memset("vector", ysh[sl][:, :, 0:1], 0.0, waits=w)
                else:
                    P.dma("sync", ysh[sl][:], srcv[:, :, t0 - 1:t0 + TTA], ld, waits=w)
                ldw = [(ld, P.cnt[ld])] + ([P.mark("vector")] if i == 0 else [])
                tw = list(tb_rel) if tb_rel else []
                for k4, tb in enumerate((tbl_cs, tbl_ss, tbl_ci, tbl_si)):
                    P.dma("sync", tb4[:, k4, :],
                          bass.AP(tb.tensor, tb.offset + t0, [[0, 2], [T, HD], [1, TTA]]),
                          tbs, waits=tw)
                tbw = [(tbs, P.cnt[tbs])]
                for k4, col in ((0, sb_fqs[:, l, 0:1]), (1, sb_fqs[:, l, 1:2]),
                                (2, sb_fks[:, l, 0:1]), (3, sb_fks[:, l, 1:2])):
                    B.ts("vector", ab4[:, k4, :], tb4[:, k4, :], col, ALU.mult, waits=tbw)
                tb_rel = [P.mark("vector")]
                B.copy("vector", yshb[:, 0:4, :], ysh[sl][:, 0:4, 1:TTA + 1], waits=ldw)
                B.copy("vector", yshb[:, 4:8, :], ysh[sl][:, 4:8, 0:TTA])
                m_yb = P.mark("vector")
                wsh = list(kv_rel) if kv_rel else []
                if i == 0:
                    B.memset("vector", kwin[:, :, 0:TTA], 0.0, waits=wsh)
                    B.memset("vector", vwin[:, 0:2, :], 0.0)
                else:
                    B.copy("vector", kwin[:, :, 0:TTA], kwin[:, :, TTA:2 * TTA], waits=wsh)
                    B.copy("vector", vwin[:, 0:2, :], vwin[:, 2:4, :])
                m_shift = P.mark("vector")
                for tn in range(2):
                    for c in range(NCH):
                        bk = c % 2
                        ps = ps_mm[bk]
                        w0 = [m_yb, LW] + ([mm_rel[bk]] if mm_rel[bk] else [])
                        for ci in range(NCH):
                            B.mm(ps[:, 0:TTA], w_qkv[:, ci, (tn * 8 + c) * 128:(tn * 8 + c + 1) * 128],
                                 yshb[:, ci, :], ci == 0, ci == NCH - 1,
                                 waits=w0 if ci == 0 else (),
                                 inc="pe" if ci == NCH - 1 else None)
                        B.scopy(qkf[:, c, :], ps[:, 0:TTA],
                                waits=[("pe", P.cnt["pe"]), P.mark("vector")])
                        B.act(qkq[:, c, :], qkf[:, c, :], AF.Square)
                        mm_rel[bk] = P.mark("scalar")
                    m_sq = P.mark("scalar")
                    for c in range(NCH):
                        B.mm(ps_nq[:], sb_hsel[:, c, :], qkq[:, c, :], c == 0, c == NCH - 1,
                             waits=([m_sq] + ([nq_rel] if nq_rel else [])) if c == 0 else (),
                             inc="pe" if c == NCH - 1 else None)
                    B.act(rn2[:], ps_nq[:], AF.Ln,
                          waits=[("pe", P.cnt["pe"]), P.mark("vector")])
                    nq_rel = P.mark("scalar")
                    B.act(rn2[:], rn2[:], AF.Exp, scale=-0.5)
                    m_rn = P.mark("scalar")
                    for c in range(NCH):
                        B.mm(ps_rb[:], hselT[0:16, c, :], rn2[:], True, True,
                             waits=[m_rn, LW] + ([rb_rel] if rb_rel else []), inc="pe")
                        pe_rb = ("pe", P.cnt["pe"])
                        B.tt("vector", qkf[:, c, :], qkf[:, c, :], ps_rb[:], ALU.mult,
                             waits=[pe_rb])
                        B.copy("vector", qkq[:, c, :], qkf[:, c, :])
                        rb_rel = P.mark("vector")
                        m_qnb = rb_rel
                        B.mm(ps_rot[:], sb_prot[:], qkq[:, c, :], True, True,
                             waits=[m_qnb] + ([rot_rel] if rot_rel else []), inc="pe")
                        pe_rot = ("pe", P.cnt["pe"])
                        B.tt("vector", qkf[:, c, :], qkf[:, c, :], ab4[:, 2 * tn, :], ALU.mult)
                        B.tt("vector", tmpf[:], ps_rot[:], ab4[:, 2 * tn + 1, :], ALU.mult,
                             waits=[pe_rot])
                        dstqp = qpb[:, c, :] if tn == 0 else kwin[:, c, TTA:2 * TTA]
                        ww = [m_shift] if tn == 1 else []
                        B.tt("vector", dstqp, qkf[:, c, :], tmpf[:], ALU.add, waits=ww)
                        rot_rel = P.mark("vector")
                m_kq = P.mark("vector")
                for s2 in range(2):
                    for eh in range(2):
                        bk = eh % 2
                        ps = ps_mm[bk]
                        w0 = [m_yb, LW] + ([mm_rel[bk]] if mm_rel[bk] else [])
                        for ci in range(NCH):
                            B.mm(ps[:], yshb[:, ci, s2 * 128:(s2 + 1) * 128],
                                 w_qkv[:, ci, 2 * D + eh * 512:2 * D + (eh + 1) * 512],
                                 ci == 0, ci == NCH - 1,
                                 waits=w0 if ci == 0 else (),
                                 inc="pe" if ci == NCH - 1 else None)
                        B.act(vwin[:, 2 + s2, eh * 512:(eh + 1) * 512], ps[:], AF.Copy,
                              waits=[("pe", P.cnt["pe"]), m_shift])
                        mm_rel[bk] = P.mark("scalar")
                m_v = P.mark("scalar")
                units = [(qsub, c, par) for qsub in range(2)
                         for c in range(NCH) for par in range(2)]
                NU = len(units)
                pes_u = [None] * NU
                mb_u = [None] * NU
                mexp_u = [None] * NU
                tile_pv_guard = kv_rel[0] if kv_rel else None
                for step in range(NU + 1):
                    if step < NU:
                        u = step
                        qsub, c, par = units[u]
                        qt0 = t0 + 128 * qsub
                        var = 1 if qt0 == 0 else (2 if qt0 == 128 else 0)
                        base = 64 * par
                        ps = ps_sim[u % 2]
                        w0 = [m_kq, m_v] + ([sim_rel[u % 2]] if sim_rel[u % 2] else [])
                        B.mm(ps[:], sb_ident[:], sb_band[:, var, :], True, False,
                             waits=w0)
                        B.mm(ps[:], qpb[base:base + 64, c, qsub * 128:(qsub + 1) * 128],
                             kwin[base:base + 64, c, qsub * 128:qsub * 128 + 384],
                             False, True, inc="pe")
                        pes_u[u] = ("pe", P.cnt["pe"])
                        B.red(sml[:, 4 * u:4 * u + 1], ps[:], ALU.max, negate=True,
                              waits=[pes_u[u]] + ([P.mark("scalar")] if u == 0 else []))
                        B.ts("vector", sml[:, 4 * u + 1:4 * u + 2],
                             sml[:, 4 * u:4 * u + 1], 8.0, ALU.mult)
                        mb_u[u] = P.mark("vector")
                    if step >= 1:
                        v = step - 1
                        qsub, c, par = units[v]
                        base = 64 * par
                        h = 2 * c + par
                        ps = ps_sim[v % 2]
                        prv = probs_all[:, v, :]
                        w1 = [mb_u[v], pes_u[v]]
                        if tile_pv_guard is not None:
                            w1.append(tile_pv_guard)
                        B.act(prv, ps[:], AF.Exp, scale=8.0,
                              bias=sml[:, 4 * v + 1:4 * v + 2],
                              accum=sml[:, 4 * v + 2:4 * v + 3], waits=w1)
                        sim_rel[v % 2] = P.mark("scalar")
                        mexp_u[v] = sim_rel[v % 2]
                        B.recip(sml[:, 4 * v + 3:4 * v + 4],
                                sml[:, 4 * v + 2:4 * v + 3], waits=[mexp_u[v]])
                        B.ts("vector", prv, prv, sml[:, 4 * v + 3:4 * v + 4], ALU.mult)
                        m_sc = P.mark("vector")
                        for kc in range(3):
                            B.tr(rot_bf[:, kc * 128:(kc + 1) * 128],
                                 prv[:, kc * 128:(kc + 1) * 128], sb_ident[:],
                                 waits=([m_sc] + ([rot_rel] if rot_rel else []))
                                 if kc == 0 else (),
                                 inc="pe" if kc == 2 else None)
                        prb = prT[v % 2]
                        B.scopy(prb[:], rot_bf[:, 0:384],
                                waits=[("pe", P.cnt["pe"])] +
                                ([prT_rel[v % 2]] if prT_rel[v % 2] else []))
                        rot_rel = P.mark("scalar")
                        m_pt = P.mark("scalar")
                        for kc in range(3):
                            B.mm(ps_pvo[base:base + 64, :],
                                 vwin[:, qsub + kc, 64 * h:64 * h + 64],
                                 prb[:, kc * 128:(kc + 1) * 128],
                                 kc == 0, kc == 2,
                                 waits=([m_pt] + ([pvo_rel] if (par == 0 and pvo_rel) else []))
                                 if kc == 0 else (),
                                 inc="pe" if kc == 2 else None)
                        prT_rel[v % 2] = ("pe", P.cnt["pe"])
                        if par == 1:
                            B.act(aout[:, c, qsub * 128:(qsub + 1) * 128], ps_pvo[:],
                                  AF.Copy, waits=[("pe", P.cnt["pe"])])
                            pvo_rel = P.mark("scalar")
                kv_rel = [("pe", P.cnt["pe"])]
                m_ao = P.mark("scalar")
                for co in range(NCH):
                    bk = co % 2
                    ps = ps_mm[bk]
                    w0 = [m_ao, LW] + ([mm_rel[bk]] if mm_rel[bk] else [])
                    for ci in range(NCH):
                        B.mm(ps[:, 0:TTA], w_o[:, ci, co * 128:(co + 1) * 128],
                             aout[:, ci, :], ci == 0, ci == NCH - 1,
                             waits=w0 if ci == 0 else (),
                             inc="pe" if ci == NCH - 1 else None)
                    B.tt("vector", yout[sl][:, co, :], ps[:, 0:TTA],
                         ysh[sl][:, co, 1:TTA + 1], ALU.add,
                         waits=[("pe", P.cnt["pe"])])
                    mm_rel[bk] = P.mark("vector")
                m_yo = P.mark("vector")
                P.dma("sync", dstv[:, :, t0:t0 + TTA], yout[sl][:], st, waits=[m_yo])
                slot_rel[sl] = [m_yo, (st, P.cnt[st])]
            P.fence()

    FFN_BLOCKS = [(j, 128) for j in range(21)] + [(21, 42)]

    def inner_ffn(l, ysrc, ydst):
        TTF = 256
        NT = T // TTF
        with ExitStack() as ph:
            sbp = lambda nm, shp, dt: ph.enter_context(nc.sbuf_tensor(B.unique(nm), shp, dt))
            psp = lambda nm, shp: ph.enter_context(nc.psum_tensor(B.unique(nm), shp, F32))
            lw = P.sem("ff_lw")
            w1 = sbp("ffw1", [128, NCH, 2 * DI], BF16)
            w2 = sbp("ffw2", [128, 22, D], BF16)
            biasr = sbp("ffbr", [65, 30, 128], BF16)
            P.dma("gpsimd", w1[:],
                  fW1.rearrange("l (c p) n -> p l c n", p=128)[:, l, :, :], lw)
            P.dma("gpsimd", w2[:, 0:21, :],
                  fW2[l:l + 1, 0:2688, :].rearrange("o (j p) n -> p (o j) n", p=128), lw)
            P.dma("gpsimd", w2[0:42, 21, :],
                  fW2[l:l + 1, 2688:2730, :].rearrange("o j n -> (o j) n"), lw)
            P.dma("sync", biasr[0:1, :, :], c_fbr[l:l + 1, 0, :, :], lw)
            P.dma("sync", biasr[64:65, :, :], c_fbr[l:l + 1, 1, :, :], lw)
            LW = (lw, P.cnt[lw])
            ysh = [sbp("ffy", [128, NCH, TTF + 1], F32) for _ in range(2)]
            yshb = sbp("ffyb", [128, NCH, TTF], BF16)
            xnb = [sbp("ffxb", [128, NCH, TTF], BF16) for _ in range(2)]
            hb = sbp("ffh", [128, 22, TTF], BF16)

            R = sbp("ffrw", [1, 3 * 256], F32)
            gfp = sbp("ffg", [128, TTF], F32)
            yout = [sbp("ffyo", [128, NCH, TTF], F32) for _ in range(2)]
            ps_w = [psp("ffpw", [128, 512]) for _ in range(2)]
            ps_s = psp("ffps", [1, TTF])
            ps_q = psp("ffpq", [1, TTF])
            ps_m = psp("ffpm", [128, TTF])
            ps_r = psp("ffpr", [128, TTF])
            ld = P.sem("ff_ld")
            st = P.sem("ff_st")
            srcv, dstv = fm(ysrc), fm(ydst)
            slot_rel = [None, None]
            w_rel = [None, None]
            s_rel = [None]
            m_rel = [None]
            m_xb = {}

            def stageA(i):
                sl = i % 2
                t0 = i * TTF
                w = list(slot_rel[sl]) if slot_rel[sl] else [LW]
                if i == 0:
                    P.dma("sync", ysh[sl][:, :, 1:TTF + 1], srcv[:, :, 0:TTF], ld, waits=w)
                    B.memset("vector", ysh[sl][:, :, 0:1], 0.0, waits=w)
                else:
                    P.dma("sync", ysh[sl][:], srcv[:, :, t0 - 1:t0 + TTF], ld, waits=w)
                ldw = [(ld, P.cnt[ld])] + ([P.mark("vector")] if i == 0 else [])
                B.copy("vector", yshb[:, 0:4, :], ysh[sl][:, 0:4, 1:TTF + 1],
                       waits=ldw + [("pe", P.cnt["pe"])])
                B.copy("vector", yshb[:, 4:8, :], ysh[sl][:, 4:8, 0:TTF])
                m_yb = P.mark("vector")
                ysq = xnb[sl]
                B.act(ysq[:], yshb[:], AF.Square, waits=[m_yb])
                m_sq = P.mark("scalar")
                for c in range(NCH):
                    B.mm(ps_s[:], sb_ones1[:], yshb[:, c, :], c == 0, c == NCH - 1,
                         waits=([m_yb] + ([s_rel[0]] if s_rel[0] else [])) if c == 0 else (),
                         inc="pe" if c == NCH - 1 else None)
                pe_s = ("pe", P.cnt["pe"])
                for c in range(NCH):
                    B.mm(ps_q[:], sb_ones1[:], ysq[:, c, :], c == 0, c == NCH - 1,
                         waits=[m_sq] if c == 0 else (),
                         inc="pe" if c == NCH - 1 else None)
                pe_q = ("pe", P.cnt["pe"])
                r0 = R[0:1, 0:TTF]
                r1 = R[0:1, 256:256 + TTF]
                r2 = R[0:1, 512:512 + TTF]
                B.scopy(r0, ps_s[:], waits=[pe_s, P.mark("vector")])
                B.scopy(r1, ps_q[:], waits=[pe_q])
                s_rel[0] = P.mark("scalar")
                B.ts("vector", r0, r0, 1.0 / D, ALU.mult, waits=[s_rel[0]])
                B.ts("vector", r1, r1, 1.0 / D, ALU.mult)
                B.tt("vector", r2, r0, r0, ALU.mult)
                B.tt("vector", r1, r1, r2, ALU.subtract)
                m_var = P.mark("vector")
                B.act(r2, r1, AF.Ln, bias=1e-5, waits=[m_var])
                B.act(r2, r2, AF.Exp, scale=-0.5)
                m_rstd = P.mark("scalar")
                B.mm(ps_m[:], sb_onesr[:], r0, True, True,
                     waits=[m_var, m_rstd] + ([m_rel[0]] if m_rel[0] else []), inc="pe")
                pe_m = ("pe", P.cnt["pe"])
                B.mm(ps_r[:], sb_onesr[:], r2, True, True, inc="pe")
                pe_r = ("pe", P.cnt["pe"])
                xnf = yout[sl]
                B.tt("vector", xnf[:, 0:4, :], ysh[sl][:, 0:4, 1:TTF + 1],
                     bc_mid(ps_m[:], 4), ALU.subtract,
                     waits=[pe_m, pe_r] + (list(slot_rel[sl]) if slot_rel[sl] else []))
                B.tt("vector", xnf[:, 4:8, :], ysh[sl][:, 4:8, 0:TTF],
                     bc_mid(ps_m[:], 4), ALU.subtract)
                B.tt("vector", xnf[:], xnf[:], bc_mid(ps_r[:], NCH), ALU.mult)
                m_rel[0] = P.mark("vector")
                m_xn = m_rel[0]
                for c in range(NCH):
                    B.ts("vector", xnb[sl][:, c, :], xnf[:, c, :],
                         sb_flng[:, l, c:c + 1], ALU.mult,
                         s2=sb_flnb[:, l, c:c + 1], op2=ALU.add,
                         waits=[m_xn] if c == 0 else ())
                m_xb[i] = P.mark("vector")

            def stageB(i):
                sl = i % 2
                t0 = i * TTF
                for (j, sz) in FFN_BLOCKS:
                    off = 256 * j
                    for half in (1, 0):
                        bk = half
                        ps = ps_w[bk]
                        o0 = off + half * sz
                        w0 = [m_xb[i], LW] + ([w_rel[bk]] if w_rel[bk] else [])
                        bsl = biasr[0:1, j, 0:sz] if half == 0 else biasr[64:65, j, 0:sz]
                        osl = sb_onesrb[0:1, 0:TTF] if half == 0 else sb_onesrb[64:65, 0:TTF]
                        B.mm(ps[0:sz, 0:TTF], bsl, osl, True, False, waits=w0)
                        for ci in range(NCH):
                            B.mm(ps[0:sz, 0:TTF], w1[:, ci, o0:o0 + sz],
                                 xnb[sl][:, ci, :], False, ci == NCH - 1,
                                 inc="pe" if ci == NCH - 1 else None)
                        if half == 1:
                            B.act(gfp[0:sz, :], ps[0:sz, 0:TTF], AF.Gelu,
                                  waits=[("pe", P.cnt["pe"]), P.mark("vector")])
                            w_rel[1] = P.mark("scalar")
                    B.tt("vector", hb[0:sz, j, :], ps_w[0][0:sz, 0:TTF], gfp[0:sz, :],
                         ALU.mult, waits=[("pe", P.cnt["pe"]), w_rel[1]])
                    w_rel[0] = P.mark("vector")
                m_h = P.mark("vector")
                for co in range(NCH):
                    bk = co % 2
                    ps = ps_w[bk]
                    w0 = [m_h, LW] + ([w_rel[bk]] if w_rel[bk] else [])
                    B.mm(ps[:, 0:TTF], biasr[0:1, 22 + co, :], sb_onesrb[0:1, 0:TTF],
                         True, False, waits=w0)
                    for (j, sz) in FFN_BLOCKS:
                        B.mm(ps[:, 0:TTF], w2[0:sz, j, co * 128:(co + 1) * 128],
                             hb[0:sz, j, :], False, j == 21,
                             inc="pe" if j == 21 else None)
                    B.tt("vector", yout[sl][:, co, :], ps[:, 0:TTF],
                         ysh[sl][:, co, 1:TTF + 1], ALU.add,
                         waits=[("pe", P.cnt["pe"])])
                    w_rel[bk] = P.mark("vector")
                m_yo = P.mark("vector")
                P.dma("sync", dstv[:, :, t0:t0 + TTF], yout[sl][:], st, waits=[m_yo])
                slot_rel[sl] = [m_yo, (st, P.cnt[st])]

            stageA(0)
            for i in range(NT):
                if i + 1 < NT:
                    stageA(i + 1)
                stageB(i)
            P.fence()

    IDX16 = B.dram("IDX16", [H, T], F32)

    def attn_b(x_src, x_dst):
        with ExitStack() as ph:
            sbp = lambda nm, shp, dt=F32: ph.enter_context(nc.sbuf_tensor(B.unique(nm), shp, dt))
            psp = lambda nm, shp: ph.enter_context(nc.psum_tensor(B.unique(nm), shp, F32))
            lw = P.sem("ab_lw")
            wkt = sbp("abwk", [128, NCH, D])
            wsb = sbp("abws", [128, NCH, H])
            sb_U = sbp("abU", [128, NCH, H])
            idxrow = sbp("abir", [H, 512])
            ps_u = psp("abpu", [128, H])
            ps_ix = psp("abpi", [H, 512])
            P.dma("sync", wkt[:], bWk_orig.rearrange("(ce p) d -> p ce d", p=128), lw)
            P.dma("sync", wsb[:], bwsblk.rearrange("(ce p) h -> p ce h", p=128), lw)
            LW = (lw, P.cnt[lw])
            u_rel = [None]
            for dc in range(NCH):
                for ce in range(NCH):
                    B.mm(ps_u[:], wkt[:, ce, dc * 128:(dc + 1) * 128], wsb[:, ce, :],
                         ce == 0, ce == NCH - 1,
                         waits=([LW] + ([u_rel[0]] if u_rel[0] else [])) if ce == 0 else (),
                         inc="pe" if ce == NCH - 1 else None)
                B.scopy(sb_U[:, dc, :], ps_u[:], waits=[("pe", P.cnt["pe"])])
                u_rel[0] = P.mark("scalar")
            ix_rel = [None]

            def idx_hook(i, t0, of_tile, m_ready):
                for c in range(NCH):
                    B.mm(ps_ix[:], sb_U[:, c, :], of_tile[:, c, :], c == 0, c == NCH - 1,
                         waits=([m_ready, u_rel[0]] + ([ix_rel[0]] if ix_rel[0] else []))
                         if c == 0 else (),
                         inc="pe" if c == NCH - 1 else None)
                B.scopy(idxrow[:], ps_ix[:], waits=[("pe", P.cnt["pe"])])
                m = P.mark("scalar")
                P.dma("sync", IDX16[:, t0:t0 + 512], idxrow[:], "ab_ix", waits=[m])
                ix_rel[0] = m

            ln_pass("ln3", x_src, dst_bf=XNB, idx_hook=idx_hook)
        proj_pass("pj3", [(bWqT, NCH), (bWkT, NCH), (bWvT, NCH)], XNB,
                  [("fm", QS), ("tm", KTM), ("tm", VTM)])
        with ExitStack() as pho:
            sbo = lambda nm, shp, dt=F32: pho.enter_context(nc.sbuf_tensor(B.unique(nm), shp, dt))
            ksp = sbo("abks", [128, NCH, 64], BF16)
            vsp = sbo("abvs", [64, H, 64], BF16)
            rank_tm = sbo("abrt", [128, 64, H])
            with ExitStack() as ph:
                sbp = lambda nm, shp, dt=F32: ph.enter_context(nc.sbuf_tensor(B.unique(nm), shp, dt))
                psp = lambda nm, shp: ph.enter_context(nc.psum_tensor(B.unique(nm), shp, F32))
                sbI = sbp("absi", [128, 1024])
                idx16 = sbp("abix", [H, T])
                mask16 = sbp("abmk", [H, T])
                rank16 = sbp("abrk", [H, T])
                st8 = sbp("abst", [H, 8])
                rowb = sbp("abrw", [1, 128])
                rowc = sbp("abrc", [1, 16])
                thb = sbp("abth", [128, 1])
                msk = sbp("abms", [128, 1024])
                cnt_p = sbp("abcp", [128, 1])
                ps_row = psp("abpr", [1, 128])
                ps_col = psp("abpc", [16, 1])
                ps_th = psp("abpt", [128, 1])
                ps_tr = psp("abptr", [128, 16])
                ldi = P.sem("ab_ldi")
                P.dma("sync", sbI[:],
                      bass.AP(IDX16.tensor, IDX16.offset,
                              [[T, 16], [1024, 8], [1, 1024]]), ldi)
                P.dma("sync", idx16[:], IDX16[:, :], ldi)
                ldw = [(ldi, P.cnt[ldi])]

                def row_to_col(src_row16, dst_col, extra=()):
                    B.mm(ps_col[:], src_row16, sb_onescf[0:1, 0:1], True, True,
                         waits=list(extra) + [P.mark("scalar"), P.mark("vector")], inc="pe")
                    B.scopy(dst_col, ps_col[:], waits=[("pe", P.cnt["pe"])])
                    return P.mark("scalar")

                def part_to_head(col128, dst_row16, op, extra=()):
                    B.mm(ps_row[:], col128, sb_identf[:], True, True,
                         waits=list(extra) + [P.mark("scalar")], inc="pe")
                    B.scopy(rowb[:], ps_row[:], waits=[("pe", P.cnt["pe"])])
                    m = P.mark("scalar")
                    B.red(dst_row16, rowb[0:1, :].rearrange("a (h b) -> a h b", b=8),
                          op, waits=[m])
                    return P.mark("vector")

                B.red(cnt_p[:], sbI[:], ALU.max, waits=ldw)
                m1 = part_to_head(cnt_p[:], rowc[:], ALU.max, extra=[P.mark("vector")])
                B.ts("vector", rowc[:], rowc[:], 1.0, ALU.add)
                row_to_col(rowc[:], st8[:, 1:2], extra=[P.mark("vector")])
                B.red(cnt_p[:], sbI[:], ALU.min, waits=ldw + [P.mark("scalar")])
                m2 = part_to_head(cnt_p[:], rowc[:], ALU.min, extra=[P.mark("vector")])
                B.ts("vector", rowc[:], rowc[:], -1.0, ALU.add)
                row_to_col(rowc[:], st8[:, 0:1], extra=[P.mark("vector")])
                for it in range(45):
                    B.tt("vector", st8[:, 2:3], st8[:, 0:1], st8[:, 1:2], ALU.add,
                         waits=[P.mark("scalar")])
                    B.ts("vector", st8[:, 2:3], st8[:, 2:3], 0.5, ALU.mult)
                    m_mid = P.mark("vector")
                    B.mm(ps_th[:], sb_gselT[:], st8[:, 2:3], True, True,
                         waits=[m_mid, P.mark("scalar")], inc="pe")
                    B.scopy(thb[:], ps_th[:], waits=[("pe", P.cnt["pe"])])
                    m_th = P.mark("scalar")
                    B.ts("vector", msk[:], sbI[:], thb[:], ALU.is_ge, waits=[m_th])
                    B.red(cnt_p[:], msk[:], ALU.add)
                    part_to_head(cnt_p[:], rowc[:], ALU.add, extra=[P.mark("vector")])
                    row_to_col(rowc[:], st8[:, 3:4], extra=[P.mark("vector")])
                    B.ts("vector", st8[:, 4:5], st8[:, 3:4], 64.0, ALU.is_ge,
                         waits=[P.mark("scalar")])
                    B.tt("vector", st8[:, 5:6], st8[:, 2:3], st8[:, 0:1], ALU.subtract)
                    B.tt("vector", st8[:, 5:6], st8[:, 5:6], st8[:, 4:5], ALU.mult)
                    B.tt("vector", st8[:, 0:1], st8[:, 0:1], st8[:, 5:6], ALU.add)
                    B.tt("vector", st8[:, 6:7], st8[:, 1:2], st8[:, 2:3], ALU.subtract)
                    B.tt("vector", st8[:, 6:7], st8[:, 6:7], st8[:, 4:5], ALU.mult)
                    B.tt("vector", st8[:, 1:2], st8[:, 2:3], st8[:, 6:7], ALU.add)
                B.ts("vector", mask16[:], idx16[:], st8[:, 0:1], ALU.is_ge,
                     waits=ldw + [P.mark("vector")])
                ones_b = bass.AP(sb_onescf[:].tensor, sb_onescf[:].offset,
                                 [list(sb_onescf[:].ap[0])[:1] + [16], [0, T]])
                B.scan(rank16[:], mask16[:], ones_b, 0.0, ALU.add, ALU.mult)
                B.tt("vector", rank16[:], rank16[:], mask16[:], ALU.mult)
                m_rk = P.mark("vector")
                tr_rel = None
                for tc in range(64):
                    B.tr(ps_tr[:], rank16[0:16, tc * 128:(tc + 1) * 128],
                         sb_identf[0:16, 0:16],
                         waits=[m_rk] + ([tr_rel] if tr_rel else []), inc="pe")
                    B.scopy(rank_tm[:, tc, :], ps_tr[:], waits=[("pe", P.cnt["pe"])])
                    tr_rel = P.mark("scalar")
                P.fence()
            with ExitStack() as ph:
                sbp = lambda nm, shp, dt=F32: ph.enter_context(nc.sbuf_tensor(B.unique(nm), shp, dt))
                kvt = [sbp("abkv", [128, 2, D], BF16) for _ in range(2)]
                Sg = sbp("absg", [128, 4, 64], BF16)
                ps_c = [ph.enter_context(nc.psum_tensor(B.unique("abpc2"), [64, 512], F32))
                        for _ in range(8)]
                ldk = P.sem("ab_ldk")
                ktmv, vtmv = tm(KTM), tm(VTM)
                slot_rel2 = [None, None]
                for pass_ in range(4):
                    heads = [4 * pass_ + z for z in range(4)]
                    for tc in range(64):
                        sl = tc % 2
                        w = list(slot_rel2[sl]) if slot_rel2[sl] else []
                        P.dma("sync", kvt[sl][:, 0, :], ktmv[:, tc, :], ldk, waits=w)
                        P.dma("sync", kvt[sl][:, 1, :], vtmv[:, tc, :], ldk, waits=w)
                        ldw2 = [(ldk, P.cnt[ldk])]
                        for z, h in enumerate(heads):
                            B.ts("vector", Sg[:, z, :], sb_iota64[:],
                                 rank_tm[:, tc, h:h + 1], ALU.is_equal,
                                 waits=(ldw2 + [P.mark("tensor") if False else
                                        ("pe", P.cnt["pe"])]) if z == 0 else ())
                        m_sg = P.mark("vector")
                        for z, h in enumerate(heads):
                            B.mm(ps_c[2 * z][:, 0:64], kvt[sl][:, 0, 64 * h:64 * h + 64],
                                 Sg[:, z, :], tc == 0, tc == 63,
                                 waits=[m_sg] + ldw2,
                                 inc="pe" if tc == 63 else None)
                            B.mm(ps_c[2 * z + 1][:, 0:64], Sg[:, z, :],
                                 kvt[sl][:, 1, 64 * h:64 * h + 64], tc == 0, tc == 63,
                                 inc="pe" if tc == 63 else None)
                        slot_rel2[sl] = [("pe", P.cnt["pe"]), m_sg]
                    pe_f = ("pe", P.cnt["pe"])
                    for z, h in enumerate(heads):
                        B.scopy(ksp[64 * (h % 2):64 * (h % 2) + 64, h // 2, :],
                                ps_c[2 * z][:, 0:64], waits=[pe_f])
                        B.scopy(vsp[0:64, h, :], ps_c[2 * z + 1][:, 0:64], waits=[pe_f])
                    slot_rel2 = [[P.mark("scalar")], [P.mark("scalar")]]
                P.fence()
            with ExitStack() as ph:
                sbp = lambda nm, shp, dt=F32: ph.enter_context(nc.sbuf_tensor(B.unique(nm), shp, dt))
                psp = lambda nm, shp: ph.enter_context(nc.psum_tensor(B.unique(nm), shp, F32))
                lw2 = P.sem("ab_lw2")
                w_oT = sbp("abwo", [128, NCH, D], BF16)
                P.dma("gpsimd", w_oT[:], bWoT.rearrange("(c p) o -> p c o", p=128), lw2)
                LW2 = (lw2, P.cnt[lw2])
                qt = [sbp("abqt", [128, NCH, 512], BF16) for _ in range(2)]
                xr = [sbp("abxr", [128, NCH, 512], F32) for _ in range(2)]
                aout = sbp("abao", [128, NCH, 512], BF16)
                yout = [sbp("abyo", [128, NCH, 512], F32) for _ in range(2)]
                probs_all = sbp("abpb", [128, 64, 64], BF16)
                prT = [sbp("abpt2", [64, 128], BF16) for _ in range(2)]
                sml = sbp("absm", [128, 256])
                ps_s = [psp("abps", [128, 64]) for _ in range(2)]
                ps_t = [ph.enter_context(nc.psum_tensor(B.unique("abptx"), [64, 128], BF16))
                        for _ in range(2)]
                ps_o = psp("abpo", [128, 128])
                ps_mm = [psp("abpm", [128, 512]) for _ in range(2)]
                ldq = P.sem("ab_ldq")
                stx = P.sem("ab_stx")
                qsv = fm(QS)
                xrv, xdv = fm(x_src), fm(x_dst)
                slot_rel3 = [None, None]
                sim_rel = [None, None]
                prT_rel = [None, None]
                tr_rel = [None, None]
                pvo_rel = None
                mm_rel = [None, None]
                tile_pv_guard = None
                for i in range(16):
                    sl = i % 2
                    t0 = i * 512
                    w = list(slot_rel3[sl]) if slot_rel3[sl] else [LW2]
                    P.dma("sync", qt[sl][:], qsv[:, :, t0:t0 + 512], ldq, waits=w)
                    P.dma("sync", xr[sl][:], xrv[:, :, t0:t0 + 512], ldq, waits=w)
                    ldw3 = [(ldq, P.cnt[ldq])]
                    units = [(qsub, c, par) for qsub in range(4)
                             for c in range(NCH) for par in range(2)]
                    NU = len(units)
                    pes_u = [None] * NU
                    mb_u = [None] * NU
                    for step in range(NU + 1):
                        if step < NU:
                            u = step
                            qsub, c, par = units[u]
                            base = 64 * par
                            ps = ps_s[u % 2]
                            w0 = ldw3 + ([sim_rel[u % 2]] if sim_rel[u % 2] else [])
                            B.mm(ps[:], qt[sl][base:base + 64, c, qsub * 128:(qsub + 1) * 128],
                                 ksp[base:base + 64, c, :], True, True, waits=w0, inc="pe")
                            pes_u[u] = ("pe", P.cnt["pe"])
                            B.red(sml[:, 4 * u:4 * u + 1], ps[:], ALU.max, negate=True,
                                  waits=[pes_u[u]] + ([P.mark("scalar")] if u == 0 else []))
                            B.ts("vector", sml[:, 4 * u + 1:4 * u + 2],
                                 sml[:, 4 * u:4 * u + 1], 0.125, ALU.mult)
                            mb_u[u] = P.mark("vector")
                        if step >= 1:
                            v = step - 1
                            qsub, c, par = units[v]
                            base = 64 * par
                            h = 2 * c + par
                            ps = ps_s[v % 2]
                            prv = probs_all[:, v, :]
                            w1 = [mb_u[v], pes_u[v]]
                            if tile_pv_guard is not None:
                                w1.append(tile_pv_guard)
                            B.act(prv, ps[:], AF.Exp, scale=0.125,
                                  bias=sml[:, 4 * v + 1:4 * v + 2],
                                  accum=sml[:, 4 * v + 2:4 * v + 3], waits=w1)
                            sim_rel[v % 2] = P.mark("scalar")
                            B.recip(sml[:, 4 * v + 3:4 * v + 4],
                                    sml[:, 4 * v + 2:4 * v + 3], waits=[sim_rel[v % 2]])
                            B.ts("vector", prv, prv, sml[:, 4 * v + 3:4 * v + 4], ALU.mult)
                            m_sc = P.mark("vector")
                            B.tr(ps_t[v % 2][:], prv, sb_ident[:],
                                 waits=[m_sc] + ([tr_rel[v % 2]] if tr_rel[v % 2] else []),
                                 inc="pe")
                            prb = prT[v % 2]
                            B.scopy(prb[:], ps_t[v % 2][:],
                                    waits=[("pe", P.cnt["pe"])] +
                                    ([prT_rel[v % 2]] if prT_rel[v % 2] else []))
                            tr_rel[v % 2] = P.mark("scalar")
                            m_pt = P.mark("scalar")
                            B.mm(ps_o[base:base + 64, :], vsp[0:64, h, :], prb[:],
                                 True, True,
                                 waits=[m_pt] + ([pvo_rel] if (par == 0 and pvo_rel) else []),
                                 inc="pe")
                            prT_rel[v % 2] = ("pe", P.cnt["pe"])
                            if par == 1:
                                B.act(aout[:, c, qsub * 128:(qsub + 1) * 128], ps_o[:],
                                      AF.Copy, waits=[("pe", P.cnt["pe"])])
                                pvo_rel = P.mark("scalar")
                    tile_pv_guard = ("pe", P.cnt["pe"])
                    m_ao = P.mark("scalar")
                    for co in range(NCH):
                        bk = co % 2
                        ps = ps_mm[bk]
                        w0 = [m_ao, LW2] + ([mm_rel[bk]] if mm_rel[bk] else [])
                        for ci in range(NCH):
                            B.mm(ps[:], w_oT[:, ci, co * 128:(co + 1) * 128],
                                 aout[:, ci, :], ci == 0, ci == NCH - 1,
                                 waits=w0 if ci == 0 else (),
                                 inc="pe" if ci == NCH - 1 else None)
                        B.tt("vector", yout[sl][:, co, :], ps[:],
                             xr[sl][:, co, :], ALU.add, waits=[("pe", P.cnt["pe"])])
                        mm_rel[bk] = P.mark("vector")
                    m_yo = P.mark("vector")
                    P.dma("sync", xdv[:, :, t0:t0 + 512], yout[sl][:], stx, waits=[m_yo])
                    slot_rel3[sl] = [m_yo, (stx, P.cnt[stx])]
                P.fence()

    ln_pass("ln1", xin, dst_bf=XNB)
    if "xnb" in debug_taps:
        P.dma("sync", dbg["dbg_xnb"][:, :], XNB[:, :], "dbgst")
        P.fence()
    if stop_after == "ln1":
        P.dma("sync", out[:, :], xin[:, :], "stout")
        P.fence()
        return nc, finish(B)

    proj_pass("pj1", [(aQw, NCH), (aKw, NCH), (aVw, NCH)], XNB,
              [("fm", QS), ("fm", KS), ("tm", VTM)])
    if "qkv" in debug_taps:
        P.dma("sync", dbg["dbg_q"][:, :], QS[:, :], "dbgst")
        P.dma("sync", dbg["dbg_k"][:, :], KS[:, :], "dbgst")
        P.dma("sync", dbg["dbg_v"][:, :], VTM[:, :], "dbgst")
        P.fence()
    if stop_after == "proj1":
        P.dma("sync", out[:, :], xin[:, :], "stout")
        P.fence()
        return nc, finish(B)

    attn_a_flash(xin, S_a)
    cur = S_a
    if stop_after == "sub1":
        P.dma("sync", out[:, :], cur[:, :], "stout")
        P.fence()
        return nc, finish(B)

    def inner_block(tag, cur):
        ln_pass("ln_" + tag, cur, dst_f32=S0)
        y, alt = S0, S1
        nsub = DEPTH * 2
        if stop_after and stop_after.startswith("inner"):
            nsub = min(nsub, int(stop_after[5:]))
        for l in range(DEPTH):
            if 2 * l >= nsub:
                break
            inner_attn(l, y, alt)
            y, alt = alt, y
            if 2 * l + 1 >= nsub:
                break
            inner_ffn(l, y, alt)
            y, alt = alt, y
        return y

    y = inner_block("i2", cur)
    if stop_after and stop_after.startswith("inner"):
        P.dma("sync", out[:, :], y[:, :], "stout")
        P.fence()
        return nc, finish(B)
    ln_pass("add2", cur, src2=y, dst_sum=S_b, no_norm=True)
    cur = S_b
    if stop_after == "sub2":
        P.dma("sync", out[:, :], cur[:, :], "stout")
        P.fence()
        return nc, finish(B)

    attn_b(cur, S_a)
    cur = S_a
    if stop_after == "sub3":
        P.dma("sync", out[:, :], cur[:, :], "stout")
        P.fence()
        return nc, finish(B)

    y = inner_block("i4", cur)
    ln_pass("add4", cur, src2=y, dst_sum=out, no_norm=True)
    P.fence()
    return nc, finish(B)


def finish(B):
    B.P.build(B.nc)
    B.es.close()
    B.nc.finalize()
    return B



_BUILD_CACHE = {}


def _host_constants():
    import ml_dtypes
    bf = ml_dtypes.bfloat16
    C = {}
    half = np.arange(0, HD, 2, dtype=np.float64)
    inv_freq = 1.0 / 10000.0 ** (half / HD)
    t = np.arange(T, dtype=np.float64)
    freqs = t[:, None] * inv_freq[None, :]
    freqs = np.concatenate([freqs, freqs], -1)
    base_scale = (half + 0.4 * HD) / (1.4 * HD)
    power = (t - T // 2) / (256 // 2)
    scale = base_scale[None, :] ** power[:, None]
    scale = np.concatenate([scale, scale], -1)
    C["tbl_cs"] = np.ascontiguousarray((np.cos(freqs) * scale).T.astype(np.float32))
    C["tbl_ss"] = np.ascontiguousarray((np.sin(freqs) * scale).T.astype(np.float32))
    C["tbl_ci"] = np.ascontiguousarray((np.cos(freqs) / scale).T.astype(np.float32))
    C["tbl_si"] = np.ascontiguousarray((np.sin(freqs) / scale).T.astype(np.float32))
    C["c_ident"] = np.eye(128, dtype=np.float32).astype(bf)
    C["c_identf"] = np.eye(128, dtype=np.float32)
    ii = np.arange(128)[:, None]
    cc = np.arange(1024)[None, :]
    C["c_stair"] = np.where(cc > ii + 384, np.float32(NEG), np.float32(0)).astype(bf)
    il = np.arange(128)[:, None]
    jl = np.arange(384)[None, :]
    band = (jl < il) | (jl > il + 256)
    b0 = band | (jl < 256)
    b1 = band | (jl < 128)
    C["c_band"] = np.stack([
        np.where(band, np.float32(NEG), np.float32(0)),
        np.where(b0, np.float32(NEG), np.float32(0)),
        np.where(b1, np.float32(NEG), np.float32(0))]).astype(bf)
    prot = np.zeros((128, 128), np.float32)
    for b in range(2):
        o = 64 * b
        for m in range(32):
            prot[o + m + 32, o + m] = -1.0
        for m in range(32, 64):
            prot[o + m - 32, o + m] = 1.0
    C["c_prot"] = prot.astype(bf)
    C["c_ones1"] = np.ones((128, 1), np.float32).astype(bf)
    C["c_onesr"] = np.ones((1, 128), np.float32)
    C["c_onescf"] = np.ones((128, 1), np.float32)
    C["c_iota64"] = np.tile(np.arange(1, HD + 1, dtype=np.float32)[None, :], (128, 1))
    hsel = np.zeros((NCH, 128, H), np.float32)
    for c in range(NCH):
        for p in range(128):
            hsel[c, p, 2 * c + p // 64] = 1.0
    C["c_hsel"] = hsel.astype(bf)
    hsel2 = np.zeros((128, 2), np.float32)
    hsel2[np.arange(128), np.arange(128) // 64] = 1.0
    C["c_hsel2"] = hsel2.astype(bf)
    C["c_pairsel"] = np.ascontiguousarray(hsel2.T.astype(np.float32))
    C["c_hselT"] = np.ascontiguousarray(hsel.transpose(2, 0, 1).astype(np.float32))
    gsel = np.zeros((128, H), np.float32)
    for p in range(128):
        gsel[p, p // 8] = 1.0
    C["c_gsel"] = gsel.astype(bf)
    C["c_gselT"] = np.ascontiguousarray(gsel.T.astype(np.float32))
    fv = np.zeros(8, np.float32)
    fv[:len(FVALS)] = FVALS
    C["c_fvals"] = np.tile(fv[None, :], (128, 1))
    C["c_onesrb"] = np.ones((1, 512), np.float32).astype(bf)
    return C


def _host_weights(inputs):
    W = {}
    f32 = lambda a: np.ascontiguousarray(np.asarray(a, dtype=np.float32))
    W["aQw"] = f32(inputs["L0_attn_a_Qw"])
    W["aKw"] = f32(inputs["L0_attn_a_Kw"])
    W["aVw"] = f32(inputs["L0_attn_a_Vw"])
    W["bWqT"] = f32(np.asarray(inputs["L0_attn_b_Wq"]).T)
    W["bWkT"] = f32(np.asarray(inputs["L0_attn_b_Wk"]).T)
    W["bWvT"] = f32(np.asarray(inputs["L0_attn_b_Wv"]).T)
    W["bWoT"] = f32(np.asarray(inputs["L0_attn_b_Wo"]).T)
    W["bWk_orig"] = f32(inputs["L0_attn_b_Wk"])
    ws = np.asarray(inputs["L0_attn_b_w_score"], dtype=np.float32)
    wsblk = np.zeros((D, H), np.float32)
    for h in range(H):
        wsblk[64 * h:64 * h + 64, h] = ws
    W["bwsblk"] = wsblk
    W["fWqkv"] = f32(inputs["L0_ffn_Wqkv"])
    W["fWo"] = f32(inputs["L0_ffn_Wo"])
    W1 = f32(inputs["L0_ffn_W1"])
    b1 = f32(inputs["L0_ffn_b1"])
    blocks, sizes = [], []
    for j in range(21):
        blocks.append(np.arange(128 * j, 128 * (j + 1)))
        sizes.append(128)
    blocks.append(np.arange(2688, 2730))
    sizes.append(42)
    perm = []
    for blk in blocks:
        perm.extend(blk.tolist())
        perm.extend((blk + DI).tolist())
    perm = np.asarray(perm)
    W["fW1"] = np.ascontiguousarray(W1[:, :, perm])
    W["fW2"] = f32(inputs["L0_ffn_W2"])
    b1p = b1[:, perm]
    b1cm = np.zeros((DEPTH, 128, 44), np.float32)
    off = 0
    for j, sz in enumerate(sizes):
        b1cm[:, 0:sz, 2 * j] = b1p[:, off:off + sz]
        b1cm[:, 0:sz, 2 * j + 1] = b1p[:, off + sz:off + 2 * sz]
        off += 2 * sz
    W["fb1"] = b1cm
    import ml_dtypes as _md
    _bf = _md.bfloat16
    b1r = np.zeros((DEPTH, 44, 128), np.float32)
    off = 0
    for j, sz in enumerate(sizes):
        b1r[:, 2 * j, 0:sz] = b1p[:, off:off + sz]
        b1r[:, 2 * j + 1, 0:sz] = b1p[:, off + sz:off + 2 * sz]
        off += 2 * sz
    fbr = np.zeros((DEPTH, 2, 30, 128), np.float32)
    fbr[:, 0, 0:22, :] = b1r[:, 0::2, :]
    fbr[:, 1, 0:22, :] = b1r[:, 1::2, :]
    fbr[:, 0, 22:30, :] = np.asarray(inputs["L0_ffn_b2"], np.float32).reshape(DEPTH, NCH, 128)
    W["c_fbr"] = fbr.astype(_bf)
    cm = lambda v: np.ascontiguousarray(
        np.asarray(v, np.float32).reshape(DEPTH, NCH, 128).transpose(0, 2, 1))
    W["fb2"] = cm(inputs["L0_ffn_b2"])
    W["flng"] = cm(inputs["L0_ffn_ln_g"])
    W["flnb"] = cm(inputs["L0_ffn_ln_b"])
    qs = np.asarray(inputs["L0_ffn_q_scale"], np.float32)
    ks = np.asarray(inputs["L0_ffn_k_scale"], np.float32)

    def qcols(v):
        a = np.tile(v, (1, 2))
        b = np.tile(np.concatenate([v[:, 32:], v[:, :32]], 1), (1, 2))
        return np.ascontiguousarray(np.stack([a, b], axis=-1))
    W["fqs"] = qcols(qs)
    W["fks"] = qcols(ks)
    return W


DEV_NCORES = None


def kernel(stop_after="all", debug_taps=(), **inputs):
    key = (stop_after, tuple(sorted(debug_taps)))
    if key not in _BUILD_CACHE:
        _BUILD_CACHE[key] = build_program(stop_after=stop_after, debug_taps=debug_taps)
    nc, _ = _BUILD_CACHE[key]
    C = _host_constants()
    W = _host_weights(inputs)
    x = np.asarray(inputs["L0_attn_a_x"], dtype=np.float32)
    ncores = DEV_NCORES or NCORES
    in_maps = []
    for b in range(ncores):
        m = dict(C)
        m.update(W)
        m["xin"] = np.ascontiguousarray(x[b].T)
        in_maps.append(m)
    res = run_bass_kernel_spmd(nc, in_maps, core_ids=list(range(ncores)))
    outs = [np.asarray(r["xout"]).T for r in res.results]
    full = np.stack(outs, axis=0).astype(np.float32)
    if debug_taps:
        return full, res
    return full


# version 04
# speedup vs baseline: 1.0428x; 1.0428x over previous
import numpy as np
from contextlib import ExitStack

import concourse.bass as bass
import concourse.mybir as mybir
from concourse.bass_utils import run_bass_kernel_spmd

F32 = mybir.dt.float32
BF16 = mybir.dt.bfloat16
AF = mybir.ActivationFunctionType
ALU = mybir.AluOpType
AX = mybir.AxisListType

T = 8192
D = 1024
NCH = D // 128
NEG = -60000.0
H = 16
HD = 64
DEPTH = 6
DI = 2730
NCORES = 8
FVALS = (1e-5, 1.0 / 32.0, 8.0, 0.125)

ENGINES = ("sync", "gpsimd", "vector", "scalar", "tensor")


def bc_mid(ap2, n):
    assert len(ap2.ap) == 2
    return bass.AP(ap2.tensor, ap2.offset, [list(ap2.ap[0]), [0, n], list(ap2.ap[1])])


class Plan:
    def __init__(self):
        self.ops = {e: [] for e in ENGINES}
        self.cnt = {}
        self.semnames = []

    def sem(self, name):
        if name not in self.cnt:
            self.cnt[name] = 0
            self.semnames.append(name)
        return name

    def emit(self, eng, fn, waits=(), inc=None, incn=1):
        if inc is not None:
            self.sem(inc)
            self.cnt[inc] += incn
        self.ops[eng].append((tuple(waits), fn, inc, incn))
        return (inc, self.cnt[inc]) if inc is not None else None

    def chain(self, eng, fn, waits=()):
        s = self.sem(eng + "_self")
        w = list(waits)
        if self.cnt[s] > 0:
            w.append((s, self.cnt[s]))
        return self.emit(eng, fn, w, inc=s)

    def mark(self, eng):
        s = self.sem(eng + "_self")
        return (s, self.cnt[s])

    def dma(self, eng, dst, src, sem, waits=()):
        self.sem(sem)
        fn = lambda e, dst=dst, src=src: e.dma_start(out=dst, in_=src)
        return self.emit(eng, fn, waits, inc=sem, incn=16)

    def wait_only(self, eng, waits):
        fn = lambda e: e.nop()
        self.ops[eng].append((tuple(waits), fn, None, 0))

    def fence(self):
        waits = [(s, self.cnt[s]) for s in self.semnames if self.cnt[s] > 0]
        for e in ENGINES:
            self.wait_only(e, waits)

    def build(self, nc):
        es = ExitStack()
        SEM = {n: es.enter_context(nc.semaphore(n)) for n in self.semnames}
        blk = es.enter_context(nc.Block())
        for eng_name in ENGINES:
            def body(e, eng_name=eng_name):
                wm = {}
                for waits, fn, inc, incn in self.ops[eng_name]:
                    for (s, v) in waits:
                        if wm.get(s, 0) >= v:
                            continue
                        e.wait_ge(SEM[s], v)
                        wm[s] = v
                    ins = fn(e)
                    if inc is not None:
                        ins.then_inc(SEM[inc], incn)
            getattr(blk, eng_name)(body)
        es.close()


class Builder:
    def __init__(self, nc):
        self.nc = nc
        self.P = Plan()
        self.es = ExitStack()
        self.uid = 0

    def dram(self, name, shape, dt, kind="Internal"):
        return self.nc.dram_tensor(name, shape, dt, kind=kind).ap()

    def sb(self, name, shape, dt=F32):
        return self.es.enter_context(self.nc.sbuf_tensor(name, shape, dt))

    def unique(self, prefix):
        self.uid += 1
        return f"{prefix}{self.uid}"

    def copy(self, eng, dst, src, waits=()):
        return self.P.chain(eng, lambda e, d=dst, s=src: e.tensor_copy(d, s), waits)

    def act(self, dst, src, func, bias=0.0, scale=1.0, accum=None, waits=()):
        fn = lambda e, d=dst, s=src, f=func, b=bias, sc=scale, a=accum: e.activation(
            d, s, f, bias=b, scale=sc, accum_out=a)
        return self.P.chain("scalar", fn, waits)

    def scopy(self, dst, src, waits=()):
        return self.P.chain("scalar", lambda e, d=dst, s=src: e.copy(d, s), waits)

    def tt(self, eng, dst, a, b, op, waits=()):
        return self.P.chain(eng, lambda e, d=dst, x=a, y=b, o=op: e.tensor_tensor(d, x, y, o), waits)

    def ts(self, eng, dst, a, s1, op, s2=None, op2=None, accum=None, waits=()):
        def fn(e, d=dst, x=a, s1=s1, s2=s2, op=op, op2=op2, acc=accum):
            if s2 is None and acc is None:
                return e.tensor_single_scalar(d, x, s1, op)
            return e.tensor_scalar(d, x, s1, s2, op, op2 if op2 is not None else ALU.bypass,
                                   accum_out=acc)
        return self.P.chain(eng, fn, waits)

    def stt(self, eng, dst, in0, scalar, in1, op0, op1, waits=()):
        fn = lambda e, d=dst, a=in0, s=scalar, b=in1, o0=op0, o1=op1: e.scalar_tensor_tensor(
            d, a, s, b, o0, o1)
        return self.P.chain(eng, fn, waits)

    def red(self, dst, src, op, negate=False, waits=()):
        fn = lambda e, d=dst, s=src, o=op, n=negate: e.tensor_reduce(d, s, AX.X, o, negate=n)
        return self.P.chain("vector", fn, waits)

    def recip(self, dst, src, waits=()):
        return self.P.chain("vector", lambda e, d=dst, s=src: e.reciprocal(d, s), waits)

    def memset(self, eng, dst, val, waits=()):
        return self.P.chain(eng, lambda e, d=dst, v=val: e.memset(d, v), waits)

    def scan(self, dst, d0, d1, init, op0, op1, waits=()):
        fn = lambda e, d=dst, a=d0, b=d1, i=init, o0=op0, o1=op1: e.tensor_tensor_scan(
            d, a, b, i, o0, o1)
        return self.P.chain("vector", fn, waits)

    def mm(self, out, lhsT, rhs, start, stop, waits=(), inc=None):
        fn = lambda e, o=out, l=lhsT, r=rhs, st=start, sp=stop: e.matmul(
            o, l, r, start=st, stop=sp)
        return self.P.emit("tensor", fn, waits, inc=inc)

    def tr(self, out, src, ident, waits=(), inc=None):
        fn = lambda e, o=out, s=src, i=ident: e.transpose(o, s, i)
        return self.P.emit("tensor", fn, waits, inc=inc)



def build_program(stop_after="all", debug_taps=()):
    nc = bass.Bass("TRN2", target_bir_lowering=False, debug=False)
    B = Builder(nc)
    P = B.P
    P.sem("pe")

    xin = B.dram("xin", [D, T], F32, kind="ExternalInput")
    aQw = B.dram("aQw", [D, D], F32, kind="ExternalInput")
    aKw = B.dram("aKw", [D, D], F32, kind="ExternalInput")
    aVw = B.dram("aVw", [D, D], F32, kind="ExternalInput")
    bWqT = B.dram("bWqT", [D, D], F32, kind="ExternalInput")
    bWkT = B.dram("bWkT", [D, D], F32, kind="ExternalInput")
    bWvT = B.dram("bWvT", [D, D], F32, kind="ExternalInput")
    bWoT = B.dram("bWoT", [D, D], F32, kind="ExternalInput")
    bWk_orig = B.dram("bWk_orig", [D, D], F32, kind="ExternalInput")
    bwsblk = B.dram("bwsblk", [D, H], F32, kind="ExternalInput")
    fWqkv = B.dram("fWqkv", [DEPTH, D, 3 * D], F32, kind="ExternalInput")
    fWo = B.dram("fWo", [DEPTH, D, D], F32, kind="ExternalInput")
    fW1 = B.dram("fW1", [DEPTH, D, 2 * DI], F32, kind="ExternalInput")
    fW2 = B.dram("fW2", [DEPTH, DI, D], F32, kind="ExternalInput")
    fb1 = B.dram("fb1", [DEPTH, 128, 44], F32, kind="ExternalInput")
    fb2 = B.dram("fb2", [DEPTH, 128, NCH], F32, kind="ExternalInput")
    flng = B.dram("flng", [DEPTH, 128, NCH], F32, kind="ExternalInput")
    flnb = B.dram("flnb", [DEPTH, 128, NCH], F32, kind="ExternalInput")
    fqs = B.dram("fqs", [DEPTH, 128, 2], F32, kind="ExternalInput")
    fks = B.dram("fks", [DEPTH, 128, 2], F32, kind="ExternalInput")
    tbl_cs = B.dram("tbl_cs", [HD, T], F32, kind="ExternalInput")
    tbl_ss = B.dram("tbl_ss", [HD, T], F32, kind="ExternalInput")
    tbl_ci = B.dram("tbl_ci", [HD, T], F32, kind="ExternalInput")
    tbl_si = B.dram("tbl_si", [HD, T], F32, kind="ExternalInput")
    c_ident = B.dram("c_ident", [128, 128], BF16, kind="ExternalInput")
    c_identf = B.dram("c_identf", [128, 128], F32, kind="ExternalInput")
    c_stair = B.dram("c_stair", [128, 1024], BF16, kind="ExternalInput")
    c_band = B.dram("c_band", [3, 128, 384], BF16, kind="ExternalInput")
    c_prot = B.dram("c_prot", [128, 128], BF16, kind="ExternalInput")
    c_ones1 = B.dram("c_ones1", [128, 1], BF16, kind="ExternalInput")
    c_onesr = B.dram("c_onesr", [1, 128], F32, kind="ExternalInput")
    c_onescf = B.dram("c_onescf", [128, 1], F32, kind="ExternalInput")
    c_iota64 = B.dram("c_iota64", [128, HD], F32, kind="ExternalInput")
    c_hsel = B.dram("c_hsel", [NCH, 128, H], BF16, kind="ExternalInput")
    c_hsel2 = B.dram("c_hsel2", [128, 2], BF16, kind="ExternalInput")
    c_pairsel = B.dram("c_pairsel", [2, 128], F32, kind="ExternalInput")
    c_hselT = B.dram("c_hselT", [H, NCH, 128], BF16, kind="ExternalInput")
    c_gsel = B.dram("c_gsel", [128, H], BF16, kind="ExternalInput")
    c_gselT = B.dram("c_gselT", [H, 128], F32, kind="ExternalInput")
    c_fvals = B.dram("c_fvals", [128, 8], F32, kind="ExternalInput")
    c_onesrb = B.dram("c_onesrb", [1, 512], BF16, kind="ExternalInput")
    c_fbr = B.dram("c_fbr", [DEPTH, 2, 30, 128], BF16, kind="ExternalInput")

    out = B.dram("xout", [D, T], F32, kind="ExternalOutput")

    S_a = B.dram("S_a", [D, T], F32)
    S_b = B.dram("S_b", [D, T], F32)
    S0 = B.dram("S0", [D, T], F32)
    S1 = B.dram("S1", [D, T], F32)
    XNB = B.dram("XNB", [D, T], BF16)
    QS = B.dram("QS", [D, T], BF16)
    KS = B.dram("KS", [D, T], BF16)
    KTM = B.dram("KTM", [T, D], BF16)
    VTM = B.dram("VTM", [T, D], BF16)
    RANKB = B.dram("RANKB", [T, H], F32)
    IDXB = B.dram("IDXB", [128, 8 * 1024], F32)

    dbg = {}
    for tap in debug_taps:
        if tap == "xnb":
            dbg["dbg_xnb"] = B.dram("dbg_xnb", [D, T], BF16, kind="ExternalOutput")
        if tap == "qkv":
            dbg["dbg_q"] = B.dram("dbg_q", [D, T], BF16, kind="ExternalOutput")
            dbg["dbg_k"] = B.dram("dbg_k", [D, T], BF16, kind="ExternalOutput")
            dbg["dbg_v"] = B.dram("dbg_v", [T, D], BF16, kind="ExternalOutput")

    def fm(apx):
        return apx.rearrange("(c p) t -> p c t", p=128)

    def tm(apx):
        return apx.rearrange("(s p) d -> p s d", p=128)

    sb_ident = B.sb("sb_ident", [128, 128], BF16)
    sb_identf = B.sb("sb_identf", [128, 128], F32)
    sb_stair = B.sb("sb_stair", [128, 1024], BF16)
    sb_band = B.sb("sb_band", [128, 3, 384], BF16)
    sb_prot = B.sb("sb_prot", [128, 128], BF16)
    sb_ones1 = B.sb("sb_ones1", [128, 1], BF16)
    sb_onesr = B.sb("sb_onesr", [1, 128], F32)
    sb_onescf = B.sb("sb_onescf", [128, 1], F32)
    sb_iota64 = B.sb("sb_iota64", [128, HD], F32)
    sb_hsel = B.sb("sb_hsel", [128, NCH, H], BF16)
    sb_hsel2 = B.sb("sb_hsel2", [128, 2], BF16)
    sb_pairsel = B.sb("sb_pairsel", [2, 128], F32)
    sb_gsel = B.sb("sb_gsel", [128, H], BF16)
    sb_gselT = B.sb("sb_gselT", [H, 128], F32)
    sb_fb1 = B.sb("sb_fb1", [128, DEPTH, 44], F32)
    sb_fb2 = B.sb("sb_fb2", [128, DEPTH, NCH], F32)
    sb_flng = B.sb("sb_flng", [128, DEPTH, NCH], F32)
    sb_flnb = B.sb("sb_flnb", [128, DEPTH, NCH], F32)
    sb_fqs = B.sb("sb_fqs", [128, DEPTH, 2], F32)
    sb_fks = B.sb("sb_fks", [128, DEPTH, 2], F32)
    sb_fvals = B.sb("sb_fvals", [128, 8], F32)
    sb_onesrb = B.sb("sb_onesrb", [65, 512], BF16)
    for _i, _v in enumerate(FVALS):
        nc.const_aps.register(F32, _v, sb_fvals[:, _i:_i + 1])

    P.sem("ldc")
    for dst, src in [
        (sb_ident[:], c_ident[:, :]), (sb_identf[:], c_identf[:, :]),
        (sb_stair[:], c_stair[:, :]),
        (sb_band[:], c_band.rearrange("m p n -> p m n")), (sb_prot[:], c_prot[:, :]),
        (sb_ones1[:], c_ones1[:, :]), (sb_onesr[:], c_onesr[:, :]),
        (sb_onescf[:], c_onescf[:, :]),
        (sb_iota64[:], c_iota64[:, :]), (sb_hsel[:], c_hsel.rearrange("c p h -> p c h")),
        (sb_hsel2[:], c_hsel2[:, :]), (sb_pairsel[:], c_pairsel[:, :]),
        (sb_gsel[:], c_gsel[:, :]), (sb_gselT[:], c_gselT[:, :]),
        (sb_fb1[:], fb1.rearrange("l p n -> p l n")), (sb_fb2[:], fb2.rearrange("l p n -> p l n")),
        (sb_flng[:], flng.rearrange("l p n -> p l n")), (sb_flnb[:], flnb.rearrange("l p n -> p l n")),
        (sb_fqs[:], fqs.rearrange("l p n -> p l n")), (sb_fks[:], fks.rearrange("l p n -> p l n")),
        (sb_fvals[:], c_fvals[:, :]), (sb_onesrb[0:1, :], c_onesrb[:, :]),
        (sb_onesrb[64:65, :], c_onesrb[:, :]),
    ]:
        P.dma("sync", dst, src, "ldc")
    LDC = ("ldc", P.cnt["ldc"])
    for e in ENGINES:
        P.wait_only(e, [LDC])

    TTL = 512
    NTL = T // TTL

    def ln_pass(name, src, src2=None, dst_bf=None, dst_f32=None, dst_sum=None,
                no_norm=False, idx_hook=None):
        with ExitStack() as ph:
            sbp = lambda nm, shp, dt=F32: ph.enter_context(
                nc.sbuf_tensor(B.unique(nm), shp, dt))
            ln_x = [sbp("lnx", [128, NCH, TTL]) for _ in range(2)]
            ln_x2 = [sbp("lnx2", [128, NCH, TTL]) for _ in range(2)] if src2 is not None else None
            ln_xb = [sbp("lnxb", [128, NCH, TTL], BF16) for _ in range(2)]
            ln_ob = [sbp("lnob", [128, NCH, TTL], BF16) for _ in range(2)] if dst_bf is not None else None
            ln_of = [sbp("lnof", [128, NCH, TTL]) for _ in range(2)]
            ln_xq = sbp("lnxq", [128, NCH, TTL], BF16)
            R = sbp("lnrows", [1, 3 * 512])
            ln_mb = sbp("lnmb", [128, TTL])
            ln_rb = sbp("lnrb", [128, TTL])
            ps_s = ph.enter_context(nc.psum_tensor(B.unique("lnS"), [1, TTL], F32))
            ps_q = ph.enter_context(nc.psum_tensor(B.unique("lnQ"), [1, TTL], F32))
            ps_m = ph.enter_context(nc.psum_tensor(B.unique("lnM"), [128, TTL], F32))
            ps_r = ph.enter_context(nc.psum_tensor(B.unique("lnR"), [128, TTL], F32))
            ld = P.sem(name + "_ld")
            st = P.sem(name + "_st")
            slot_rel = [None, None]
            srcv = fm(src)
            src2v = fm(src2) if src2 is not None else None
            dbv = fm(dst_bf) if dst_bf is not None else None
            dfv = fm(dst_f32) if dst_f32 is not None else None
            dsv = fm(dst_sum) if dst_sum is not None else None
            for i in range(NTL):
                sl = i % 2
                t0 = i * TTL
                w = list(slot_rel[sl]) if slot_rel[sl] else []
                P.dma("sync", ln_x[sl][:], srcv[:, :, t0:t0 + TTL], ld, waits=w)
                if src2 is not None:
                    P.dma("sync", ln_x2[sl][:], src2v[:, :, t0:t0 + TTL], ld, waits=w)
                ldw = [(ld, P.cnt[ld])]
                if src2 is not None:
                    B.tt("vector", ln_x[sl][:], ln_x[sl][:], ln_x2[sl][:], ALU.add, waits=ldw)
                    ldw = [P.mark("vector")]
                    if dst_sum is not None:
                        P.dma("sync", dsv[:, :, t0:t0 + TTL], ln_x[sl][:], st, waits=ldw)
                if no_norm:
                    slot_rel[sl] = ldw + [(st, P.cnt[st])]
                    continue
                B.copy("vector", ln_xb[sl][:], ln_x[sl][:], waits=ldw)
                m_xb = P.mark("vector")
                B.act(ln_xq[:], ln_xb[sl][:], AF.Square, waits=[m_xb])
                m_xq = P.mark("scalar")
                for c in range(NCH):
                    B.mm(ps_s[:], sb_ones1[:], ln_xb[sl][:, c, :], c == 0, c == NCH - 1,
                         waits=[m_xb, P.mark("scalar")] if c == 0 else (),
                         inc="pe" if c == NCH - 1 else None)
                pe_s = ("pe", P.cnt["pe"])
                for c in range(NCH):
                    B.mm(ps_q[:], sb_ones1[:], ln_xq[:, c, :], c == 0, c == NCH - 1,
                         waits=[m_xq] if c == 0 else (),
                         inc="pe" if c == NCH - 1 else None)
                pe_q = ("pe", P.cnt["pe"])
                r0 = R[0:1, 0:TTL]
                r1 = R[0:1, 512:512 + TTL]
                r2 = R[0:1, 1024:1024 + TTL]
                B.scopy(r0, ps_s[:], waits=[pe_s, P.mark("vector")])
                B.scopy(r1, ps_q[:], waits=[pe_q])
                m_rows = P.mark("scalar")
                B.ts("vector", r0, r0, 1.0 / D, ALU.mult, waits=[m_rows])
                B.ts("vector", r1, r1, 1.0 / D, ALU.mult)
                B.tt("vector", r2, r0, r0, ALU.mult)
                B.tt("vector", r1, r1, r2, ALU.subtract)
                m_var = P.mark("vector")
                B.act(r2, r1, AF.Ln, bias=1e-5, waits=[m_var])
                B.act(r2, r2, AF.Exp, scale=-0.5)
                m_rstd = P.mark("scalar")
                B.mm(ps_m[:], sb_onesr[:], r0, True, True,
                     waits=[m_var, m_rstd, P.mark("scalar")], inc="pe")
                pe_m = ("pe", P.cnt["pe"])
                B.mm(ps_r[:], sb_onesr[:], r2, True, True, inc="pe")
                pe_r = ("pe", P.cnt["pe"])
                B.scopy(ln_mb[:], ps_m[:], waits=[pe_m, P.mark("vector")])
                B.scopy(ln_rb[:], ps_r[:], waits=[pe_r])
                m_bc = P.mark("scalar")
                B.tt("vector", ln_of[sl][:], ln_x[sl][:], bc_mid(ln_mb[:], NCH),
                     ALU.subtract, waits=[m_bc])
                if dst_f32 is not None or idx_hook is not None:
                    B.tt("vector", ln_of[sl][:], ln_of[sl][:], bc_mid(ln_rb[:], NCH), ALU.mult)
                    m_nf = P.mark("vector")
                    if dst_bf is not None:
                        B.copy("vector", ln_ob[sl][:], ln_of[sl][:])
                    if dst_f32 is not None:
                        P.dma("sync", dfv[:, :, t0:t0 + TTL], ln_of[sl][:], st, waits=[m_nf])
                elif dst_bf is not None:
                    B.tt("vector", ln_ob[sl][:], ln_of[sl][:], bc_mid(ln_rb[:], NCH), ALU.mult)
                m_out = P.mark("vector")
                if dst_bf is not None:
                    P.dma("sync", dbv[:, :, t0:t0 + TTL], ln_ob[sl][:], st, waits=[m_out])
                if idx_hook is not None:
                    idx_hook(i, t0, ln_of[sl], m_out)
                    slot_rel[sl] = [m_out, (st, P.cnt[st]), ("pe", P.cnt["pe"])]
                else:
                    slot_rel[sl] = [m_out, (st, P.cnt[st])]
            P.fence()

    def proj_pass(name, wlist, src_b, outs, tile_cols=512):
        with ExitStack() as ph:
            lw = P.sem(name + "_lw")
            sbw = []
            for wi, (wd, nco) in enumerate(wlist):
                wt = ph.enter_context(nc.sbuf_tensor(B.unique(name + "w"),
                                                     [128, NCH, nco * 128], BF16))
                P.dma("gpsimd", wt[:], wd.rearrange("(c p) o -> p c o", p=128), lw)
                sbw.append(wt)
            LW = (lw, P.cnt[lw])
            nfm = sum(1 for k, _ in outs if k == "fm")
            xb = [ph.enter_context(nc.sbuf_tensor(B.unique(name + "x"),
                                                  [128, NCH, tile_cols], BF16)) for _ in range(2)]
            obs = {}
            for oi, (kind, od) in enumerate(outs):
                if kind == "fm":
                    obs[oi] = [ph.enter_context(nc.sbuf_tensor(
                        B.unique(name + "o"), [128, NCH, tile_cols], BF16)) for _ in range(2)]
                else:
                    obs[oi] = [ph.enter_context(nc.sbuf_tensor(
                        B.unique(name + "ot"), [128, tile_cols // 128, D], BF16)) for _ in range(2)]
            pss = [ph.enter_context(nc.psum_tensor(B.unique(name + "p"), [128, tile_cols], F32))
                   for _ in range(4)]
            bank_rel = [None] * 4
            ld = P.sem(name + "_ld")
            st = P.sem(name + "_st")
            srcv3 = fm(src_b)
            slot_rel = [None, None]
            ntiles = T // tile_cols
            pb = 0
            for i in range(ntiles):
                sl = i % 2
                t0 = i * tile_cols
                w = list(slot_rel[sl]) if slot_rel[sl] else [LW]
                P.dma("sync", xb[sl][:], srcv3[:, :, t0:t0 + tile_cols], ld, waits=w)
                ldw = [(ld, P.cnt[ld])]
                rel_marks = []
                for oi, (kind, od) in enumerate(outs):
                    wt = sbw[oi]
                    ob = obs[oi][sl]
                    if kind == "fm":
                        for co in range(NCH):
                            bk = pb % 4
                            ps = pss[bk]
                            pb += 1
                            w0 = ldw + [LW] + ([bank_rel[bk]] if bank_rel[bk] else [])
                            for ci in range(NCH):
                                B.mm(ps[:], wt[:, ci, co * 128:(co + 1) * 128],
                                     xb[sl][:, ci, :], ci == 0, ci == NCH - 1,
                                     waits=w0 if ci == 0 else (),
                                     inc="pe" if ci == NCH - 1 else None)
                            B.act(ob[:, co, :], ps[:], AF.Copy, waits=[("pe", P.cnt["pe"])])
                            bank_rel[bk] = P.mark("scalar")
                        m_o = P.mark("scalar")
                        P.dma("sync", fm(od)[:, :, t0:t0 + tile_cols], ob[:], st, waits=[m_o])
                        rel_marks.append(m_o)
                    else:
                        nts = tile_cols // 128
                        for s4 in range(nts):
                            for eh in range(2):
                                bk = pb % 4
                                ps = pss[bk]
                                pb += 1
                                w0 = ldw + [LW] + ([bank_rel[bk]] if bank_rel[bk] else [])
                                for ci in range(NCH):
                                    B.mm(ps[:], xb[sl][:, ci, s4 * 128:(s4 + 1) * 128],
                                         wt[:, ci, eh * 512:(eh + 1) * 512],
                                         ci == 0, ci == NCH - 1,
                                         waits=w0 if ci == 0 else (),
                                         inc="pe" if ci == NCH - 1 else None)
                                B.act(ob[:, s4, eh * 512:(eh + 1) * 512], ps[:], AF.Copy,
                                      waits=[("pe", P.cnt["pe"])])
                                bank_rel[bk] = P.mark("scalar")
                        m_o = P.mark("scalar")
                        P.dma("sync", tm(od)[:, i * nts:(i + 1) * nts, :], ob[:], st,
                              waits=[m_o])
                        rel_marks.append(m_o)
                slot_rel[sl] = rel_marks + [(st, P.cnt[st])]
            P.fence()

    def attn_a_flash(x_res_src, x_dst):
        QSUP = 1024
        KSUP = 1024
        NQT = QSUP // 128
        KCH = KSUP // 128
        with ExitStack() as ph:
            sbp = lambda nm, shp, dt: ph.enter_context(nc.sbuf_tensor(B.unique(nm), shp, dt))
            sb_qt = sbp("faq", [128, NCH, QSUP], BF16)
            sb_kt = sbp("fak", [128, NCH, KSUP], BF16)
            sb_v = sbp("fav", [128, KCH, D], BF16)
            sb_o = sbp("fao", [128, NCH, QSUP], F32)
            sb_pr = sbp("fapr", [128, KCH, QSUP], BF16)
            sb_ml = sbp("faml", [128, 64], F32)
            sb_row = sbp("farow", [1, 2 * QSUP], F32)
            sb_ab = sbp("faab", [128, QSUP], F32)
            sb_xr = sbp("faxr", [128, NCH, QSUP], F32)
            psp = lambda nm, shp, dt: ph.enter_context(nc.psum_tensor(B.unique(nm), shp, dt))
            ps_sim = [psp("fas", [128, 512], F32) for _ in range(2)]
            ps_tr = [psp("fat", [128, 128], BF16) for _ in range(2)]
            ps_pv = [psp("fap", [128, 512], F32) for _ in range(2)]
            ps_row = psp("far", [1, 512], F32)
            ps_ab = psp("faa", [128, 512], F32)
            lq = P.sem("fa_lq")
            lk = P.sem("fa_lk")
            lx = P.sem("fa_lx")
            stx = P.sem("fa_st")
            qsv, ksv = fm(QS), fm(KS)
            xrv, xdv = fm(x_res_src), fm(x_dst)
            vv = tm(VTM)
            n_qs = T // QSUP
            prev_qs_rel = None
            for qs in range(n_qs):
                q0 = qs * QSUP
                n_ks = ((qs + 1) * QSUP + KSUP - 1) // KSUP
                w = list(prev_qs_rel) if prev_qs_rel else []
                P.dma("sync", sb_qt[:], qsv[:, :, q0:q0 + QSUP], lq, waits=w)
                P.dma("sync", sb_xr[:], xrv[:, :, q0:q0 + QSUP], lx, waits=w)
                mq = [(lq, P.cnt[lq])]
                mx = [(lx, P.cnt[lx])]
                B.memset("vector", sb_ml[:, 0:8], -3.0e38, waits=w)
                B.memset("vector", sb_ml[:, 8:16], 0.0)
                for c in range(NCH):
                    B.memset("vector", sb_o[:, c, :], 0.0)
                m_init = P.mark("vector")
                last_mg_m = None
                for ks in range(n_ks):
                    k0 = ks * KSUP
                    wk = [m_init] if ks == 0 else list(last_mg_m)
                    P.dma("sync", sb_kt[:], ksv[:, :, k0:k0 + KSUP], lk, waits=wk)
                    P.dma("sync", sb_v[:], vv[:, k0 // 128:(k0 + KSUP) // 128, :], lk,
                          waits=wk)
                    mk = [(lk, P.cnt[lk])]
                    tile_cov = []
                    m_exp_last = None
                    for j in range(NQT):
                        jq = qs * NQT + j
                        qend = (jq + 1) * 128
                        ncov = max(0, min(KSUP, qend - k0))
                        nsub = (ncov + 511) // 512
                        tile_cov.append((ncov + 127) // 128)
                        if nsub == 0:
                            continue
                        for u in range(nsub):
                            ps = ps_sim[u % 2]
                            k0s = k0 + u * 512
                            diag = (k0s <= jq * 128) and (jq * 128 < k0s + 512)
                            bw = mq + mk + [P.mark("scalar")]
                            first = True
                            if diag:
                                pvar = (jq * 128 - k0s) // 128
                                B.mm(ps[:], sb_ident[:],
                                     sb_stair[:, 384 - 128 * pvar:896 - 128 * pvar],
                                     True, False, waits=bw)
                                first = False
                            for ci in range(NCH):
                                B.mm(ps[:], sb_qt[:, ci, j * 128:(j + 1) * 128],
                                     sb_kt[:, ci, u * 512:(u + 1) * 512],
                                     first and ci == 0, ci == NCH - 1,
                                     waits=bw if (first and ci == 0) else (),
                                     inc="pe" if ci == NCH - 1 else None)
                            pes = ("pe", P.cnt["pe"])
                            B.red(sb_ml[:, 16 + u:17 + u], ps[:], ALU.max, waits=[pes])
                        B.red(sb_ml[:, 32:33], sb_ml[:, 16:16 + nsub], ALU.max)
                        B.tt("vector", sb_ml[:, 32:33], sb_ml[:, 0 + j:1 + j],
                             sb_ml[:, 32:33], ALU.max)
                        B.tt("vector", sb_ml[:, 33:34], sb_ml[:, 0 + j:1 + j],
                             sb_ml[:, 32:33], ALU.subtract)
                        B.copy("vector", sb_ml[:, 0 + j:1 + j], sb_ml[:, 32:33])
                        B.ts("vector", sb_ml[:, 34:35], sb_ml[:, 32:33], -1.0 / 32.0, ALU.mult)
                        m_v = P.mark("vector")
                        B.act(sb_ml[:, 24 + j:25 + j], sb_ml[:, 33:34], AF.Exp,
                              scale=1.0 / 32.0, waits=[m_v])
                        for u in range(nsub):
                            ps = ps_sim[u % 2]
                            B.act(sb_pr[:, 4 * u:4 * (u + 1), j * 128:(j + 1) * 128],
                                  ps[:], AF.Exp, scale=1.0 / 32.0, bias=sb_ml[:, 34:35],
                                  accum=sb_ml[:, 20 + u:21 + u])
                        m_exp_last = P.mark("scalar")
                        B.red(sb_ml[:, 33:34], sb_ml[:, 20:20 + nsub], ALU.add,
                              waits=[m_exp_last])
                        B.tt("vector", sb_ml[:, 8 + j:9 + j], sb_ml[:, 8 + j:9 + j],
                             sb_ml[:, 24 + j:25 + j], ALU.mult)
                        B.tt("vector", sb_ml[:, 8 + j:9 + j], sb_ml[:, 8 + j:9 + j],
                             sb_ml[:, 33:34], ALU.add)
                    for j in range(NQT):
                        cov = tile_cov[j]
                        if cov < KCH:
                            B.memset("vector", sb_pr[:, cov:KCH, j * 128:(j + 1) * 128], 0.0)
                    m_rag = P.mark("vector")
                    for hf in range(2):
                        for j4 in range(4):
                            j = 4 * hf + j4
                            jq = qs * NQT + j
                            covered = (jq + 1) * 128 > k0
                            src_col = sb_ml[:, 24 + j:25 + j] if covered else sb_onescf[:]
                            B.mm(ps_row[0:1, j4 * 128:(j4 + 1) * 128], src_col, sb_identf[:],
                                 True, True,
                                 waits=[m_exp_last, P.mark("scalar")] if j4 == 0 else (),
                                 inc="pe" if j4 == 3 else None)
                        B.scopy(sb_row[0:1, hf * 512:(hf + 1) * 512], ps_row[0:1, :],
                                waits=[("pe", P.cnt["pe"])])
                        m_ar = P.mark("scalar")
                        B.mm(ps_ab[:], sb_onesr[:], sb_row[0:1, hf * 512:(hf + 1) * 512],
                             True, True, waits=[m_ar, P.mark("scalar")], inc="pe")
                        B.scopy(sb_ab[:, hf * 512:(hf + 1) * 512], ps_ab[:],
                                waits=[("pe", P.cnt["pe"])])
                    m_ab = P.mark("scalar")
                    maxcov = max(tile_cov)
                    tb = 0
                    tr_rel = [None, None]
                    for j in range(NQT):
                        for kc in range(tile_cov[j]):
                            bk = tb % 2
                            tb += 1
                            w0 = [m_exp_last, m_rag]
                            if tr_rel[bk]:
                                w0.append(tr_rel[bk])
                            B.tr(ps_tr[bk][:], sb_pr[:, kc, j * 128:(j + 1) * 128],
                                 sb_ident[:], waits=w0, inc="pe")
                            B.copy("vector", sb_pr[:, kc, j * 128:(j + 1) * 128],
                                   ps_tr[bk][:], waits=[("pe", P.cnt["pe"])])
                            tr_rel[bk] = P.mark("vector")
                    m_trv = P.mark("vector")
                    for dc in range(NCH):
                        for half in range(2):
                            ps = ps_pv[half]
                            for kc in range(maxcov):
                                B.mm(ps[:], sb_v[:, kc, dc * 128:(dc + 1) * 128],
                                     sb_pr[:, kc, half * 512:(half + 1) * 512],
                                     kc == 0, kc == maxcov - 1,
                                     waits=[m_trv, P.mark("vector")] if kc == 0 else (),
                                     inc="pe" if kc == maxcov - 1 else None)
                        pe_pv = ("pe", P.cnt["pe"])
                        B.tt("vector", sb_o[:, dc, :], sb_o[:, dc, :], sb_ab[:],
                             ALU.mult, waits=[pe_pv, m_ab])
                        B.tt("vector", sb_o[:, dc, 0:512], sb_o[:, dc, 0:512],
                             ps_pv[0][:], ALU.add)
                        B.tt("vector", sb_o[:, dc, 512:1024], sb_o[:, dc, 512:1024],
                             ps_pv[1][:], ALU.add)
                    last_mg_m = [P.mark("vector")]
                B.recip(sb_ml[:, 40:48], sb_ml[:, 8:16], waits=last_mg_m)
                m_li = P.mark("vector")
                for hf in range(2):
                    for j4 in range(4):
                        j = 4 * hf + j4
                        B.mm(ps_row[0:1, j4 * 128:(j4 + 1) * 128], sb_ml[:, 40 + j:41 + j],
                             sb_identf[:], True, True,
                             waits=[m_li, P.mark("scalar")] if j4 == 0 else (),
                             inc="pe" if j4 == 3 else None)
                    B.scopy(sb_row[0:1, QSUP + hf * 512:QSUP + (hf + 1) * 512],
                            ps_row[0:1, :], waits=[("pe", P.cnt["pe"])])
                    m_lr = P.mark("scalar")
                    B.mm(ps_ab[:], sb_onesr[:],
                         sb_row[0:1, QSUP + hf * 512:QSUP + (hf + 1) * 512],
                         True, True, waits=[m_lr, P.mark("scalar")], inc="pe")
                    B.scopy(sb_ab[:, hf * 512:(hf + 1) * 512], ps_ab[:],
                            waits=[("pe", P.cnt["pe"])])
                m_lb = P.mark("scalar")
                for dc in range(NCH):
                    B.tt("vector", sb_o[:, dc, :], sb_o[:, dc, :], sb_ab[:], ALU.mult,
                         waits=[m_lb])
                    B.tt("vector", sb_o[:, dc, :], sb_o[:, dc, :], sb_xr[:, dc, :],
                         ALU.add, waits=mx)
                m_fin = P.mark("vector")
                P.dma("sync", xdv[:, :, q0:q0 + QSUP], sb_o[:], stx, waits=[m_fin])
                prev_qs_rel = [m_fin, (stx, P.cnt[stx])]
            P.fence()

    def inner_attn(l, ysrc, ydst):
        TTA = 256
        NT = T // TTA
        with ExitStack() as ph:
            sbp = lambda nm, shp, dt: ph.enter_context(nc.sbuf_tensor(B.unique(nm), shp, dt))
            psp = lambda nm, shp, dt=F32: ph.enter_context(nc.psum_tensor(B.unique(nm), shp, dt))
            lw = P.sem("ia_lw")
            w_qkv = sbp("iawq", [128, NCH, 3 * D], BF16)
            w_o = sbp("iawo", [128, NCH, D], BF16)
            hselT = sbp("iaht", [H, NCH, 128], BF16)
            P.dma("gpsimd", w_qkv[:],
                  fWqkv.rearrange("l (c p) n -> p l c n", p=128)[:, l, :, :], lw)
            P.dma("gpsimd", w_o[:],
                  fWo.rearrange("l (c p) n -> p l c n", p=128)[:, l, :, :], lw)
            P.dma("sync", hselT[:], c_hselT[:, :, :], lw)
            LW = (lw, P.cnt[lw])
            ysh = [sbp("iay", [128, NCH, TTA + 1], F32) for _ in range(2)]
            yshb = [sbp("iayb", [128, NCH, TTA], BF16) for _ in range(2)]
            qkf = sbp("iaqk", [128, NCH, TTA], F32)
            qkq = sbp("iaqq", [128, NCH, TTA], BF16)
            qpb = sbp("iaqp", [128, NCH, TTA], BF16)
            kwin = sbp("iakw", [128, NCH, 2 * TTA], BF16)
            vwin = sbp("iavw", [128, 4, D], BF16)
            tb4 = sbp("iatb", [128, 4, TTA], F32)
            ab4 = sbp("iaab", [128, 4, TTA], F32)
            rn2 = sbp("iarn", [16, TTA], F32)
            rn2b = sbp("iarnb", [16, TTA], BF16)
            probs_all = sbp("iapr", [128, 32, 384], BF16)
            prT = [sbp("iapt", [128, 3 * 128], BF16) for _ in range(2)]
            tmpf = sbp("iatm", [128, TTA], F32)
            sml = sbp("iasm", [128, 128], F32)
            aout = sbp("iaao", [128, NCH, TTA], BF16)
            yout = [sbp("iayo", [128, NCH, TTA], F32) for _ in range(2)]
            ps_mm = [psp("iam", [128, 512]) for _ in range(2)]
            ps_nq = psp("ian", [16, TTA])
            ps_rb = psp("iab", [128, TTA])
            ps_rot = psp("iar", [128, TTA])
            ps_sim = [psp("ias", [128, 384]) for _ in range(2)]
            ps_pvo = psp("iap", [128, 128])
            rot_bf = ps_rot[:].bitcast(BF16)
            ld = P.sem("ia_ld")
            st = P.sem("ia_st")
            tbs = P.sem("ia_tb")
            srcv, dstv = fm(ysrc), fm(ydst)
            slot_rel = [None, None]
            tb_rel = None
            kv_rel = None
            mm_rel = [None, None]
            nq_rel = None
            rb_rel = None
            rot_rel = None
            sim_rel = [None, None]
            prT_rel = [None, None]
            pvo_rel = None
            ldw_i = {}

            def emit_load(i):
                if i >= NT:
                    return
                sl = i % 2
                t0 = i * TTA
                w = list(slot_rel[sl]) if slot_rel[sl] else [LW]
                if i == 0:
                    P.dma("sync", ysh[sl][:, :, 1:TTA + 1], srcv[:, :, 0:TTA], ld, waits=w)
                    B.memset("vector", ysh[sl][:, :, 0:1], 0.0, waits=w)
                else:
                    P.dma("sync", ysh[sl][:], srcv[:, :, t0 - 1:t0 + TTA], ld, waits=w)
                ldw_i[i] = [(ld, P.cnt[ld])] + ([P.mark("vector")] if i == 0 else [])

            def emit_cast(i):
                if i >= NT:
                    return
                sl = i % 2
                B.copy("vector", yshb[sl][:, 0:4, :], ysh[sl][:, 0:4, 1:TTA + 1],
                       waits=ldw_i[i] + [("pe", P.cnt["pe"])])
                B.copy("vector", yshb[sl][:, 4:8, :], ysh[sl][:, 4:8, 0:TTA])

            emit_load(0)
            emit_cast(0)
            for i in range(NT):
                sl = i % 2
                t0 = i * TTA
                tw = list(tb_rel) if tb_rel else []
                for k4, tb in enumerate((tbl_cs, tbl_ss, tbl_ci, tbl_si)):
                    P.dma("sync", tb4[:, k4, :],
                          bass.AP(tb.tensor, tb.offset + t0, [[0, 2], [T, HD], [1, TTA]]),
                          tbs, waits=tw)
                tbw = [(tbs, P.cnt[tbs])]
                for k4, col in ((0, sb_fqs[:, l, 0:1]), (1, sb_fqs[:, l, 1:2]),
                                (2, sb_fks[:, l, 0:1]), (3, sb_fks[:, l, 1:2])):
                    B.ts("vector", ab4[:, k4, :], tb4[:, k4, :], col, ALU.mult, waits=tbw)
                tb_rel = [P.mark("vector")]
                m_yb = P.mark("vector")
                wsh = list(kv_rel) if kv_rel else []
                if i == 0:
                    B.memset("vector", kwin[:, :, 0:TTA], 0.0, waits=wsh)
                    B.memset("vector", vwin[:, 0:2, :], 0.0)
                else:
                    B.copy("vector", kwin[:, :, 0:TTA], kwin[:, :, TTA:2 * TTA], waits=wsh)
                    B.copy("vector", vwin[:, 0:2, :], vwin[:, 2:4, :])
                m_shift = P.mark("vector")
                for tn in range(2):
                    for c in range(NCH):
                        bk = c % 2
                        ps = ps_mm[bk]
                        w0 = [m_yb, LW] + ([mm_rel[bk]] if mm_rel[bk] else [])
                        for ci in range(NCH):
                            B.mm(ps[:, 0:TTA], w_qkv[:, ci, (tn * 8 + c) * 128:(tn * 8 + c + 1) * 128],
                                 yshb[sl][:, ci, :], ci == 0, ci == NCH - 1,
                                 waits=w0 if ci == 0 else (),
                                 inc="pe" if ci == NCH - 1 else None)
                        B.scopy(qkf[:, c, :], ps[:, 0:TTA],
                                waits=[("pe", P.cnt["pe"]), P.mark("vector")])
                        B.act(qkq[:, c, :], qkf[:, c, :], AF.Square)
                        mm_rel[bk] = P.mark("scalar")
                    m_sq = P.mark("scalar")
                    for c in range(NCH):
                        B.mm(ps_nq[:], sb_hsel[:, c, :], qkq[:, c, :], c == 0, c == NCH - 1,
                             waits=([m_sq] + ([nq_rel] if nq_rel else [])) if c == 0 else (),
                             inc="pe" if c == NCH - 1 else None)
                    B.act(rn2[:], ps_nq[:], AF.Ln,
                          waits=[("pe", P.cnt["pe"]), P.mark("vector")])
                    nq_rel = P.mark("scalar")
                    B.act(rn2b[:], rn2[:], AF.Exp, scale=-0.5)
                    m_rn = P.mark("scalar")
                    for c in range(NCH):
                        B.mm(ps_rb[:], hselT[0:16, c, :], rn2b[:], True, True,
                             waits=[m_rn, LW] + ([rb_rel] if rb_rel else []), inc="pe")
                        pe_rb = ("pe", P.cnt["pe"])
                        B.tt("vector", qkf[:, c, :], qkf[:, c, :], ps_rb[:], ALU.mult,
                             waits=[pe_rb])
                        B.copy("vector", qkq[:, c, :], qkf[:, c, :])
                        rb_rel = P.mark("vector")
                        m_qnb = rb_rel
                        B.mm(ps_rot[:], sb_prot[:], qkq[:, c, :], True, True,
                             waits=[m_qnb] + ([rot_rel] if rot_rel else []), inc="pe")
                        pe_rot = ("pe", P.cnt["pe"])
                        B.tt("vector", qkf[:, c, :], qkf[:, c, :], ab4[:, 2 * tn, :], ALU.mult)
                        B.tt("vector", tmpf[:], ps_rot[:], ab4[:, 2 * tn + 1, :], ALU.mult,
                             waits=[pe_rot])
                        dstqp = qpb[:, c, :] if tn == 0 else kwin[:, c, TTA:2 * TTA]
                        ww = [m_shift] if tn == 1 else []
                        B.tt("vector", dstqp, qkf[:, c, :], tmpf[:], ALU.add, waits=ww)
                        rot_rel = P.mark("vector")
                m_kq = P.mark("vector")
                emit_load(i + 1)
                emit_cast(i + 1)
                for s2 in range(2):
                    for eh in range(2):
                        bk = eh % 2
                        ps = ps_mm[bk]
                        w0 = [m_yb, LW] + ([mm_rel[bk]] if mm_rel[bk] else [])
                        for ci in range(NCH):
                            B.mm(ps[:], yshb[sl][:, ci, s2 * 128:(s2 + 1) * 128],
                                 w_qkv[:, ci, 2 * D + eh * 512:2 * D + (eh + 1) * 512],
                                 ci == 0, ci == NCH - 1,
                                 waits=w0 if ci == 0 else (),
                                 inc="pe" if ci == NCH - 1 else None)
                        B.act(vwin[:, 2 + s2, eh * 512:(eh + 1) * 512], ps[:], AF.Copy,
                              waits=[("pe", P.cnt["pe"]), m_shift])
                        mm_rel[bk] = P.mark("scalar")
                m_v = P.mark("scalar")
                units = [(qsub, c, par) for qsub in range(2)
                         for c in range(NCH) for par in range(2)]
                NU = len(units)
                pes_u = [None] * NU
                mb_u = [None] * NU
                mexp_u = [None] * NU
                tile_pv_guard = kv_rel[0] if kv_rel else None
                for step in range(NU + 1):
                    if step < NU:
                        u = step
                        qsub, c, par = units[u]
                        qt0 = t0 + 128 * qsub
                        var = 1 if qt0 == 0 else (2 if qt0 == 128 else 0)
                        base = 64 * par
                        ps = ps_sim[u % 2]
                        w0 = [m_kq, m_v] + ([sim_rel[u % 2]] if sim_rel[u % 2] else [])
                        B.mm(ps[:], sb_ident[:], sb_band[:, var, :], True, False,
                             waits=w0)
                        B.mm(ps[:], qpb[base:base + 64, c, qsub * 128:(qsub + 1) * 128],
                             kwin[base:base + 64, c, qsub * 128:qsub * 128 + 384],
                             False, True, inc="pe")
                        pes_u[u] = ("pe", P.cnt["pe"])
                        B.red(sml[:, 4 * u:4 * u + 1], ps[:], ALU.max, negate=True,
                              waits=[pes_u[u]] + ([P.mark("scalar")] if u == 0 else []))
                        B.ts("vector", sml[:, 4 * u + 1:4 * u + 2],
                             sml[:, 4 * u:4 * u + 1], 8.0, ALU.mult)
                        mb_u[u] = P.mark("vector")
                    if step >= 1:
                        v = step - 1
                        qsub, c, par = units[v]
                        base = 64 * par
                        h = 2 * c + par
                        ps = ps_sim[v % 2]
                        prv = probs_all[:, v, :]
                        w1 = [mb_u[v], pes_u[v]]
                        if tile_pv_guard is not None:
                            w1.append(tile_pv_guard)
                        B.act(prv, ps[:], AF.Exp, scale=8.0,
                              bias=sml[:, 4 * v + 1:4 * v + 2],
                              accum=sml[:, 4 * v + 2:4 * v + 3], waits=w1)
                        sim_rel[v % 2] = P.mark("scalar")
                        mexp_u[v] = sim_rel[v % 2]
                        B.recip(sml[:, 4 * v + 3:4 * v + 4],
                                sml[:, 4 * v + 2:4 * v + 3], waits=[mexp_u[v]])
                        B.ts("vector", prv, prv, sml[:, 4 * v + 3:4 * v + 4], ALU.mult)
                        m_sc = P.mark("vector")
                        for kc in range(3):
                            B.tr(rot_bf[:, kc * 128:(kc + 1) * 128],
                                 prv[:, kc * 128:(kc + 1) * 128], sb_ident[:],
                                 waits=([m_sc] + ([rot_rel] if rot_rel else []))
                                 if kc == 0 else (),
                                 inc="pe" if kc == 2 else None)
                        prb = prT[v % 2]
                        B.scopy(prb[:], rot_bf[:, 0:384],
                                waits=[("pe", P.cnt["pe"])] +
                                ([prT_rel[v % 2]] if prT_rel[v % 2] else []))
                        rot_rel = P.mark("scalar")
                        m_pt = P.mark("scalar")
                        for kc in range(3):
                            B.mm(ps_pvo[base:base + 64, :],
                                 vwin[:, qsub + kc, 64 * h:64 * h + 64],
                                 prb[:, kc * 128:(kc + 1) * 128],
                                 kc == 0, kc == 2,
                                 waits=([m_pt] + ([pvo_rel] if (par == 0 and pvo_rel) else []))
                                 if kc == 0 else (),
                                 inc="pe" if kc == 2 else None)
                        prT_rel[v % 2] = ("pe", P.cnt["pe"])
                        if par == 1:
                            B.act(aout[:, c, qsub * 128:(qsub + 1) * 128], ps_pvo[:],
                                  AF.Copy, waits=[("pe", P.cnt["pe"])])
                            pvo_rel = P.mark("scalar")
                kv_rel = [("pe", P.cnt["pe"])]
                m_ao = P.mark("scalar")
                for co in range(NCH):
                    bk = co % 2
                    ps = ps_mm[bk]
                    w0 = [m_ao, LW] + ([mm_rel[bk]] if mm_rel[bk] else [])
                    for ci in range(NCH):
                        B.mm(ps[:, 0:TTA], w_o[:, ci, co * 128:(co + 1) * 128],
                             aout[:, ci, :], ci == 0, ci == NCH - 1,
                             waits=w0 if ci == 0 else (),
                             inc="pe" if ci == NCH - 1 else None)
                    B.tt("vector", yout[sl][:, co, :], ps[:, 0:TTA],
                         ysh[sl][:, co, 1:TTA + 1], ALU.add,
                         waits=[("pe", P.cnt["pe"])])
                    mm_rel[bk] = P.mark("vector")
                m_yo = P.mark("vector")
                P.dma("sync", dstv[:, :, t0:t0 + TTA], yout[sl][:], st, waits=[m_yo])
                slot_rel[sl] = [m_yo, (st, P.cnt[st])]
            P.fence()

    FFN_BLOCKS = [(j, 128) for j in range(21)] + [(21, 42)]

    def inner_ffn(l, ysrc, ydst):
        TTF = 256
        NT = T // TTF
        with ExitStack() as ph:
            sbp = lambda nm, shp, dt: ph.enter_context(nc.sbuf_tensor(B.unique(nm), shp, dt))
            psp = lambda nm, shp: ph.enter_context(nc.psum_tensor(B.unique(nm), shp, F32))
            lw = P.sem("ff_lw")
            w1 = sbp("ffw1", [128, NCH, 2 * DI], BF16)
            w2 = sbp("ffw2", [128, 22, D], BF16)
            biasr = sbp("ffbr", [65, 30, 128], BF16)
            P.dma("gpsimd", w1[:],
                  fW1.rearrange("l (c p) n -> p l c n", p=128)[:, l, :, :], lw)
            P.dma("gpsimd", w2[:, 0:21, :],
                  fW2[l:l + 1, 0:2688, :].rearrange("o (j p) n -> p (o j) n", p=128), lw)
            P.dma("gpsimd", w2[0:42, 21, :],
                  fW2[l:l + 1, 2688:2730, :].rearrange("o j n -> (o j) n"), lw)
            P.dma("sync", biasr[0:1, :, :], c_fbr[l:l + 1, 0, :, :], lw)
            P.dma("sync", biasr[64:65, :, :], c_fbr[l:l + 1, 1, :, :], lw)
            LW = (lw, P.cnt[lw])
            ysh = [sbp("ffy", [128, NCH, TTF + 1], F32) for _ in range(2)]
            yshb = sbp("ffyb", [128, NCH, TTF], BF16)
            xnb = [sbp("ffxb", [128, NCH, TTF], BF16) for _ in range(2)]
            hb = sbp("ffh", [128, 22, TTF], BF16)

            R = sbp("ffrw", [1, 3 * 256], F32)
            gfp = sbp("ffg", [128, TTF], F32)
            yout = [sbp("ffyo", [128, NCH, TTF], F32) for _ in range(2)]
            ps_w = [psp("ffpw", [128, 512]) for _ in range(4)]
            ps_s = psp("ffps", [1, TTF])
            ps_q = psp("ffpq", [1, TTF])
            ps_m = psp("ffpm", [128, TTF])
            ps_r = psp("ffpr", [128, TTF])
            ld = P.sem("ff_ld")
            st = P.sem("ff_st")
            srcv, dstv = fm(ysrc), fm(ydst)
            slot_rel = [None, None]
            w_rel = [None, None, None, None]
            s_rel = [None]
            m_rel = [None]
            m_xb = {}

            def stageA(i):
                sl = i % 2
                t0 = i * TTF
                w = list(slot_rel[sl]) if slot_rel[sl] else [LW]
                if i == 0:
                    P.dma("sync", ysh[sl][:, :, 1:TTF + 1], srcv[:, :, 0:TTF], ld, waits=w)
                    B.memset("vector", ysh[sl][:, :, 0:1], 0.0, waits=w)
                else:
                    P.dma("sync", ysh[sl][:], srcv[:, :, t0 - 1:t0 + TTF], ld, waits=w)
                ldw = [(ld, P.cnt[ld])] + ([P.mark("vector")] if i == 0 else [])
                B.copy("vector", yshb[:, 0:4, :], ysh[sl][:, 0:4, 1:TTF + 1],
                       waits=ldw + [("pe", P.cnt["pe"])])
                B.copy("vector", yshb[:, 4:8, :], ysh[sl][:, 4:8, 0:TTF])
                m_yb = P.mark("vector")
                ysq = xnb[sl]
                B.act(ysq[:], yshb[:], AF.Square, waits=[m_yb])
                m_sq = P.mark("scalar")
                for c in range(NCH):
                    B.mm(ps_s[:], sb_ones1[:], yshb[:, c, :], c == 0, c == NCH - 1,
                         waits=([m_yb] + ([s_rel[0]] if s_rel[0] else [])) if c == 0 else (),
                         inc="pe" if c == NCH - 1 else None)
                pe_s = ("pe", P.cnt["pe"])
                for c in range(NCH):
                    B.mm(ps_q[:], sb_ones1[:], ysq[:, c, :], c == 0, c == NCH - 1,
                         waits=[m_sq] if c == 0 else (),
                         inc="pe" if c == NCH - 1 else None)
                pe_q = ("pe", P.cnt["pe"])
                r0 = R[0:1, 0:TTF]
                r1 = R[0:1, 256:256 + TTF]
                r2 = R[0:1, 512:512 + TTF]
                B.scopy(r0, ps_s[:], waits=[pe_s, P.mark("vector")])
                B.scopy(r1, ps_q[:], waits=[pe_q])
                s_rel[0] = P.mark("scalar")
                B.ts("vector", r0, r0, 1.0 / D, ALU.mult, waits=[s_rel[0]])
                B.ts("vector", r1, r1, 1.0 / D, ALU.mult)
                B.tt("vector", r2, r0, r0, ALU.mult)
                B.tt("vector", r1, r1, r2, ALU.subtract)
                m_var = P.mark("vector")
                B.act(r2, r1, AF.Ln, bias=1e-5, waits=[m_var])
                B.act(r2, r2, AF.Exp, scale=-0.5)
                m_rstd = P.mark("scalar")
                B.mm(ps_m[:], sb_onesr[:], r0, True, True,
                     waits=[m_var, m_rstd] + ([m_rel[0]] if m_rel[0] else []), inc="pe")
                pe_m = ("pe", P.cnt["pe"])
                B.mm(ps_r[:], sb_onesr[:], r2, True, True, inc="pe")
                pe_r = ("pe", P.cnt["pe"])
                xnf = yout[sl]
                B.tt("vector", xnf[:, 0:4, :], ysh[sl][:, 0:4, 1:TTF + 1],
                     bc_mid(ps_m[:], 4), ALU.subtract,
                     waits=[pe_m, pe_r] + (list(slot_rel[sl]) if slot_rel[sl] else []))
                B.tt("vector", xnf[:, 4:8, :], ysh[sl][:, 4:8, 0:TTF],
                     bc_mid(ps_m[:], 4), ALU.subtract)
                B.tt("vector", xnf[:], xnf[:], bc_mid(ps_r[:], NCH), ALU.mult)
                m_rel[0] = P.mark("vector")
                m_xn = m_rel[0]
                for c in range(NCH):
                    B.ts("vector", xnb[sl][:, c, :], xnf[:, c, :],
                         sb_flng[:, l, c:c + 1], ALU.mult,
                         s2=sb_flnb[:, l, c:c + 1], op2=ALU.add,
                         waits=[m_xn] if c == 0 else ())
                m_xb[i] = P.mark("vector")

            def stageB(i, emit_mid=None):
                sl = i % 2
                t0 = i * TTF
                for (j, sz) in FFN_BLOCKS:
                    off = 256 * j
                    for half in (1, 0):
                        bk = 2 * (j % 2) + half
                        ps = ps_w[bk]
                        o0 = off + half * sz
                        w0 = [m_xb[i], LW] + ([w_rel[bk]] if w_rel[bk] else [])
                        bsl = biasr[0:1, j, 0:sz] if half == 0 else biasr[64:65, j, 0:sz]
                        osl = sb_onesrb[0:1, 0:TTF] if half == 0 else sb_onesrb[64:65, 0:TTF]
                        B.mm(ps[0:sz, 0:TTF], bsl, osl, True, False, waits=w0)
                        for ci in range(NCH):
                            B.mm(ps[0:sz, 0:TTF], w1[:, ci, o0:o0 + sz],
                                 xnb[sl][:, ci, :], False, ci == NCH - 1,
                                 inc="pe" if ci == NCH - 1 else None)
                        if half == 1:
                            B.act(gfp[0:sz, :], ps[0:sz, 0:TTF], AF.Gelu,
                                  waits=[("pe", P.cnt["pe"]), P.mark("vector")])
                            w_rel[bk] = P.mark("scalar")
                    abk = 2 * (j % 2)
                    B.tt("vector", hb[0:sz, j, :], ps_w[abk][0:sz, 0:TTF], gfp[0:sz, :],
                         ALU.mult, waits=[("pe", P.cnt["pe"]), w_rel[abk + 1]])
                    w_rel[abk] = P.mark("vector")
                if emit_mid is not None:
                    emit_mid()
                m_h = P.mark("vector")
                for co in range(NCH):
                    bk = co % 4
                    ps = ps_w[bk]
                    w0 = [m_h, LW] + ([w_rel[bk]] if w_rel[bk] else [])
                    B.mm(ps[:, 0:TTF], biasr[0:1, 22 + co, :], sb_onesrb[0:1, 0:TTF],
                         True, False, waits=w0)
                    for (j, sz) in FFN_BLOCKS:
                        B.mm(ps[:, 0:TTF], w2[0:sz, j, co * 128:(co + 1) * 128],
                             hb[0:sz, j, :], False, j == 21,
                             inc="pe" if j == 21 else None)
                    B.tt("vector", yout[sl][:, co, :], ps[:, 0:TTF],
                         ysh[sl][:, co, 1:TTF + 1], ALU.add,
                         waits=[("pe", P.cnt["pe"])])
                    w_rel[bk] = P.mark("vector")
                m_yo = P.mark("vector")
                P.dma("sync", dstv[:, :, t0:t0 + TTF], yout[sl][:], st, waits=[m_yo])
                slot_rel[sl] = [m_yo, (st, P.cnt[st])]

            stageA(0)
            for i in range(NT):
                stageB(i, emit_mid=(lambda i=i: stageA(i + 1)) if i + 1 < NT else None)
            P.fence()

    IDX16 = B.dram("IDX16", [H, T], F32)

    def attn_b(x_src, x_dst):
        with ExitStack() as ph:
            sbp = lambda nm, shp, dt=F32: ph.enter_context(nc.sbuf_tensor(B.unique(nm), shp, dt))
            psp = lambda nm, shp: ph.enter_context(nc.psum_tensor(B.unique(nm), shp, F32))
            lw = P.sem("ab_lw")
            wkt = sbp("abwk", [128, NCH, D])
            wsb = sbp("abws", [128, NCH, H])
            sb_U = sbp("abU", [128, NCH, H])
            idxrow = sbp("abir", [H, 512])
            ps_u = psp("abpu", [128, H])
            ps_ix = psp("abpi", [H, 512])
            P.dma("sync", wkt[:], bWk_orig.rearrange("(ce p) d -> p ce d", p=128), lw)
            P.dma("sync", wsb[:], bwsblk.rearrange("(ce p) h -> p ce h", p=128), lw)
            LW = (lw, P.cnt[lw])
            u_rel = [None]
            for dc in range(NCH):
                for ce in range(NCH):
                    B.mm(ps_u[:], wkt[:, ce, dc * 128:(dc + 1) * 128], wsb[:, ce, :],
                         ce == 0, ce == NCH - 1,
                         waits=([LW] + ([u_rel[0]] if u_rel[0] else [])) if ce == 0 else (),
                         inc="pe" if ce == NCH - 1 else None)
                B.scopy(sb_U[:, dc, :], ps_u[:], waits=[("pe", P.cnt["pe"])])
                u_rel[0] = P.mark("scalar")
            ix_rel = [None]

            def idx_hook(i, t0, of_tile, m_ready):
                for c in range(NCH):
                    B.mm(ps_ix[:], sb_U[:, c, :], of_tile[:, c, :], c == 0, c == NCH - 1,
                         waits=([m_ready, u_rel[0]] + ([ix_rel[0]] if ix_rel[0] else []))
                         if c == 0 else (),
                         inc="pe" if c == NCH - 1 else None)
                B.scopy(idxrow[:], ps_ix[:], waits=[("pe", P.cnt["pe"])])
                m = P.mark("scalar")
                P.dma("sync", IDX16[:, t0:t0 + 512], idxrow[:], "ab_ix", waits=[m])
                ix_rel[0] = m

            ln_pass("ln3", x_src, dst_bf=XNB, idx_hook=idx_hook)
        proj_pass("pj3", [(bWqT, NCH), (bWkT, NCH), (bWvT, NCH)], XNB,
                  [("fm", QS), ("tm", KTM), ("tm", VTM)])
        with ExitStack() as pho:
            sbo = lambda nm, shp, dt=F32: pho.enter_context(nc.sbuf_tensor(B.unique(nm), shp, dt))
            ksp = sbo("abks", [128, NCH, 64], BF16)
            vsp = sbo("abvs", [64, H, 64], BF16)
            rank_tm = sbo("abrt", [128, 64, H])
            with ExitStack() as ph:
                sbp = lambda nm, shp, dt=F32: ph.enter_context(nc.sbuf_tensor(B.unique(nm), shp, dt))
                psp = lambda nm, shp: ph.enter_context(nc.psum_tensor(B.unique(nm), shp, F32))
                sbI = sbp("absi", [128, 1024])
                idx16 = sbp("abix", [H, T])
                mask16 = sbp("abmk", [H, T])
                rank16 = sbp("abrk", [H, T])
                st8 = sbp("abst", [H, 8])
                rowb = sbp("abrw", [1, 128])
                rowc = sbp("abrc", [1, 16])
                thb = sbp("abth", [128, 1])
                msk = sbp("abms", [128, 1024])
                cnt_p = sbp("abcp", [128, 1])
                ps_row = psp("abpr", [1, 128])
                ps_col = psp("abpc", [16, 1])
                ps_th = psp("abpt", [128, 1])
                ps_tr = psp("abptr", [128, 16])
                ldi = P.sem("ab_ldi")
                P.dma("sync", sbI[:],
                      bass.AP(IDX16.tensor, IDX16.offset,
                              [[T, 16], [1024, 8], [1, 1024]]), ldi)
                P.dma("sync", idx16[:], IDX16[:, :], ldi)
                ldw = [(ldi, P.cnt[ldi])]

                def row_to_col(src_row16, dst_col, extra=()):
                    B.mm(ps_col[:], src_row16, sb_onescf[0:1, 0:1], True, True,
                         waits=list(extra) + [P.mark("scalar"), P.mark("vector")], inc="pe")
                    B.scopy(dst_col, ps_col[:], waits=[("pe", P.cnt["pe"])])
                    return P.mark("scalar")

                def part_to_head(col128, dst_row16, op, extra=()):
                    B.mm(ps_row[:], col128, sb_identf[:], True, True,
                         waits=list(extra) + [P.mark("scalar")], inc="pe")
                    B.scopy(rowb[:], ps_row[:], waits=[("pe", P.cnt["pe"])])
                    m = P.mark("scalar")
                    B.red(dst_row16, rowb[0:1, :].rearrange("a (h b) -> a h b", b=8),
                          op, waits=[m])
                    return P.mark("vector")

                B.red(cnt_p[:], sbI[:], ALU.max, waits=ldw)
                m1 = part_to_head(cnt_p[:], rowc[:], ALU.max, extra=[P.mark("vector")])
                B.ts("vector", rowc[:], rowc[:], 1.0, ALU.add)
                row_to_col(rowc[:], st8[:, 1:2], extra=[P.mark("vector")])
                B.red(cnt_p[:], sbI[:], ALU.min, waits=ldw + [P.mark("scalar")])
                m2 = part_to_head(cnt_p[:], rowc[:], ALU.min, extra=[P.mark("vector")])
                B.ts("vector", rowc[:], rowc[:], -1.0, ALU.add)
                row_to_col(rowc[:], st8[:, 0:1], extra=[P.mark("vector")])
                for it in range(45):
                    B.tt("vector", st8[:, 2:3], st8[:, 0:1], st8[:, 1:2], ALU.add,
                         waits=[P.mark("scalar")])
                    B.ts("vector", st8[:, 2:3], st8[:, 2:3], 0.5, ALU.mult)
                    m_mid = P.mark("vector")
                    B.mm(ps_th[:], sb_gselT[:], st8[:, 2:3], True, True,
                         waits=[m_mid, P.mark("scalar")], inc="pe")
                    B.scopy(thb[:], ps_th[:], waits=[("pe", P.cnt["pe"])])
                    m_th = P.mark("scalar")
                    B.ts("vector", msk[:], sbI[:], thb[:], ALU.is_ge, waits=[m_th])
                    B.red(cnt_p[:], msk[:], ALU.add)
                    part_to_head(cnt_p[:], rowc[:], ALU.add, extra=[P.mark("vector")])
                    row_to_col(rowc[:], st8[:, 3:4], extra=[P.mark("vector")])
                    B.ts("vector", st8[:, 4:5], st8[:, 3:4], 64.0, ALU.is_ge,
                         waits=[P.mark("scalar")])
                    B.tt("vector", st8[:, 5:6], st8[:, 2:3], st8[:, 0:1], ALU.subtract)
                    B.tt("vector", st8[:, 5:6], st8[:, 5:6], st8[:, 4:5], ALU.mult)
                    B.tt("vector", st8[:, 0:1], st8[:, 0:1], st8[:, 5:6], ALU.add)
                    B.tt("vector", st8[:, 6:7], st8[:, 1:2], st8[:, 2:3], ALU.subtract)
                    B.tt("vector", st8[:, 6:7], st8[:, 6:7], st8[:, 4:5], ALU.mult)
                    B.tt("vector", st8[:, 1:2], st8[:, 2:3], st8[:, 6:7], ALU.add)
                B.ts("vector", mask16[:], idx16[:], st8[:, 0:1], ALU.is_ge,
                     waits=ldw + [P.mark("vector")])
                ones_b = bass.AP(sb_onescf[:].tensor, sb_onescf[:].offset,
                                 [list(sb_onescf[:].ap[0])[:1] + [16], [0, T]])
                B.scan(rank16[:], mask16[:], ones_b, 0.0, ALU.add, ALU.mult)
                B.tt("vector", rank16[:], rank16[:], mask16[:], ALU.mult)
                m_rk = P.mark("vector")
                tr_rel = None
                for tc in range(64):
                    B.tr(ps_tr[:], rank16[0:16, tc * 128:(tc + 1) * 128],
                         sb_identf[0:16, 0:16],
                         waits=[m_rk] + ([tr_rel] if tr_rel else []), inc="pe")
                    B.scopy(rank_tm[:, tc, :], ps_tr[:], waits=[("pe", P.cnt["pe"])])
                    tr_rel = P.mark("scalar")
                P.fence()
            with ExitStack() as ph:
                sbp = lambda nm, shp, dt=F32: ph.enter_context(nc.sbuf_tensor(B.unique(nm), shp, dt))
                kvt = [sbp("abkv", [128, 2, D], BF16) for _ in range(2)]
                Sg = sbp("absg", [128, 4, 64], BF16)
                ps_c = [ph.enter_context(nc.psum_tensor(B.unique("abpc2"), [64, 512], F32))
                        for _ in range(8)]
                ldk = P.sem("ab_ldk")
                ktmv, vtmv = tm(KTM), tm(VTM)
                slot_rel2 = [None, None]
                for pass_ in range(4):
                    heads = [4 * pass_ + z for z in range(4)]
                    for tc in range(64):
                        sl = tc % 2
                        w = list(slot_rel2[sl]) if slot_rel2[sl] else []
                        P.dma("sync", kvt[sl][:, 0, :], ktmv[:, tc, :], ldk, waits=w)
                        P.dma("sync", kvt[sl][:, 1, :], vtmv[:, tc, :], ldk, waits=w)
                        ldw2 = [(ldk, P.cnt[ldk])]
                        for z, h in enumerate(heads):
                            B.ts("vector", Sg[:, z, :], sb_iota64[:],
                                 rank_tm[:, tc, h:h + 1], ALU.is_equal,
                                 waits=(ldw2 + [P.mark("tensor") if False else
                                        ("pe", P.cnt["pe"])]) if z == 0 else ())
                        m_sg = P.mark("vector")
                        for z, h in enumerate(heads):
                            B.mm(ps_c[2 * z][:, 0:64], kvt[sl][:, 0, 64 * h:64 * h + 64],
                                 Sg[:, z, :], tc == 0, tc == 63,
                                 waits=[m_sg] + ldw2,
                                 inc="pe" if tc == 63 else None)
                            B.mm(ps_c[2 * z + 1][:, 0:64], Sg[:, z, :],
                                 kvt[sl][:, 1, 64 * h:64 * h + 64], tc == 0, tc == 63,
                                 inc="pe" if tc == 63 else None)
                        slot_rel2[sl] = [("pe", P.cnt["pe"]), m_sg]
                    pe_f = ("pe", P.cnt["pe"])
                    for z, h in enumerate(heads):
                        B.scopy(ksp[64 * (h % 2):64 * (h % 2) + 64, h // 2, :],
                                ps_c[2 * z][:, 0:64], waits=[pe_f])
                        B.scopy(vsp[0:64, h, :], ps_c[2 * z + 1][:, 0:64], waits=[pe_f])
                    slot_rel2 = [[P.mark("scalar")], [P.mark("scalar")]]
                P.fence()
            with ExitStack() as ph:
                sbp = lambda nm, shp, dt=F32: ph.enter_context(nc.sbuf_tensor(B.unique(nm), shp, dt))
                psp = lambda nm, shp: ph.enter_context(nc.psum_tensor(B.unique(nm), shp, F32))
                lw2 = P.sem("ab_lw2")
                w_oT = sbp("abwo", [128, NCH, D], BF16)
                P.dma("gpsimd", w_oT[:], bWoT.rearrange("(c p) o -> p c o", p=128), lw2)
                LW2 = (lw2, P.cnt[lw2])
                qt = [sbp("abqt", [128, NCH, 512], BF16) for _ in range(2)]
                xr = [sbp("abxr", [128, NCH, 512], F32) for _ in range(2)]
                aout = sbp("abao", [128, NCH, 512], BF16)
                yout = [sbp("abyo", [128, NCH, 512], F32) for _ in range(2)]
                probs_all = sbp("abpb", [128, 64, 64], BF16)
                prT = [sbp("abpt2", [64, 128], BF16) for _ in range(2)]
                sml = sbp("absm", [128, 256])
                ps_s = [psp("abps", [128, 64]) for _ in range(2)]
                ps_t = [ph.enter_context(nc.psum_tensor(B.unique("abptx"), [64, 128], BF16))
                        for _ in range(2)]
                ps_o = psp("abpo", [128, 128])
                ps_mm = [psp("abpm", [128, 512]) for _ in range(2)]
                ldq = P.sem("ab_ldq")
                stx = P.sem("ab_stx")
                qsv = fm(QS)
                xrv, xdv = fm(x_src), fm(x_dst)
                slot_rel3 = [None, None]
                sim_rel = [None, None]
                prT_rel = [None, None]
                tr_rel = [None, None]
                pvo_rel = None
                mm_rel = [None, None]
                tile_pv_guard = None
                for i in range(16):
                    sl = i % 2
                    t0 = i * 512
                    w = list(slot_rel3[sl]) if slot_rel3[sl] else [LW2]
                    P.dma("sync", qt[sl][:], qsv[:, :, t0:t0 + 512], ldq, waits=w)
                    P.dma("sync", xr[sl][:], xrv[:, :, t0:t0 + 512], ldq, waits=w)
                    ldw3 = [(ldq, P.cnt[ldq])]
                    units = [(qsub, c, par) for qsub in range(4)
                             for c in range(NCH) for par in range(2)]
                    NU = len(units)
                    pes_u = [None] * NU
                    mb_u = [None] * NU
                    for step in range(NU + 1):
                        if step < NU:
                            u = step
                            qsub, c, par = units[u]
                            base = 64 * par
                            ps = ps_s[u % 2]
                            w0 = ldw3 + ([sim_rel[u % 2]] if sim_rel[u % 2] else [])
                            B.mm(ps[:], qt[sl][base:base + 64, c, qsub * 128:(qsub + 1) * 128],
                                 ksp[base:base + 64, c, :], True, True, waits=w0, inc="pe")
                            pes_u[u] = ("pe", P.cnt["pe"])
                            B.red(sml[:, 4 * u:4 * u + 1], ps[:], ALU.max, negate=True,
                                  waits=[pes_u[u]] + ([P.mark("scalar")] if u == 0 else []))
                            B.ts("vector", sml[:, 4 * u + 1:4 * u + 2],
                                 sml[:, 4 * u:4 * u + 1], 0.125, ALU.mult)
                            mb_u[u] = P.mark("vector")
                        if step >= 1:
                            v = step - 1
                            qsub, c, par = units[v]
                            base = 64 * par
                            h = 2 * c + par
                            ps = ps_s[v % 2]
                            prv = probs_all[:, v, :]
                            w1 = [mb_u[v], pes_u[v]]
                            if tile_pv_guard is not None:
                                w1.append(tile_pv_guard)
                            B.act(prv, ps[:], AF.Exp, scale=0.125,
                                  bias=sml[:, 4 * v + 1:4 * v + 2],
                                  accum=sml[:, 4 * v + 2:4 * v + 3], waits=w1)
                            sim_rel[v % 2] = P.mark("scalar")
                            B.recip(sml[:, 4 * v + 3:4 * v + 4],
                                    sml[:, 4 * v + 2:4 * v + 3], waits=[sim_rel[v % 2]])
                            B.ts("vector", prv, prv, sml[:, 4 * v + 3:4 * v + 4], ALU.mult)
                            m_sc = P.mark("vector")
                            B.tr(ps_t[v % 2][:], prv, sb_ident[:],
                                 waits=[m_sc] + ([tr_rel[v % 2]] if tr_rel[v % 2] else []),
                                 inc="pe")
                            prb = prT[v % 2]
                            B.scopy(prb[:], ps_t[v % 2][:],
                                    waits=[("pe", P.cnt["pe"])] +
                                    ([prT_rel[v % 2]] if prT_rel[v % 2] else []))
                            tr_rel[v % 2] = P.mark("scalar")
                            m_pt = P.mark("scalar")
                            B.mm(ps_o[base:base + 64, :], vsp[0:64, h, :], prb[:],
                                 True, True,
                                 waits=[m_pt] + ([pvo_rel] if (par == 0 and pvo_rel) else []),
                                 inc="pe")
                            prT_rel[v % 2] = ("pe", P.cnt["pe"])
                            if par == 1:
                                B.act(aout[:, c, qsub * 128:(qsub + 1) * 128], ps_o[:],
                                      AF.Copy, waits=[("pe", P.cnt["pe"])])
                                pvo_rel = P.mark("scalar")
                    tile_pv_guard = ("pe", P.cnt["pe"])
                    m_ao = P.mark("scalar")
                    for co in range(NCH):
                        bk = co % 2
                        ps = ps_mm[bk]
                        w0 = [m_ao, LW2] + ([mm_rel[bk]] if mm_rel[bk] else [])
                        for ci in range(NCH):
                            B.mm(ps[:], w_oT[:, ci, co * 128:(co + 1) * 128],
                                 aout[:, ci, :], ci == 0, ci == NCH - 1,
                                 waits=w0 if ci == 0 else (),
                                 inc="pe" if ci == NCH - 1 else None)
                        B.tt("vector", yout[sl][:, co, :], ps[:],
                             xr[sl][:, co, :], ALU.add, waits=[("pe", P.cnt["pe"])])
                        mm_rel[bk] = P.mark("vector")
                    m_yo = P.mark("vector")
                    P.dma("sync", xdv[:, :, t0:t0 + 512], yout[sl][:], stx, waits=[m_yo])
                    slot_rel3[sl] = [m_yo, (stx, P.cnt[stx])]
                P.fence()

    ln_pass("ln1", xin, dst_bf=XNB)
    if "xnb" in debug_taps:
        P.dma("sync", dbg["dbg_xnb"][:, :], XNB[:, :], "dbgst")
        P.fence()
    if stop_after == "ln1":
        P.dma("sync", out[:, :], xin[:, :], "stout")
        P.fence()
        return nc, finish(B)

    proj_pass("pj1", [(aQw, NCH), (aKw, NCH), (aVw, NCH)], XNB,
              [("fm", QS), ("fm", KS), ("tm", VTM)])
    if "qkv" in debug_taps:
        P.dma("sync", dbg["dbg_q"][:, :], QS[:, :], "dbgst")
        P.dma("sync", dbg["dbg_k"][:, :], KS[:, :], "dbgst")
        P.dma("sync", dbg["dbg_v"][:, :], VTM[:, :], "dbgst")
        P.fence()
    if stop_after == "proj1":
        P.dma("sync", out[:, :], xin[:, :], "stout")
        P.fence()
        return nc, finish(B)

    attn_a_flash(xin, S_a)
    cur = S_a
    if stop_after == "sub1":
        P.dma("sync", out[:, :], cur[:, :], "stout")
        P.fence()
        return nc, finish(B)

    def inner_block(tag, cur):
        ln_pass("ln_" + tag, cur, dst_f32=S0)
        y, alt = S0, S1
        nsub = DEPTH * 2
        if stop_after and stop_after.startswith("inner"):
            nsub = min(nsub, int(stop_after[5:]))
        for l in range(DEPTH):
            if 2 * l >= nsub:
                break
            inner_attn(l, y, alt)
            y, alt = alt, y
            if 2 * l + 1 >= nsub:
                break
            inner_ffn(l, y, alt)
            y, alt = alt, y
        return y

    y = inner_block("i2", cur)
    if stop_after and stop_after.startswith("inner"):
        P.dma("sync", out[:, :], y[:, :], "stout")
        P.fence()
        return nc, finish(B)
    ln_pass("add2", cur, src2=y, dst_sum=S_b, no_norm=True)
    cur = S_b
    if stop_after == "sub2":
        P.dma("sync", out[:, :], cur[:, :], "stout")
        P.fence()
        return nc, finish(B)

    attn_b(cur, S_a)
    cur = S_a
    if stop_after == "sub3":
        P.dma("sync", out[:, :], cur[:, :], "stout")
        P.fence()
        return nc, finish(B)

    y = inner_block("i4", cur)
    ln_pass("add4", cur, src2=y, dst_sum=out, no_norm=True)
    P.fence()
    return nc, finish(B)


def finish(B):
    B.P.build(B.nc)
    B.es.close()
    B.nc.finalize()
    return B



_BUILD_CACHE = {}


def _host_constants():
    import ml_dtypes
    bf = ml_dtypes.bfloat16
    C = {}
    half = np.arange(0, HD, 2, dtype=np.float64)
    inv_freq = 1.0 / 10000.0 ** (half / HD)
    t = np.arange(T, dtype=np.float64)
    freqs = t[:, None] * inv_freq[None, :]
    freqs = np.concatenate([freqs, freqs], -1)
    base_scale = (half + 0.4 * HD) / (1.4 * HD)
    power = (t - T // 2) / (256 // 2)
    scale = base_scale[None, :] ** power[:, None]
    scale = np.concatenate([scale, scale], -1)
    C["tbl_cs"] = np.ascontiguousarray((np.cos(freqs) * scale).T.astype(np.float32))
    C["tbl_ss"] = np.ascontiguousarray((np.sin(freqs) * scale).T.astype(np.float32))
    C["tbl_ci"] = np.ascontiguousarray((np.cos(freqs) / scale).T.astype(np.float32))
    C["tbl_si"] = np.ascontiguousarray((np.sin(freqs) / scale).T.astype(np.float32))
    C["c_ident"] = np.eye(128, dtype=np.float32).astype(bf)
    C["c_identf"] = np.eye(128, dtype=np.float32)
    ii = np.arange(128)[:, None]
    cc = np.arange(1024)[None, :]
    C["c_stair"] = np.where(cc > ii + 384, np.float32(NEG), np.float32(0)).astype(bf)
    il = np.arange(128)[:, None]
    jl = np.arange(384)[None, :]
    band = (jl < il) | (jl > il + 256)
    b0 = band | (jl < 256)
    b1 = band | (jl < 128)
    C["c_band"] = np.stack([
        np.where(band, np.float32(NEG), np.float32(0)),
        np.where(b0, np.float32(NEG), np.float32(0)),
        np.where(b1, np.float32(NEG), np.float32(0))]).astype(bf)
    prot = np.zeros((128, 128), np.float32)
    for b in range(2):
        o = 64 * b
        for m in range(32):
            prot[o + m + 32, o + m] = -1.0
        for m in range(32, 64):
            prot[o + m - 32, o + m] = 1.0
    C["c_prot"] = prot.astype(bf)
    C["c_ones1"] = np.ones((128, 1), np.float32).astype(bf)
    C["c_onesr"] = np.ones((1, 128), np.float32)
    C["c_onescf"] = np.ones((128, 1), np.float32)
    C["c_iota64"] = np.tile(np.arange(1, HD + 1, dtype=np.float32)[None, :], (128, 1))
    hsel = np.zeros((NCH, 128, H), np.float32)
    for c in range(NCH):
        for p in range(128):
            hsel[c, p, 2 * c + p // 64] = 1.0
    C["c_hsel"] = hsel.astype(bf)
    hsel2 = np.zeros((128, 2), np.float32)
    hsel2[np.arange(128), np.arange(128) // 64] = 1.0
    C["c_hsel2"] = hsel2.astype(bf)
    C["c_pairsel"] = np.ascontiguousarray(hsel2.T.astype(np.float32))
    C["c_hselT"] = np.ascontiguousarray(hsel.transpose(2, 0, 1)).astype(bf)
    gsel = np.zeros((128, H), np.float32)
    for p in range(128):
        gsel[p, p // 8] = 1.0
    C["c_gsel"] = gsel.astype(bf)
    C["c_gselT"] = np.ascontiguousarray(gsel.T.astype(np.float32))
    fv = np.zeros(8, np.float32)
    fv[:len(FVALS)] = FVALS
    C["c_fvals"] = np.tile(fv[None, :], (128, 1))
    C["c_onesrb"] = np.ones((1, 512), np.float32).astype(bf)
    return C


def _host_weights(inputs):
    W = {}
    f32 = lambda a: np.ascontiguousarray(np.asarray(a, dtype=np.float32))
    W["aQw"] = f32(inputs["L0_attn_a_Qw"])
    W["aKw"] = f32(inputs["L0_attn_a_Kw"])
    W["aVw"] = f32(inputs["L0_attn_a_Vw"])
    W["bWqT"] = f32(np.asarray(inputs["L0_attn_b_Wq"]).T)
    W["bWkT"] = f32(np.asarray(inputs["L0_attn_b_Wk"]).T)
    W["bWvT"] = f32(np.asarray(inputs["L0_attn_b_Wv"]).T)
    W["bWoT"] = f32(np.asarray(inputs["L0_attn_b_Wo"]).T)
    W["bWk_orig"] = f32(inputs["L0_attn_b_Wk"])
    ws = np.asarray(inputs["L0_attn_b_w_score"], dtype=np.float32)
    wsblk = np.zeros((D, H), np.float32)
    for h in range(H):
        wsblk[64 * h:64 * h + 64, h] = ws
    W["bwsblk"] = wsblk
    W["fWqkv"] = f32(inputs["L0_ffn_Wqkv"])
    W["fWo"] = f32(inputs["L0_ffn_Wo"])
    W1 = f32(inputs["L0_ffn_W1"])
    b1 = f32(inputs["L0_ffn_b1"])
    blocks, sizes = [], []
    for j in range(21):
        blocks.append(np.arange(128 * j, 128 * (j + 1)))
        sizes.append(128)
    blocks.append(np.arange(2688, 2730))
    sizes.append(42)
    perm = []
    for blk in blocks:
        perm.extend(blk.tolist())
        perm.extend((blk + DI).tolist())
    perm = np.asarray(perm)
    W["fW1"] = np.ascontiguousarray(W1[:, :, perm])
    W["fW2"] = f32(inputs["L0_ffn_W2"])
    b1p = b1[:, perm]
    b1cm = np.zeros((DEPTH, 128, 44), np.float32)
    off = 0
    for j, sz in enumerate(sizes):
        b1cm[:, 0:sz, 2 * j] = b1p[:, off:off + sz]
        b1cm[:, 0:sz, 2 * j + 1] = b1p[:, off + sz:off + 2 * sz]
        off += 2 * sz
    W["fb1"] = b1cm
    import ml_dtypes as _md
    _bf = _md.bfloat16
    b1r = np.zeros((DEPTH, 44, 128), np.float32)
    off = 0
    for j, sz in enumerate(sizes):
        b1r[:, 2 * j, 0:sz] = b1p[:, off:off + sz]
        b1r[:, 2 * j + 1, 0:sz] = b1p[:, off + sz:off + 2 * sz]
        off += 2 * sz
    fbr = np.zeros((DEPTH, 2, 30, 128), np.float32)
    fbr[:, 0, 0:22, :] = b1r[:, 0::2, :]
    fbr[:, 1, 0:22, :] = b1r[:, 1::2, :]
    fbr[:, 0, 22:30, :] = np.asarray(inputs["L0_ffn_b2"], np.float32).reshape(DEPTH, NCH, 128)
    W["c_fbr"] = fbr.astype(_bf)
    cm = lambda v: np.ascontiguousarray(
        np.asarray(v, np.float32).reshape(DEPTH, NCH, 128).transpose(0, 2, 1))
    W["fb2"] = cm(inputs["L0_ffn_b2"])
    W["flng"] = cm(inputs["L0_ffn_ln_g"])
    W["flnb"] = cm(inputs["L0_ffn_ln_b"])
    qs = np.asarray(inputs["L0_ffn_q_scale"], np.float32)
    ks = np.asarray(inputs["L0_ffn_k_scale"], np.float32)

    def qcols(v):
        a = np.tile(v, (1, 2))
        b = np.tile(np.concatenate([v[:, 32:], v[:, :32]], 1), (1, 2))
        return np.ascontiguousarray(np.stack([a, b], axis=-1))
    W["fqs"] = qcols(qs)
    W["fks"] = qcols(ks)
    return W


DEV_NCORES = None


def kernel(stop_after="all", debug_taps=(), **inputs):
    key = (stop_after, tuple(sorted(debug_taps)))
    if key not in _BUILD_CACHE:
        _BUILD_CACHE[key] = build_program(stop_after=stop_after, debug_taps=debug_taps)
    nc, _ = _BUILD_CACHE[key]
    C = _host_constants()
    W = _host_weights(inputs)
    x = np.asarray(inputs["L0_attn_a_x"], dtype=np.float32)
    ncores = DEV_NCORES or NCORES
    in_maps = []
    for b in range(ncores):
        m = dict(C)
        m.update(W)
        m["xin"] = np.ascontiguousarray(x[b].T)
        in_maps.append(m)
    res = run_bass_kernel_spmd(nc, in_maps, core_ids=list(range(ncores)))
    outs = [np.asarray(r["xout"]).T for r in res.results]
    full = np.stack(outs, axis=0).astype(np.float32)
    if debug_taps:
        return full, res
    return full


# version 05
# speedup vs baseline: 1.3198x; 1.2657x over previous
import numpy as np
from contextlib import ExitStack

import concourse.bass as bass
import concourse.mybir as mybir
from concourse.bass_utils import run_bass_kernel_spmd

F32 = mybir.dt.float32
BF16 = mybir.dt.bfloat16
AF = mybir.ActivationFunctionType
ALU = mybir.AluOpType
AX = mybir.AxisListType

T = 8192
D = 1024
NCH = D // 128
NEG = -60000.0
H = 16
HD = 64
DEPTH = 6
DI = 2730
NCORES = 8
FVALS = (1e-5, 1.0 / 32.0, 8.0, 0.125)

ENGINES = ("sync", "gpsimd", "vector", "scalar", "tensor")


def bc_mid(ap2, n):
    assert len(ap2.ap) == 2
    return bass.AP(ap2.tensor, ap2.offset, [list(ap2.ap[0]), [0, n], list(ap2.ap[1])])


class Plan:
    def __init__(self):
        self.ops = {e: [] for e in ENGINES}
        self.cnt = {}
        self.semnames = []

    def sem(self, name):
        if name not in self.cnt:
            self.cnt[name] = 0
            self.semnames.append(name)
        return name

    def emit(self, eng, fn, waits=(), inc=None, incn=1):
        if inc is not None:
            self.sem(inc)
            self.cnt[inc] += incn
        self.ops[eng].append((tuple(waits), fn, inc, incn))
        return (inc, self.cnt[inc]) if inc is not None else None

    def chain(self, eng, fn, waits=()):
        s = self.sem(eng + "_self")
        w = list(waits)
        if self.cnt[s] > 0:
            w.append((s, self.cnt[s]))
        return self.emit(eng, fn, w, inc=s)

    def mark(self, eng):
        s = self.sem(eng + "_self")
        return (s, self.cnt[s])

    def dma(self, eng, dst, src, sem, waits=()):
        self.sem(sem)
        fn = lambda e, dst=dst, src=src: e.dma_start(out=dst, in_=src)
        return self.emit(eng, fn, waits, inc=sem, incn=16)

    def wait_only(self, eng, waits):
        fn = lambda e: e.nop()
        self.ops[eng].append((tuple(waits), fn, None, 0))

    def fence(self):
        waits = [(s, self.cnt[s]) for s in self.semnames if self.cnt[s] > 0]
        for e in ENGINES:
            self.wait_only(e, waits)

    def build(self, nc):
        es = ExitStack()
        SEM = {n: es.enter_context(nc.semaphore(n)) for n in self.semnames}
        blk = es.enter_context(nc.Block())
        for eng_name in ENGINES:
            def body(e, eng_name=eng_name):
                wm = {}
                for waits, fn, inc, incn in self.ops[eng_name]:
                    for (s, v) in waits:
                        if wm.get(s, 0) >= v:
                            continue
                        e.wait_ge(SEM[s], v)
                        wm[s] = v
                    ins = fn(e)
                    if inc is not None:
                        ins.then_inc(SEM[inc], incn)
            getattr(blk, eng_name)(body)
        es.close()


class Builder:
    def __init__(self, nc):
        self.nc = nc
        self.P = Plan()
        self.es = ExitStack()
        self.uid = 0

    def dram(self, name, shape, dt, kind="Internal"):
        return self.nc.dram_tensor(name, shape, dt, kind=kind).ap()

    def sb(self, name, shape, dt=F32):
        return self.es.enter_context(self.nc.sbuf_tensor(name, shape, dt))

    def unique(self, prefix):
        self.uid += 1
        return f"{prefix}{self.uid}"

    def copy(self, eng, dst, src, waits=()):
        return self.P.chain(eng, lambda e, d=dst, s=src: e.tensor_copy(d, s), waits)

    def act(self, dst, src, func, bias=0.0, scale=1.0, accum=None, waits=()):
        fn = lambda e, d=dst, s=src, f=func, b=bias, sc=scale, a=accum: e.activation(
            d, s, f, bias=b, scale=sc, accum_out=a)
        return self.P.chain("scalar", fn, waits)

    def scopy(self, dst, src, waits=()):
        return self.P.chain("scalar", lambda e, d=dst, s=src: e.copy(d, s), waits)

    def tt(self, eng, dst, a, b, op, waits=()):
        return self.P.chain(eng, lambda e, d=dst, x=a, y=b, o=op: e.tensor_tensor(d, x, y, o), waits)

    def ts(self, eng, dst, a, s1, op, s2=None, op2=None, accum=None, waits=()):
        def fn(e, d=dst, x=a, s1=s1, s2=s2, op=op, op2=op2, acc=accum):
            if s2 is None and acc is None:
                return e.tensor_single_scalar(d, x, s1, op)
            return e.tensor_scalar(d, x, s1, s2, op, op2 if op2 is not None else ALU.bypass,
                                   accum_out=acc)
        return self.P.chain(eng, fn, waits)

    def stt(self, eng, dst, in0, scalar, in1, op0, op1, waits=()):
        fn = lambda e, d=dst, a=in0, s=scalar, b=in1, o0=op0, o1=op1: e.scalar_tensor_tensor(
            d, a, s, b, o0, o1)
        return self.P.chain(eng, fn, waits)

    def red(self, dst, src, op, negate=False, waits=()):
        fn = lambda e, d=dst, s=src, o=op, n=negate: e.tensor_reduce(d, s, AX.X, o, negate=n)
        return self.P.chain("vector", fn, waits)

    def recip(self, dst, src, waits=()):
        return self.P.chain("vector", lambda e, d=dst, s=src: e.reciprocal(d, s), waits)

    def memset(self, eng, dst, val, waits=()):
        return self.P.chain(eng, lambda e, d=dst, v=val: e.memset(d, v), waits)

    def scan(self, dst, d0, d1, init, op0, op1, waits=()):
        fn = lambda e, d=dst, a=d0, b=d1, i=init, o0=op0, o1=op1: e.tensor_tensor_scan(
            d, a, b, i, o0, o1)
        return self.P.chain("vector", fn, waits)

    def mm(self, out, lhsT, rhs, start, stop, waits=(), inc=None):
        fn = lambda e, o=out, l=lhsT, r=rhs, st=start, sp=stop: e.matmul(
            o, l, r, start=st, stop=sp)
        return self.P.emit("tensor", fn, waits, inc=inc)

    def tr(self, out, src, ident, waits=(), inc=None):
        fn = lambda e, o=out, s=src, i=ident: e.transpose(o, s, i)
        return self.P.emit("tensor", fn, waits, inc=inc)



def build_program(stop_after="all", debug_taps=()):
    nc = bass.Bass("TRN2", target_bir_lowering=False, debug=False)
    B = Builder(nc)
    P = B.P
    P.sem("pe")

    xin = B.dram("xin", [D, T], F32, kind="ExternalInput")
    aQw = B.dram("aQw", [D, D], F32, kind="ExternalInput")
    aKw = B.dram("aKw", [D, D], F32, kind="ExternalInput")
    aVw = B.dram("aVw", [D, D], F32, kind="ExternalInput")
    bWqT = B.dram("bWqT", [D, D], F32, kind="ExternalInput")
    bWkT = B.dram("bWkT", [D, D], F32, kind="ExternalInput")
    bWvT = B.dram("bWvT", [D, D], F32, kind="ExternalInput")
    bWoT = B.dram("bWoT", [D, D], F32, kind="ExternalInput")
    bWk_orig = B.dram("bWk_orig", [D, D], F32, kind="ExternalInput")
    bwsblk = B.dram("bwsblk", [D, H], F32, kind="ExternalInput")
    fWqkv = B.dram("fWqkv", [DEPTH, D, 3 * D], F32, kind="ExternalInput")
    fWo = B.dram("fWo", [DEPTH, D, D], F32, kind="ExternalInput")
    fW1 = B.dram("fW1", [DEPTH, D, 2 * DI], F32, kind="ExternalInput")
    fW2 = B.dram("fW2", [DEPTH, DI, D], F32, kind="ExternalInput")
    fb1 = B.dram("fb1", [DEPTH, 128, 44], F32, kind="ExternalInput")
    fb2 = B.dram("fb2", [DEPTH, 128, NCH], F32, kind="ExternalInput")
    flng = B.dram("flng", [DEPTH, 128, NCH], F32, kind="ExternalInput")
    flnb = B.dram("flnb", [DEPTH, 128, NCH], F32, kind="ExternalInput")
    fqs = B.dram("fqs", [DEPTH, 128, 2], F32, kind="ExternalInput")
    fks = B.dram("fks", [DEPTH, 128, 2], F32, kind="ExternalInput")
    tbl_cs = B.dram("tbl_cs", [HD, T], F32, kind="ExternalInput")
    tbl_ss = B.dram("tbl_ss", [HD, T], F32, kind="ExternalInput")
    tbl_ci = B.dram("tbl_ci", [HD, T], F32, kind="ExternalInput")
    tbl_si = B.dram("tbl_si", [HD, T], F32, kind="ExternalInput")
    c_ident = B.dram("c_ident", [128, 128], BF16, kind="ExternalInput")
    c_identf = B.dram("c_identf", [128, 128], F32, kind="ExternalInput")
    c_stair = B.dram("c_stair", [128, 1024], BF16, kind="ExternalInput")
    c_band = B.dram("c_band", [3, 128, 384], BF16, kind="ExternalInput")
    c_prot = B.dram("c_prot", [128, 128], BF16, kind="ExternalInput")
    c_ones1 = B.dram("c_ones1", [128, 1], BF16, kind="ExternalInput")
    c_onesr = B.dram("c_onesr", [1, 128], F32, kind="ExternalInput")
    c_onescf = B.dram("c_onescf", [128, 1], F32, kind="ExternalInput")
    c_iota64 = B.dram("c_iota64", [128, HD], F32, kind="ExternalInput")
    c_hsel = B.dram("c_hsel", [NCH, 128, H], BF16, kind="ExternalInput")
    c_hsel2 = B.dram("c_hsel2", [128, 2], BF16, kind="ExternalInput")
    c_pairsel = B.dram("c_pairsel", [2, 128], F32, kind="ExternalInput")
    c_hselT = B.dram("c_hselT", [H, NCH, 128], BF16, kind="ExternalInput")
    c_gsel = B.dram("c_gsel", [128, H], BF16, kind="ExternalInput")
    c_gselT = B.dram("c_gselT", [H, 128], F32, kind="ExternalInput")
    c_fvals = B.dram("c_fvals", [128, 8], F32, kind="ExternalInput")
    c_onesrb = B.dram("c_onesrb", [1, 512], BF16, kind="ExternalInput")
    c_fbr = B.dram("c_fbr", [DEPTH, 2, 30, 128], BF16, kind="ExternalInput")

    out = B.dram("xout", [D, T], F32, kind="ExternalOutput")

    S_a = B.dram("S_a", [D, T], F32)
    S_b = B.dram("S_b", [D, T], F32)
    S0 = B.dram("S0", [D, T], F32)
    S1 = B.dram("S1", [D, T], F32)
    XNB = B.dram("XNB", [D, T], BF16)
    QS = B.dram("QS", [D, T], BF16)
    KS = B.dram("KS", [D, T], BF16)
    KTM = B.dram("KTM", [T, D], BF16)
    VTM = B.dram("VTM", [T, D], BF16)
    RANKB = B.dram("RANKB", [T, H], F32)
    IDXB = B.dram("IDXB", [128, 8 * 1024], F32)

    dbg = {}
    for tap in debug_taps:
        if tap == "xnb":
            dbg["dbg_xnb"] = B.dram("dbg_xnb", [D, T], BF16, kind="ExternalOutput")
        if tap == "qkv":
            dbg["dbg_q"] = B.dram("dbg_q", [D, T], BF16, kind="ExternalOutput")
            dbg["dbg_k"] = B.dram("dbg_k", [D, T], BF16, kind="ExternalOutput")
            dbg["dbg_v"] = B.dram("dbg_v", [T, D], BF16, kind="ExternalOutput")

    def fm(apx):
        return apx.rearrange("(c p) t -> p c t", p=128)

    def tm(apx):
        return apx.rearrange("(s p) d -> p s d", p=128)

    sb_ident = B.sb("sb_ident", [128, 128], BF16)
    sb_identf = B.sb("sb_identf", [128, 128], F32)
    sb_stair = B.sb("sb_stair", [128, 1024], BF16)
    sb_band = B.sb("sb_band", [128, 3, 384], BF16)
    sb_prot = B.sb("sb_prot", [128, 128], BF16)
    sb_ones1 = B.sb("sb_ones1", [128, 1], BF16)
    sb_onesr = B.sb("sb_onesr", [1, 128], F32)
    sb_onescf = B.sb("sb_onescf", [128, 1], F32)
    sb_iota64 = B.sb("sb_iota64", [128, HD], F32)
    sb_hsel = B.sb("sb_hsel", [128, NCH, H], BF16)
    sb_hsel2 = B.sb("sb_hsel2", [128, 2], BF16)
    sb_pairsel = B.sb("sb_pairsel", [2, 128], F32)
    sb_gsel = B.sb("sb_gsel", [128, H], BF16)
    sb_gselT = B.sb("sb_gselT", [H, 128], F32)
    sb_fb1 = B.sb("sb_fb1", [128, DEPTH, 44], F32)
    sb_fb2 = B.sb("sb_fb2", [128, DEPTH, NCH], F32)
    sb_flng = B.sb("sb_flng", [128, DEPTH, NCH], F32)
    sb_flnb = B.sb("sb_flnb", [128, DEPTH, NCH], F32)
    sb_fqs = B.sb("sb_fqs", [128, DEPTH, 2], F32)
    sb_fks = B.sb("sb_fks", [128, DEPTH, 2], F32)
    sb_fvals = B.sb("sb_fvals", [128, 8], F32)
    sb_onesrb = B.sb("sb_onesrb", [65, 512], BF16)
    for _i, _v in enumerate(FVALS):
        nc.const_aps.register(F32, _v, sb_fvals[:, _i:_i + 1])

    P.sem("ldc")
    for dst, src in [
        (sb_ident[:], c_ident[:, :]), (sb_identf[:], c_identf[:, :]),
        (sb_stair[:], c_stair[:, :]),
        (sb_band[:], c_band.rearrange("m p n -> p m n")), (sb_prot[:], c_prot[:, :]),
        (sb_ones1[:], c_ones1[:, :]), (sb_onesr[:], c_onesr[:, :]),
        (sb_onescf[:], c_onescf[:, :]),
        (sb_iota64[:], c_iota64[:, :]), (sb_hsel[:], c_hsel.rearrange("c p h -> p c h")),
        (sb_hsel2[:], c_hsel2[:, :]), (sb_pairsel[:], c_pairsel[:, :]),
        (sb_gsel[:], c_gsel[:, :]), (sb_gselT[:], c_gselT[:, :]),
        (sb_fb1[:], fb1.rearrange("l p n -> p l n")), (sb_fb2[:], fb2.rearrange("l p n -> p l n")),
        (sb_flng[:], flng.rearrange("l p n -> p l n")), (sb_flnb[:], flnb.rearrange("l p n -> p l n")),
        (sb_fqs[:], fqs.rearrange("l p n -> p l n")), (sb_fks[:], fks.rearrange("l p n -> p l n")),
        (sb_fvals[:], c_fvals[:, :]), (sb_onesrb[0:1, :], c_onesrb[:, :]),
        (sb_onesrb[64:65, :], c_onesrb[:, :]),
    ]:
        P.dma("sync", dst, src, "ldc")
    LDC = ("ldc", P.cnt["ldc"])
    for e in ENGINES:
        P.wait_only(e, [LDC])

    TTL = 512
    NTL = T // TTL

    def ln_pass(name, src, src2=None, dst_bf=None, dst_f32=None, dst_sum=None,
                no_norm=False, idx_hook=None):
        with ExitStack() as ph:
            sbp = lambda nm, shp, dt=F32: ph.enter_context(
                nc.sbuf_tensor(B.unique(nm), shp, dt))
            ln_x = [sbp("lnx", [128, NCH, TTL]) for _ in range(2)]
            ln_x2 = [sbp("lnx2", [128, NCH, TTL]) for _ in range(2)] if src2 is not None else None
            ln_xb = [sbp("lnxb", [128, NCH, TTL], BF16) for _ in range(2)]
            ln_ob = [sbp("lnob", [128, NCH, TTL], BF16) for _ in range(2)] if dst_bf is not None else None
            ln_of = [sbp("lnof", [128, NCH, TTL]) for _ in range(2)]
            ln_xq = sbp("lnxq", [128, NCH, TTL], BF16)
            R = sbp("lnrows", [1, 3 * 512])
            ln_mb = sbp("lnmb", [128, TTL])
            ln_rb = sbp("lnrb", [128, TTL])
            ps_s = ph.enter_context(nc.psum_tensor(B.unique("lnS"), [1, TTL], F32))
            ps_q = ph.enter_context(nc.psum_tensor(B.unique("lnQ"), [1, TTL], F32))
            ps_m = ph.enter_context(nc.psum_tensor(B.unique("lnM"), [128, TTL], F32))
            ps_r = ph.enter_context(nc.psum_tensor(B.unique("lnR"), [128, TTL], F32))
            ld = P.sem(name + "_ld")
            st = P.sem(name + "_st")
            slot_rel = [None, None]
            srcv = fm(src)
            src2v = fm(src2) if src2 is not None else None
            dbv = fm(dst_bf) if dst_bf is not None else None
            dfv = fm(dst_f32) if dst_f32 is not None else None
            dsv = fm(dst_sum) if dst_sum is not None else None
            for i in range(NTL):
                sl = i % 2
                t0 = i * TTL
                w = list(slot_rel[sl]) if slot_rel[sl] else []
                P.dma("sync", ln_x[sl][:], srcv[:, :, t0:t0 + TTL], ld, waits=w)
                if src2 is not None:
                    P.dma("sync", ln_x2[sl][:], src2v[:, :, t0:t0 + TTL], ld, waits=w)
                ldw = [(ld, P.cnt[ld])]
                if src2 is not None:
                    B.tt("vector", ln_x[sl][:], ln_x[sl][:], ln_x2[sl][:], ALU.add, waits=ldw)
                    ldw = [P.mark("vector")]
                    if dst_sum is not None:
                        P.dma("sync", dsv[:, :, t0:t0 + TTL], ln_x[sl][:], st, waits=ldw)
                if no_norm:
                    slot_rel[sl] = ldw + [(st, P.cnt[st])]
                    continue
                B.copy("vector", ln_xb[sl][:], ln_x[sl][:], waits=ldw)
                m_xb = P.mark("vector")
                B.act(ln_xq[:], ln_xb[sl][:], AF.Square, waits=[m_xb])
                m_xq = P.mark("scalar")
                for c in range(NCH):
                    B.mm(ps_s[:], sb_ones1[:], ln_xb[sl][:, c, :], c == 0, c == NCH - 1,
                         waits=[m_xb, P.mark("scalar")] if c == 0 else (),
                         inc="pe" if c == NCH - 1 else None)
                pe_s = ("pe", P.cnt["pe"])
                for c in range(NCH):
                    B.mm(ps_q[:], sb_ones1[:], ln_xq[:, c, :], c == 0, c == NCH - 1,
                         waits=[m_xq] if c == 0 else (),
                         inc="pe" if c == NCH - 1 else None)
                pe_q = ("pe", P.cnt["pe"])
                r0 = R[0:1, 0:TTL]
                r1 = R[0:1, 512:512 + TTL]
                r2 = R[0:1, 1024:1024 + TTL]
                B.scopy(r0, ps_s[:], waits=[pe_s, P.mark("vector")])
                B.scopy(r1, ps_q[:], waits=[pe_q])
                m_rows = P.mark("scalar")
                B.ts("vector", r0, r0, 1.0 / D, ALU.mult, waits=[m_rows])
                B.ts("vector", r1, r1, 1.0 / D, ALU.mult)
                B.tt("vector", r2, r0, r0, ALU.mult)
                B.tt("vector", r1, r1, r2, ALU.subtract)
                m_var = P.mark("vector")
                B.act(r2, r1, AF.Ln, bias=1e-5, waits=[m_var])
                B.act(r2, r2, AF.Exp, scale=-0.5)
                m_rstd = P.mark("scalar")
                B.mm(ps_m[:], sb_onesr[:], r0, True, True,
                     waits=[m_var, m_rstd, P.mark("scalar")], inc="pe")
                pe_m = ("pe", P.cnt["pe"])
                B.mm(ps_r[:], sb_onesr[:], r2, True, True, inc="pe")
                pe_r = ("pe", P.cnt["pe"])
                B.scopy(ln_mb[:], ps_m[:], waits=[pe_m, P.mark("vector")])
                B.scopy(ln_rb[:], ps_r[:], waits=[pe_r])
                m_bc = P.mark("scalar")
                B.tt("vector", ln_of[sl][:], ln_x[sl][:], bc_mid(ln_mb[:], NCH),
                     ALU.subtract, waits=[m_bc])
                if dst_f32 is not None or idx_hook is not None:
                    B.tt("vector", ln_of[sl][:], ln_of[sl][:], bc_mid(ln_rb[:], NCH), ALU.mult)
                    m_nf = P.mark("vector")
                    if dst_bf is not None:
                        B.copy("vector", ln_ob[sl][:], ln_of[sl][:])
                    if dst_f32 is not None:
                        P.dma("sync", dfv[:, :, t0:t0 + TTL], ln_of[sl][:], st, waits=[m_nf])
                elif dst_bf is not None:
                    B.tt("vector", ln_ob[sl][:], ln_of[sl][:], bc_mid(ln_rb[:], NCH), ALU.mult)
                m_out = P.mark("vector")
                if dst_bf is not None:
                    P.dma("sync", dbv[:, :, t0:t0 + TTL], ln_ob[sl][:], st, waits=[m_out])
                if idx_hook is not None:
                    idx_hook(i, t0, ln_of[sl], m_out)
                    slot_rel[sl] = [m_out, (st, P.cnt[st]), ("pe", P.cnt["pe"])]
                else:
                    slot_rel[sl] = [m_out, (st, P.cnt[st])]
            P.fence()

    def proj_pass(name, wlist, src_b, outs, tile_cols=512):
        with ExitStack() as ph:
            lw = P.sem(name + "_lw")
            sbw = []
            for wi, (wd, nco) in enumerate(wlist):
                wt = ph.enter_context(nc.sbuf_tensor(B.unique(name + "w"),
                                                     [128, NCH, nco * 128], BF16))
                P.dma("gpsimd", wt[:], wd.rearrange("(c p) o -> p c o", p=128), lw)
                sbw.append(wt)
            LW = (lw, P.cnt[lw])
            nfm = sum(1 for k, _ in outs if k == "fm")
            xb = [ph.enter_context(nc.sbuf_tensor(B.unique(name + "x"),
                                                  [128, NCH, tile_cols], BF16)) for _ in range(2)]
            obs = {}
            for oi, (kind, od) in enumerate(outs):
                if kind == "fm":
                    obs[oi] = [ph.enter_context(nc.sbuf_tensor(
                        B.unique(name + "o"), [128, NCH, tile_cols], BF16)) for _ in range(2)]
                else:
                    obs[oi] = [ph.enter_context(nc.sbuf_tensor(
                        B.unique(name + "ot"), [128, tile_cols // 128, D], BF16)) for _ in range(2)]
            pss = [ph.enter_context(nc.psum_tensor(B.unique(name + "p"), [128, tile_cols], F32))
                   for _ in range(4)]
            bank_rel = [None] * 4
            ld = P.sem(name + "_ld")
            st = P.sem(name + "_st")
            srcv3 = fm(src_b)
            slot_rel = [None, None]
            ntiles = T // tile_cols
            pb = 0
            for i in range(ntiles):
                sl = i % 2
                t0 = i * tile_cols
                w = list(slot_rel[sl]) if slot_rel[sl] else [LW]
                P.dma("sync", xb[sl][:], srcv3[:, :, t0:t0 + tile_cols], ld, waits=w)
                ldw = [(ld, P.cnt[ld])]
                rel_marks = []
                for oi, (kind, od) in enumerate(outs):
                    wt = sbw[oi]
                    ob = obs[oi][sl]
                    if kind == "fm":
                        for co in range(NCH):
                            bk = pb % 4
                            ps = pss[bk]
                            pb += 1
                            w0 = ldw + [LW] + ([bank_rel[bk]] if bank_rel[bk] else [])
                            for ci in range(NCH):
                                B.mm(ps[:], wt[:, ci, co * 128:(co + 1) * 128],
                                     xb[sl][:, ci, :], ci == 0, ci == NCH - 1,
                                     waits=w0 if ci == 0 else (),
                                     inc="pe" if ci == NCH - 1 else None)
                            B.act(ob[:, co, :], ps[:], AF.Copy, waits=[("pe", P.cnt["pe"])])
                            bank_rel[bk] = P.mark("scalar")
                        m_o = P.mark("scalar")
                        P.dma("sync", fm(od)[:, :, t0:t0 + tile_cols], ob[:], st, waits=[m_o])
                        rel_marks.append(m_o)
                    else:
                        nts = tile_cols // 128
                        for s4 in range(nts):
                            for eh in range(2):
                                bk = pb % 4
                                ps = pss[bk]
                                pb += 1
                                w0 = ldw + [LW] + ([bank_rel[bk]] if bank_rel[bk] else [])
                                for ci in range(NCH):
                                    B.mm(ps[:], xb[sl][:, ci, s4 * 128:(s4 + 1) * 128],
                                         wt[:, ci, eh * 512:(eh + 1) * 512],
                                         ci == 0, ci == NCH - 1,
                                         waits=w0 if ci == 0 else (),
                                         inc="pe" if ci == NCH - 1 else None)
                                B.act(ob[:, s4, eh * 512:(eh + 1) * 512], ps[:], AF.Copy,
                                      waits=[("pe", P.cnt["pe"])])
                                bank_rel[bk] = P.mark("scalar")
                        m_o = P.mark("scalar")
                        P.dma("sync", tm(od)[:, i * nts:(i + 1) * nts, :], ob[:], st,
                              waits=[m_o])
                        rel_marks.append(m_o)
                slot_rel[sl] = rel_marks + [(st, P.cnt[st])]
            P.fence()

    def attn_a_flash(x_res_src, x_dst):
        QSUP = 1024
        KSUP = 1024
        NQT = QSUP // 128
        KCH = KSUP // 128
        with ExitStack() as ph:
            sbp = lambda nm, shp, dt: ph.enter_context(nc.sbuf_tensor(B.unique(nm), shp, dt))
            sb_qt = sbp("faq", [128, NCH, QSUP], BF16)
            sb_kt = sbp("fak", [128, NCH, KSUP], BF16)
            sb_v = sbp("fav", [128, KCH, D], BF16)
            sb_o = sbp("fao", [128, NCH, QSUP], F32)
            sb_pr = sbp("fapr", [128, KCH, QSUP], BF16)
            sb_ml = sbp("faml", [128, 64], F32)
            sb_row = sbp("farow", [1, 2 * QSUP], F32)
            sb_ab = sbp("faab", [128, QSUP], F32)
            sb_xr = sbp("faxr", [128, NCH, QSUP], F32)
            psp = lambda nm, shp, dt: ph.enter_context(nc.psum_tensor(B.unique(nm), shp, dt))
            ps_sim = [psp("fas", [128, 512], F32) for _ in range(2)]
            ps_tr = [psp("fat", [128, 128], BF16) for _ in range(2)]
            ps_pv = [psp("fap", [128, 512], F32) for _ in range(2)]
            ps_row = psp("far", [1, 512], F32)
            ps_ab = psp("faa", [128, 512], F32)
            lq = P.sem("fa_lq")
            lk = P.sem("fa_lk")
            lx = P.sem("fa_lx")
            stx = P.sem("fa_st")
            qsv, ksv = fm(QS), fm(KS)
            xrv, xdv = fm(x_res_src), fm(x_dst)
            vv = tm(VTM)
            n_qs = T // QSUP
            prev_qs_rel = None
            for qs in range(n_qs):
                q0 = qs * QSUP
                n_ks = ((qs + 1) * QSUP + KSUP - 1) // KSUP
                w = list(prev_qs_rel) if prev_qs_rel else []
                P.dma("sync", sb_qt[:], qsv[:, :, q0:q0 + QSUP], lq, waits=w)
                P.dma("sync", sb_xr[:], xrv[:, :, q0:q0 + QSUP], lx, waits=w)
                mq = [(lq, P.cnt[lq])]
                mx = [(lx, P.cnt[lx])]
                B.memset("vector", sb_ml[:, 0:8], -3.0e38, waits=w)
                B.memset("vector", sb_ml[:, 8:16], 0.0)
                for c in range(NCH):
                    B.memset("vector", sb_o[:, c, :], 0.0)
                m_init = P.mark("vector")
                last_mg_m = None
                for ks in range(n_ks):
                    k0 = ks * KSUP
                    wk = [m_init] if ks == 0 else list(last_mg_m)
                    P.dma("sync", sb_kt[:], ksv[:, :, k0:k0 + KSUP], lk, waits=wk)
                    P.dma("sync", sb_v[:], vv[:, k0 // 128:(k0 + KSUP) // 128, :], lk,
                          waits=wk)
                    mk = [(lk, P.cnt[lk])]
                    tile_cov = []
                    m_exp_last = None
                    for j in range(NQT):
                        jq = qs * NQT + j
                        qend = (jq + 1) * 128
                        ncov = max(0, min(KSUP, qend - k0))
                        nsub = (ncov + 511) // 512
                        tile_cov.append((ncov + 127) // 128)
                        if nsub == 0:
                            continue
                        for u in range(nsub):
                            ps = ps_sim[u % 2]
                            k0s = k0 + u * 512
                            diag = (k0s <= jq * 128) and (jq * 128 < k0s + 512)
                            bw = mq + mk + [P.mark("scalar")]
                            first = True
                            if diag:
                                pvar = (jq * 128 - k0s) // 128
                                B.mm(ps[:], sb_ident[:],
                                     sb_stair[:, 384 - 128 * pvar:896 - 128 * pvar],
                                     True, False, waits=bw)
                                first = False
                            for ci in range(NCH):
                                B.mm(ps[:], sb_qt[:, ci, j * 128:(j + 1) * 128],
                                     sb_kt[:, ci, u * 512:(u + 1) * 512],
                                     first and ci == 0, ci == NCH - 1,
                                     waits=bw if (first and ci == 0) else (),
                                     inc="pe" if ci == NCH - 1 else None)
                            pes = ("pe", P.cnt["pe"])
                            B.red(sb_ml[:, 16 + u:17 + u], ps[:], ALU.max, waits=[pes])
                        B.red(sb_ml[:, 32:33], sb_ml[:, 16:16 + nsub], ALU.max)
                        B.tt("vector", sb_ml[:, 32:33], sb_ml[:, 0 + j:1 + j],
                             sb_ml[:, 32:33], ALU.max)
                        B.tt("vector", sb_ml[:, 33:34], sb_ml[:, 0 + j:1 + j],
                             sb_ml[:, 32:33], ALU.subtract)
                        B.copy("vector", sb_ml[:, 0 + j:1 + j], sb_ml[:, 32:33])
                        B.ts("vector", sb_ml[:, 34:35], sb_ml[:, 32:33], -1.0 / 32.0, ALU.mult)
                        m_v = P.mark("vector")
                        B.act(sb_ml[:, 24 + j:25 + j], sb_ml[:, 33:34], AF.Exp,
                              scale=1.0 / 32.0, waits=[m_v])
                        for u in range(nsub):
                            ps = ps_sim[u % 2]
                            B.act(sb_pr[:, 4 * u:4 * (u + 1), j * 128:(j + 1) * 128],
                                  ps[:], AF.Exp, scale=1.0 / 32.0, bias=sb_ml[:, 34:35],
                                  accum=sb_ml[:, 20 + u:21 + u])
                        m_exp_last = P.mark("scalar")
                        B.red(sb_ml[:, 33:34], sb_ml[:, 20:20 + nsub], ALU.add,
                              waits=[m_exp_last])
                        B.tt("vector", sb_ml[:, 8 + j:9 + j], sb_ml[:, 8 + j:9 + j],
                             sb_ml[:, 24 + j:25 + j], ALU.mult)
                        B.tt("vector", sb_ml[:, 8 + j:9 + j], sb_ml[:, 8 + j:9 + j],
                             sb_ml[:, 33:34], ALU.add)
                    for j in range(NQT):
                        cov = tile_cov[j]
                        if cov < KCH:
                            B.memset("vector", sb_pr[:, cov:KCH, j * 128:(j + 1) * 128], 0.0)
                    m_rag = P.mark("vector")
                    for hf in range(2):
                        for j4 in range(4):
                            j = 4 * hf + j4
                            jq = qs * NQT + j
                            covered = (jq + 1) * 128 > k0
                            src_col = sb_ml[:, 24 + j:25 + j] if covered else sb_onescf[:]
                            B.mm(ps_row[0:1, j4 * 128:(j4 + 1) * 128], src_col, sb_identf[:],
                                 True, True,
                                 waits=[m_exp_last, P.mark("scalar")] if j4 == 0 else (),
                                 inc="pe" if j4 == 3 else None)
                        B.scopy(sb_row[0:1, hf * 512:(hf + 1) * 512], ps_row[0:1, :],
                                waits=[("pe", P.cnt["pe"])])
                        m_ar = P.mark("scalar")
                        B.mm(ps_ab[:], sb_onesr[:], sb_row[0:1, hf * 512:(hf + 1) * 512],
                             True, True, waits=[m_ar, P.mark("scalar")], inc="pe")
                        B.scopy(sb_ab[:, hf * 512:(hf + 1) * 512], ps_ab[:],
                                waits=[("pe", P.cnt["pe"])])
                    m_ab = P.mark("scalar")
                    maxcov = max(tile_cov)
                    tb = 0
                    tr_rel = [None, None]
                    for j in range(NQT):
                        for kc in range(tile_cov[j]):
                            bk = tb % 2
                            tb += 1
                            w0 = [m_exp_last, m_rag]
                            if tr_rel[bk]:
                                w0.append(tr_rel[bk])
                            B.tr(ps_tr[bk][:], sb_pr[:, kc, j * 128:(j + 1) * 128],
                                 sb_ident[:], waits=w0, inc="pe")
                            B.copy("vector", sb_pr[:, kc, j * 128:(j + 1) * 128],
                                   ps_tr[bk][:], waits=[("pe", P.cnt["pe"])])
                            tr_rel[bk] = P.mark("vector")
                    m_trv = P.mark("vector")
                    for dc in range(NCH):
                        for half in range(2):
                            ps = ps_pv[half]
                            for kc in range(maxcov):
                                B.mm(ps[:], sb_v[:, kc, dc * 128:(dc + 1) * 128],
                                     sb_pr[:, kc, half * 512:(half + 1) * 512],
                                     kc == 0, kc == maxcov - 1,
                                     waits=[m_trv, P.mark("vector")] if kc == 0 else (),
                                     inc="pe" if kc == maxcov - 1 else None)
                        pe_pv = ("pe", P.cnt["pe"])
                        B.tt("vector", sb_o[:, dc, :], sb_o[:, dc, :], sb_ab[:],
                             ALU.mult, waits=[pe_pv, m_ab])
                        B.tt("vector", sb_o[:, dc, 0:512], sb_o[:, dc, 0:512],
                             ps_pv[0][:], ALU.add)
                        B.tt("vector", sb_o[:, dc, 512:1024], sb_o[:, dc, 512:1024],
                             ps_pv[1][:], ALU.add)
                    last_mg_m = [P.mark("vector")]
                B.recip(sb_ml[:, 40:48], sb_ml[:, 8:16], waits=last_mg_m)
                m_li = P.mark("vector")
                for hf in range(2):
                    for j4 in range(4):
                        j = 4 * hf + j4
                        B.mm(ps_row[0:1, j4 * 128:(j4 + 1) * 128], sb_ml[:, 40 + j:41 + j],
                             sb_identf[:], True, True,
                             waits=[m_li, P.mark("scalar")] if j4 == 0 else (),
                             inc="pe" if j4 == 3 else None)
                    B.scopy(sb_row[0:1, QSUP + hf * 512:QSUP + (hf + 1) * 512],
                            ps_row[0:1, :], waits=[("pe", P.cnt["pe"])])
                    m_lr = P.mark("scalar")
                    B.mm(ps_ab[:], sb_onesr[:],
                         sb_row[0:1, QSUP + hf * 512:QSUP + (hf + 1) * 512],
                         True, True, waits=[m_lr, P.mark("scalar")], inc="pe")
                    B.scopy(sb_ab[:, hf * 512:(hf + 1) * 512], ps_ab[:],
                            waits=[("pe", P.cnt["pe"])])
                m_lb = P.mark("scalar")
                for dc in range(NCH):
                    B.tt("vector", sb_o[:, dc, :], sb_o[:, dc, :], sb_ab[:], ALU.mult,
                         waits=[m_lb])
                    B.tt("vector", sb_o[:, dc, :], sb_o[:, dc, :], sb_xr[:, dc, :],
                         ALU.add, waits=mx)
                m_fin = P.mark("vector")
                P.dma("sync", xdv[:, :, q0:q0 + QSUP], sb_o[:], stx, waits=[m_fin])
                prev_qs_rel = [m_fin, (stx, P.cnt[stx])]
            P.fence()

    def inner_attn(l, ysrc, ydst):
        TTA = 256
        NT = T // TTA
        with ExitStack() as ph:
            sbp = lambda nm, shp, dt: ph.enter_context(nc.sbuf_tensor(B.unique(nm), shp, dt))
            psp = lambda nm, shp, dt=F32: ph.enter_context(nc.psum_tensor(B.unique(nm), shp, dt))
            lw = P.sem("ia_lw")
            w_qkv = sbp("iawq", [128, NCH, 3 * D], BF16)
            w_o = sbp("iawo", [128, NCH, D], BF16)
            hselT = sbp("iaht", [H, NCH, 128], BF16)
            P.dma("gpsimd", w_qkv[:],
                  fWqkv.rearrange("l (c p) n -> p l c n", p=128)[:, l, :, :], lw)
            P.dma("gpsimd", w_o[:],
                  fWo.rearrange("l (c p) n -> p l c n", p=128)[:, l, :, :], lw)
            P.dma("sync", hselT[:], c_hselT[:, :, :], lw)
            LW = (lw, P.cnt[lw])
            ysh = [sbp("iay", [128, NCH, TTA + 1], F32) for _ in range(2)]
            yshb = [sbp("iayb", [128, NCH, TTA], BF16) for _ in range(2)]
            qkf = sbp("iaqk", [128, NCH, TTA], F32)
            qkq = sbp("iaqq", [128, NCH, TTA], BF16)
            qpb = sbp("iaqp", [128, NCH, TTA], BF16)
            kwin = sbp("iakw", [128, NCH, 2 * TTA], BF16)
            vwin = sbp("iavw", [128, 4, D], BF16)
            tb4 = sbp("iatb", [128, 4, TTA], F32)
            ab4 = sbp("iaab", [128, 4, TTA], F32)
            rn2 = sbp("iarn", [16, TTA], F32)
            rn2b = sbp("iarnb", [16, TTA], BF16)
            probs_all = sbp("iapr", [128, 32, 384], BF16)
            prT = [sbp("iapt", [128, 3 * 128], BF16) for _ in range(2)]
            tmpf = sbp("iatm", [128, TTA], F32)
            sml = sbp("iasm", [128, 128], F32)
            aout = sbp("iaao", [128, NCH, TTA], BF16)
            yout = [sbp("iayo", [128, NCH, TTA], F32) for _ in range(2)]
            ps_mm = [psp("iam", [128, 512]) for _ in range(2)]
            ps_nq = psp("ian", [16, TTA])
            ps_rb = psp("iab", [128, TTA])
            ps_rot = psp("iar", [128, TTA])
            ps_sim = [psp("ias", [128, 384]) for _ in range(2)]
            ps_pvo = psp("iap", [128, 128])
            rot_bf = ps_rot[:].bitcast(BF16)
            rb_bf = ps_rb[:].bitcast(BF16)
            ld = P.sem("ia_ld")
            st = P.sem("ia_st")
            tbs = P.sem("ia_tb")
            srcv, dstv = fm(ysrc), fm(ydst)
            slot_rel = [None, None]
            tb_rel = None
            kv_rel = None
            mm_rel = [None, None]
            nq_rel = None
            rb_rel = None
            rot_rel = None
            sim_rel = [None, None]
            prT_rel = [None, None]
            tr_rel = [None, None]
            pvo_rel = None
            ldw_i = {}

            def emit_load(i):
                if i >= NT:
                    return
                sl = i % 2
                t0 = i * TTA
                w = list(slot_rel[sl]) if slot_rel[sl] else [LW]
                if i == 0:
                    P.dma("sync", ysh[sl][:, :, 1:TTA + 1], srcv[:, :, 0:TTA], ld, waits=w)
                    B.memset("vector", ysh[sl][:, :, 0:1], 0.0, waits=w)
                else:
                    P.dma("sync", ysh[sl][:], srcv[:, :, t0 - 1:t0 + TTA], ld, waits=w)
                ldw_i[i] = [(ld, P.cnt[ld])] + ([P.mark("vector")] if i == 0 else [])

            def emit_cast(i):
                if i >= NT:
                    return
                sl = i % 2
                B.copy("vector", yshb[sl][:, 0:4, :], ysh[sl][:, 0:4, 1:TTA + 1],
                       waits=ldw_i[i] + [("pe", P.cnt["pe"])])
                B.copy("vector", yshb[sl][:, 4:8, :], ysh[sl][:, 4:8, 0:TTA])

            emit_load(0)
            emit_cast(0)
            for i in range(NT):
                sl = i % 2
                t0 = i * TTA
                tw = list(tb_rel) if tb_rel else []
                for k4, tb in enumerate((tbl_cs, tbl_ss, tbl_ci, tbl_si)):
                    P.dma("sync", tb4[:, k4, :],
                          bass.AP(tb.tensor, tb.offset + t0, [[0, 2], [T, HD], [1, TTA]]),
                          tbs, waits=tw)
                tbw = [(tbs, P.cnt[tbs])]
                for k4, col in ((0, sb_fqs[:, l, 0:1]), (1, sb_fqs[:, l, 1:2]),
                                (2, sb_fks[:, l, 0:1]), (3, sb_fks[:, l, 1:2])):
                    B.ts("vector", ab4[:, k4, :], tb4[:, k4, :], col, ALU.mult, waits=tbw)
                tb_rel = [P.mark("vector")]
                m_yb = P.mark("vector")
                wsh = list(kv_rel) if kv_rel else []
                if i == 0:
                    B.memset("vector", kwin[:, :, 0:TTA], 0.0, waits=wsh)
                    B.memset("vector", vwin[:, 0:2, :], 0.0)
                else:
                    B.copy("vector", kwin[:, :, 0:TTA], kwin[:, :, TTA:2 * TTA], waits=wsh)
                    B.copy("vector", vwin[:, 0:2, :], vwin[:, 2:4, :])
                m_shift = P.mark("vector")
                for tn in range(2):
                    for c in range(NCH):
                        bk = c % 2
                        ps = ps_mm[bk]
                        w0 = [m_yb, LW] + ([mm_rel[bk]] if mm_rel[bk] else [])
                        for ci in range(NCH):
                            B.mm(ps[:, 0:TTA], w_qkv[:, ci, (tn * 8 + c) * 128:(tn * 8 + c + 1) * 128],
                                 yshb[sl][:, ci, :], ci == 0, ci == NCH - 1,
                                 waits=w0 if ci == 0 else (),
                                 inc="pe" if ci == NCH - 1 else None)
                        B.scopy(qkf[:, c, :], ps[:, 0:TTA],
                                waits=[("pe", P.cnt["pe"]), P.mark("vector")])
                        B.act(qkq[:, c, :], qkf[:, c, :], AF.Square)
                        mm_rel[bk] = P.mark("scalar")
                    m_sq = P.mark("scalar")
                    for c in range(NCH):
                        B.mm(ps_nq[:], sb_hsel[:, c, :], qkq[:, c, :], c == 0, c == NCH - 1,
                             waits=([m_sq] + ([nq_rel] if nq_rel else [])) if c == 0 else (),
                             inc="pe" if c == NCH - 1 else None)
                    B.act(rn2[:], ps_nq[:], AF.Ln,
                          waits=[("pe", P.cnt["pe"]), P.mark("vector")])
                    nq_rel = P.mark("scalar")
                    B.act(rn2b[:], rn2[:], AF.Exp, scale=-0.5)
                    m_rn = P.mark("scalar")
                    for c in range(NCH):
                        B.mm(ps_rb[:], hselT[0:16, c, :], rn2b[:], True, True,
                             waits=[m_rn, LW, P.mark("scalar")] +
                             ([rb_rel] if rb_rel else []), inc="pe")
                        pe_rb = ("pe", P.cnt["pe"])
                        B.tt("vector", qkf[:, c, :], qkf[:, c, :], ps_rb[:], ALU.mult,
                             waits=[pe_rb])
                        B.copy("vector", qkq[:, c, :], qkf[:, c, :])
                        rb_rel = P.mark("vector")
                        m_qnb = rb_rel
                        B.mm(ps_rot[:], sb_prot[:], qkq[:, c, :], True, True,
                             waits=[m_qnb] + ([rot_rel] if rot_rel else []), inc="pe")
                        pe_rot = ("pe", P.cnt["pe"])
                        B.tt("vector", qkf[:, c, :], qkf[:, c, :], ab4[:, 2 * tn, :], ALU.mult)
                        B.tt("vector", tmpf[:], ps_rot[:], ab4[:, 2 * tn + 1, :], ALU.mult,
                             waits=[pe_rot])
                        dstqp = qpb[:, c, :] if tn == 0 else kwin[:, c, TTA:2 * TTA]
                        ww = [m_shift] if tn == 1 else []
                        B.tt("vector", dstqp, qkf[:, c, :], tmpf[:], ALU.add, waits=ww)
                        rot_rel = P.mark("vector")
                m_kq = P.mark("vector")
                emit_load(i + 1)
                emit_cast(i + 1)
                for s2 in range(2):
                    for eh in range(2):
                        bk = eh % 2
                        ps = ps_mm[bk]
                        w0 = [m_yb, LW] + ([mm_rel[bk]] if mm_rel[bk] else [])
                        for ci in range(NCH):
                            B.mm(ps[:], yshb[sl][:, ci, s2 * 128:(s2 + 1) * 128],
                                 w_qkv[:, ci, 2 * D + eh * 512:2 * D + (eh + 1) * 512],
                                 ci == 0, ci == NCH - 1,
                                 waits=w0 if ci == 0 else (),
                                 inc="pe" if ci == NCH - 1 else None)
                        B.act(vwin[:, 2 + s2, eh * 512:(eh + 1) * 512], ps[:], AF.Copy,
                              waits=[("pe", P.cnt["pe"]), m_shift])
                        mm_rel[bk] = P.mark("scalar")
                m_v = P.mark("scalar")
                units = [(qsub, c, par) for qsub in range(2)
                         for c in range(NCH) for par in range(2)]
                NU = len(units)
                pes_u = [None] * NU
                mb_u = [None] * NU
                msc_u = [None] * NU
                mpt_u = [None] * NU
                tile_pv_guard = kv_rel[0] if kv_rel else None
                trb = [rot_bf, rb_bf]
                for step in range(NU + 3):
                    if step < NU:
                        u = step
                        qsub, c, par = units[u]
                        qt0 = t0 + 128 * qsub
                        var = 1 if qt0 == 0 else (2 if qt0 == 128 else 0)
                        base = 64 * par
                        ps = ps_sim[u % 2]
                        w0 = [m_kq, m_v] + ([sim_rel[u % 2]] if sim_rel[u % 2] else [])
                        B.mm(ps[:], sb_ident[:], sb_band[:, var, :], True, False,
                             waits=w0)
                        B.mm(ps[:], qpb[base:base + 64, c, qsub * 128:(qsub + 1) * 128],
                             kwin[base:base + 64, c, qsub * 128:qsub * 128 + 384],
                             False, True, inc="pe")
                        pes_u[u] = ("pe", P.cnt["pe"])
                        B.red(sml[:, 4 * u:4 * u + 1], ps[:], ALU.max, negate=True,
                              waits=[pes_u[u]] + ([P.mark("scalar")] if u == 0 else []))
                        B.ts("vector", sml[:, 4 * u + 1:4 * u + 2],
                             sml[:, 4 * u:4 * u + 1], 8.0, ALU.mult)
                        mb_u[u] = P.mark("vector")
                    if 1 <= step < NU + 1:
                        v = step - 1
                        ps = ps_sim[v % 2]
                        prv = probs_all[:, v, :]
                        w1 = [mb_u[v], pes_u[v]]
                        if tile_pv_guard is not None:
                            w1.append(tile_pv_guard)
                        B.act(prv, ps[:], AF.Exp, scale=8.0,
                              bias=sml[:, 4 * v + 1:4 * v + 2],
                              accum=sml[:, 4 * v + 2:4 * v + 3], waits=w1)
                        sim_rel[v % 2] = P.mark("scalar")
                        B.recip(sml[:, 4 * v + 3:4 * v + 4],
                                sml[:, 4 * v + 2:4 * v + 3], waits=[sim_rel[v % 2]])
                        B.ts("vector", prv, prv, sml[:, 4 * v + 3:4 * v + 4], ALU.mult)
                        msc_u[v] = P.mark("vector")
                    if 2 <= step < NU + 2:
                        v = step - 2
                        prv = probs_all[:, v, :]
                        tb = trb[v % 2]
                        for kc in range(3):
                            B.tr(tb[:, kc * 128:(kc + 1) * 128],
                                 prv[:, kc * 128:(kc + 1) * 128], sb_ident[:],
                                 waits=([msc_u[v]] + ([tr_rel[v % 2]] if tr_rel[v % 2] else []))
                                 if kc == 0 else (),
                                 inc="pe" if kc == 2 else None)
                        prb = prT[v % 2]
                        B.scopy(prb[:], tb[:, 0:384],
                                waits=[("pe", P.cnt["pe"])] +
                                ([prT_rel[v % 2]] if prT_rel[v % 2] else []))
                        tr_rel[v % 2] = P.mark("scalar")
                        mpt_u[v] = tr_rel[v % 2]
                    if step >= 3:
                        v = step - 3
                        qsub, c, par = units[v]
                        base = 64 * par
                        h = 2 * c + par
                        prb = prT[v % 2]
                        for kc in range(3):
                            B.mm(ps_pvo[base:base + 64, :],
                                 vwin[:, qsub + kc, 64 * h:64 * h + 64],
                                 prb[:, kc * 128:(kc + 1) * 128],
                                 kc == 0, kc == 2,
                                 waits=([mpt_u[v]] + ([pvo_rel] if (par == 0 and pvo_rel) else []))
                                 if kc == 0 else (),
                                 inc="pe" if kc == 2 else None)
                        prT_rel[v % 2] = ("pe", P.cnt["pe"])
                        if par == 1:
                            B.act(aout[:, c, qsub * 128:(qsub + 1) * 128], ps_pvo[:],
                                  AF.Copy, waits=[("pe", P.cnt["pe"])])
                            pvo_rel = P.mark("scalar")
                kv_rel = [("pe", P.cnt["pe"])]
                m_ao = P.mark("scalar")
                for co in range(NCH):
                    bk = co % 2
                    ps = ps_mm[bk]
                    w0 = [m_ao, LW] + ([mm_rel[bk]] if mm_rel[bk] else [])
                    for ci in range(NCH):
                        B.mm(ps[:, 0:TTA], w_o[:, ci, co * 128:(co + 1) * 128],
                             aout[:, ci, :], ci == 0, ci == NCH - 1,
                             waits=w0 if ci == 0 else (),
                             inc="pe" if ci == NCH - 1 else None)
                    B.tt("vector", yout[sl][:, co, :], ps[:, 0:TTA],
                         ysh[sl][:, co, 1:TTA + 1], ALU.add,
                         waits=[("pe", P.cnt["pe"])])
                    mm_rel[bk] = P.mark("vector")
                m_yo = P.mark("vector")
                P.dma("sync", dstv[:, :, t0:t0 + TTA], yout[sl][:], st, waits=[m_yo])
                slot_rel[sl] = [m_yo, (st, P.cnt[st])]
            P.fence()

    FFN_BLOCKS = [(j, 128) for j in range(21)] + [(21, 42)]

    def inner_ffn(l, ysrc, ydst):
        TTF = 256
        NT = T // TTF
        with ExitStack() as ph:
            sbp = lambda nm, shp, dt: ph.enter_context(nc.sbuf_tensor(B.unique(nm), shp, dt))
            psp = lambda nm, shp: ph.enter_context(nc.psum_tensor(B.unique(nm), shp, F32))
            lw = P.sem("ff_lw")
            w1 = sbp("ffw1", [128, NCH, 2 * DI], BF16)
            w2 = sbp("ffw2", [128, 22, D], BF16)
            biasr = sbp("ffbr", [65, 30, 128], BF16)
            P.dma("gpsimd", w1[:],
                  fW1.rearrange("l (c p) n -> p l c n", p=128)[:, l, :, :], lw)
            P.dma("gpsimd", w2[:, 0:21, :],
                  fW2[l:l + 1, 0:2688, :].rearrange("o (j p) n -> p (o j) n", p=128), lw)
            P.dma("gpsimd", w2[0:42, 21, :],
                  fW2[l:l + 1, 2688:2730, :].rearrange("o j n -> (o j) n"), lw)
            P.dma("sync", biasr[0:1, :, :], c_fbr[l:l + 1, 0, :, :], lw)
            P.dma("sync", biasr[64:65, :, :], c_fbr[l:l + 1, 1, :, :], lw)
            LW = (lw, P.cnt[lw])
            ysh = [sbp("ffy", [128, NCH, TTF + 1], F32) for _ in range(2)]
            yshb = sbp("ffyb", [128, NCH, TTF], BF16)
            xnb = [sbp("ffxb", [128, NCH, TTF], BF16) for _ in range(2)]
            hb = sbp("ffh", [128, 22, TTF], BF16)

            R = sbp("ffrw", [1, 3 * 256], F32)
            gfp = sbp("ffg", [128, TTF], F32)
            yout = [sbp("ffyo", [128, NCH, TTF], F32) for _ in range(2)]
            ps_w = [psp("ffpw", [128, 512]) for _ in range(4)]
            ps_s = psp("ffps", [1, TTF])
            ps_q = psp("ffpq", [1, TTF])
            ps_m = psp("ffpm", [128, TTF])
            ps_r = psp("ffpr", [128, TTF])
            ld = P.sem("ff_ld")
            st = P.sem("ff_st")
            srcv, dstv = fm(ysrc), fm(ydst)
            slot_rel = [None, None]
            w_rel = [None, None, None, None]
            s_rel = [None]
            m_rel = [None]
            m_xb = {}

            def stageA(i):
                sl = i % 2
                t0 = i * TTF
                w = list(slot_rel[sl]) if slot_rel[sl] else [LW]
                if i == 0:
                    P.dma("sync", ysh[sl][:, :, 1:TTF + 1], srcv[:, :, 0:TTF], ld, waits=w)
                    B.memset("vector", ysh[sl][:, :, 0:1], 0.0, waits=w)
                else:
                    P.dma("sync", ysh[sl][:], srcv[:, :, t0 - 1:t0 + TTF], ld, waits=w)
                ldw = [(ld, P.cnt[ld])] + ([P.mark("vector")] if i == 0 else [])
                B.copy("vector", yshb[:, 0:4, :], ysh[sl][:, 0:4, 1:TTF + 1],
                       waits=ldw + [("pe", P.cnt["pe"])])
                B.copy("vector", yshb[:, 4:8, :], ysh[sl][:, 4:8, 0:TTF])
                m_yb = P.mark("vector")
                ysq = xnb[sl]
                B.act(ysq[:], yshb[:], AF.Square, waits=[m_yb])
                m_sq = P.mark("scalar")
                for c in range(NCH):
                    B.mm(ps_s[:], sb_ones1[:], yshb[:, c, :], c == 0, c == NCH - 1,
                         waits=([m_yb] + ([s_rel[0]] if s_rel[0] else [])) if c == 0 else (),
                         inc="pe" if c == NCH - 1 else None)
                pe_s = ("pe", P.cnt["pe"])
                for c in range(NCH):
                    B.mm(ps_q[:], sb_ones1[:], ysq[:, c, :], c == 0, c == NCH - 1,
                         waits=[m_sq] if c == 0 else (),
                         inc="pe" if c == NCH - 1 else None)
                pe_q = ("pe", P.cnt["pe"])
                r0 = R[0:1, 0:TTF]
                r1 = R[0:1, 256:256 + TTF]
                r2 = R[0:1, 512:512 + TTF]
                B.scopy(r0, ps_s[:], waits=[pe_s, P.mark("vector")])
                B.scopy(r1, ps_q[:], waits=[pe_q])
                s_rel[0] = P.mark("scalar")
                B.ts("vector", r0, r0, 1.0 / D, ALU.mult, waits=[s_rel[0]])
                B.ts("vector", r1, r1, 1.0 / D, ALU.mult)
                B.tt("vector", r2, r0, r0, ALU.mult)
                B.tt("vector", r1, r1, r2, ALU.subtract)
                m_var = P.mark("vector")
                B.act(r2, r1, AF.Ln, bias=1e-5, waits=[m_var])
                B.act(r2, r2, AF.Exp, scale=-0.5)
                m_rstd = P.mark("scalar")
                B.mm(ps_m[:], sb_onesr[:], r0, True, True,
                     waits=[m_var, m_rstd] + ([m_rel[0]] if m_rel[0] else []), inc="pe")
                pe_m = ("pe", P.cnt["pe"])
                B.mm(ps_r[:], sb_onesr[:], r2, True, True, inc="pe")
                pe_r = ("pe", P.cnt["pe"])
                xnf = yout[sl]
                B.tt("vector", xnf[:, 0:4, :], ysh[sl][:, 0:4, 1:TTF + 1],
                     bc_mid(ps_m[:], 4), ALU.subtract,
                     waits=[pe_m, pe_r] + (list(slot_rel[sl]) if slot_rel[sl] else []))
                B.tt("vector", xnf[:, 4:8, :], ysh[sl][:, 4:8, 0:TTF],
                     bc_mid(ps_m[:], 4), ALU.subtract)
                B.tt("vector", xnf[:], xnf[:], bc_mid(ps_r[:], NCH), ALU.mult)
                m_rel[0] = P.mark("vector")
                m_xn = m_rel[0]
                for c in range(NCH):
                    B.ts("vector", xnb[sl][:, c, :], xnf[:, c, :],
                         sb_flng[:, l, c:c + 1], ALU.mult,
                         s2=sb_flnb[:, l, c:c + 1], op2=ALU.add,
                         waits=[m_xn] if c == 0 else ())
                m_xb[i] = P.mark("vector")

            def stageB(i, emit_mid=None):
                sl = i % 2
                t0 = i * TTF
                for (j, sz) in FFN_BLOCKS:
                    off = 256 * j
                    for half in (1, 0):
                        bk = 2 * (j % 2) + half
                        ps = ps_w[bk]
                        o0 = off + half * sz
                        w0 = [m_xb[i], LW] + ([w_rel[bk]] if w_rel[bk] else [])
                        bsl = biasr[0:1, j, 0:sz] if half == 0 else biasr[64:65, j, 0:sz]
                        osl = sb_onesrb[0:1, 0:TTF] if half == 0 else sb_onesrb[64:65, 0:TTF]
                        B.mm(ps[0:sz, 0:TTF], bsl, osl, True, False, waits=w0)
                        for ci in range(NCH):
                            B.mm(ps[0:sz, 0:TTF], w1[:, ci, o0:o0 + sz],
                                 xnb[sl][:, ci, :], False, ci == NCH - 1,
                                 inc="pe" if ci == NCH - 1 else None)
                        if half == 1:
                            B.act(gfp[0:sz, :], ps[0:sz, 0:TTF], AF.Gelu,
                                  waits=[("pe", P.cnt["pe"]), P.mark("vector")])
                            w_rel[bk] = P.mark("scalar")
                    abk = 2 * (j % 2)
                    B.tt("vector", hb[0:sz, j, :], ps_w[abk][0:sz, 0:TTF], gfp[0:sz, :],
                         ALU.mult, waits=[("pe", P.cnt["pe"]), w_rel[abk + 1]])
                    w_rel[abk] = P.mark("vector")
                if emit_mid is not None:
                    emit_mid()
                m_h = P.mark("vector")
                for co in range(NCH):
                    bk = co % 4
                    ps = ps_w[bk]
                    w0 = [m_h, LW] + ([w_rel[bk]] if w_rel[bk] else [])
                    B.mm(ps[:, 0:TTF], biasr[0:1, 22 + co, :], sb_onesrb[0:1, 0:TTF],
                         True, False, waits=w0)
                    for (j, sz) in FFN_BLOCKS:
                        B.mm(ps[:, 0:TTF], w2[0:sz, j, co * 128:(co + 1) * 128],
                             hb[0:sz, j, :], False, j == 21,
                             inc="pe" if j == 21 else None)
                    B.tt("vector", yout[sl][:, co, :], ps[:, 0:TTF],
                         ysh[sl][:, co, 1:TTF + 1], ALU.add,
                         waits=[("pe", P.cnt["pe"])])
                    w_rel[bk] = P.mark("vector")
                m_yo = P.mark("vector")
                P.dma("sync", dstv[:, :, t0:t0 + TTF], yout[sl][:], st, waits=[m_yo])
                slot_rel[sl] = [m_yo, (st, P.cnt[st])]

            stageA(0)
            for i in range(NT):
                stageB(i, emit_mid=(lambda i=i: stageA(i + 1)) if i + 1 < NT else None)
            P.fence()

    IDX16 = B.dram("IDX16", [H, T], F32)

    def attn_b(x_src, x_dst):
        with ExitStack() as ph:
            sbp = lambda nm, shp, dt=F32: ph.enter_context(nc.sbuf_tensor(B.unique(nm), shp, dt))
            psp = lambda nm, shp: ph.enter_context(nc.psum_tensor(B.unique(nm), shp, F32))
            lw = P.sem("ab_lw")
            wkt = sbp("abwk", [128, NCH, D])
            wsb = sbp("abws", [128, NCH, H])
            sb_U = sbp("abU", [128, NCH, H])
            idxrow = sbp("abir", [H, 512])
            ps_u = psp("abpu", [128, H])
            ps_ix = psp("abpi", [H, 512])
            P.dma("sync", wkt[:], bWk_orig.rearrange("(ce p) d -> p ce d", p=128), lw)
            P.dma("sync", wsb[:], bwsblk.rearrange("(ce p) h -> p ce h", p=128), lw)
            LW = (lw, P.cnt[lw])
            u_rel = [None]
            for dc in range(NCH):
                for ce in range(NCH):
                    B.mm(ps_u[:], wkt[:, ce, dc * 128:(dc + 1) * 128], wsb[:, ce, :],
                         ce == 0, ce == NCH - 1,
                         waits=([LW] + ([u_rel[0]] if u_rel[0] else [])) if ce == 0 else (),
                         inc="pe" if ce == NCH - 1 else None)
                B.scopy(sb_U[:, dc, :], ps_u[:], waits=[("pe", P.cnt["pe"])])
                u_rel[0] = P.mark("scalar")
            ix_rel = [None]

            def idx_hook(i, t0, of_tile, m_ready):
                for c in range(NCH):
                    B.mm(ps_ix[:], sb_U[:, c, :], of_tile[:, c, :], c == 0, c == NCH - 1,
                         waits=([m_ready, u_rel[0]] + ([ix_rel[0]] if ix_rel[0] else []))
                         if c == 0 else (),
                         inc="pe" if c == NCH - 1 else None)
                B.scopy(idxrow[:], ps_ix[:], waits=[("pe", P.cnt["pe"])])
                m = P.mark("scalar")
                P.dma("sync", IDX16[:, t0:t0 + 512], idxrow[:], "ab_ix", waits=[m])
                ix_rel[0] = m

            ln_pass("ln3", x_src, dst_bf=XNB, idx_hook=idx_hook)
        proj_pass("pj3", [(bWqT, NCH), (bWkT, NCH), (bWvT, NCH)], XNB,
                  [("fm", QS), ("tm", KTM), ("tm", VTM)])
        with ExitStack() as pho:
            sbo = lambda nm, shp, dt=F32: pho.enter_context(nc.sbuf_tensor(B.unique(nm), shp, dt))
            ksp = sbo("abks", [128, NCH, 64], BF16)
            vsp = sbo("abvs", [64, H, 64], BF16)
            rank_tm = sbo("abrt", [128, 64, H])
            with ExitStack() as ph:
                sbp = lambda nm, shp, dt=F32: ph.enter_context(nc.sbuf_tensor(B.unique(nm), shp, dt))
                psp = lambda nm, shp: ph.enter_context(nc.psum_tensor(B.unique(nm), shp, F32))
                sbI = sbp("absi", [128, 1024])
                idx16 = sbp("abix", [H, T])
                mask16 = sbp("abmk", [H, T])
                rank16 = sbp("abrk", [H, T])
                st8 = sbp("abst", [H, 8])
                rowb = sbp("abrw", [1, 128])
                rowc = sbp("abrc", [1, 16])
                thb = sbp("abth", [128, 1])
                msk = sbp("abms", [128, 1024])
                cnt_p = sbp("abcp", [128, 1])
                ps_row = psp("abpr", [1, 128])
                ps_col = psp("abpc", [16, 1])
                ps_th = psp("abpt", [128, 1])
                ps_tr = psp("abptr", [128, 16])
                ldi = P.sem("ab_ldi")
                P.dma("sync", sbI[:],
                      bass.AP(IDX16.tensor, IDX16.offset,
                              [[T, 16], [1024, 8], [1, 1024]]), ldi)
                P.dma("sync", idx16[:], IDX16[:, :], ldi)
                ldw = [(ldi, P.cnt[ldi])]

                def row_to_col(src_row16, dst_col, extra=()):
                    B.mm(ps_col[:], src_row16, sb_onescf[0:1, 0:1], True, True,
                         waits=list(extra) + [P.mark("scalar"), P.mark("vector")], inc="pe")
                    B.scopy(dst_col, ps_col[:], waits=[("pe", P.cnt["pe"])])
                    return P.mark("scalar")

                def part_to_head(col128, dst_row16, op, extra=()):
                    B.mm(ps_row[:], col128, sb_identf[:], True, True,
                         waits=list(extra) + [P.mark("scalar")], inc="pe")
                    B.scopy(rowb[:], ps_row[:], waits=[("pe", P.cnt["pe"])])
                    m = P.mark("scalar")
                    B.red(dst_row16, rowb[0:1, :].rearrange("a (h b) -> a h b", b=8),
                          op, waits=[m])
                    return P.mark("vector")

                B.red(cnt_p[:], sbI[:], ALU.max, waits=ldw)
                m1 = part_to_head(cnt_p[:], rowc[:], ALU.max, extra=[P.mark("vector")])
                B.ts("vector", rowc[:], rowc[:], 1.0, ALU.add)
                row_to_col(rowc[:], st8[:, 1:2], extra=[P.mark("vector")])
                B.red(cnt_p[:], sbI[:], ALU.min, waits=ldw + [P.mark("scalar")])
                m2 = part_to_head(cnt_p[:], rowc[:], ALU.min, extra=[P.mark("vector")])
                B.ts("vector", rowc[:], rowc[:], -1.0, ALU.add)
                row_to_col(rowc[:], st8[:, 0:1], extra=[P.mark("vector")])
                for it in range(45):
                    B.tt("vector", st8[:, 2:3], st8[:, 0:1], st8[:, 1:2], ALU.add,
                         waits=[P.mark("scalar")])
                    B.ts("vector", st8[:, 2:3], st8[:, 2:3], 0.5, ALU.mult)
                    m_mid = P.mark("vector")
                    B.mm(ps_th[:], sb_gselT[:], st8[:, 2:3], True, True,
                         waits=[m_mid, P.mark("scalar")], inc="pe")
                    B.scopy(thb[:], ps_th[:], waits=[("pe", P.cnt["pe"])])
                    m_th = P.mark("scalar")
                    B.ts("vector", msk[:], sbI[:], thb[:], ALU.is_ge, waits=[m_th])
                    B.red(cnt_p[:], msk[:], ALU.add)
                    part_to_head(cnt_p[:], rowc[:], ALU.add, extra=[P.mark("vector")])
                    row_to_col(rowc[:], st8[:, 3:4], extra=[P.mark("vector")])
                    B.ts("vector", st8[:, 4:5], st8[:, 3:4], 64.0, ALU.is_ge,
                         waits=[P.mark("scalar")])
                    B.tt("vector", st8[:, 5:6], st8[:, 2:3], st8[:, 0:1], ALU.subtract)
                    B.tt("vector", st8[:, 5:6], st8[:, 5:6], st8[:, 4:5], ALU.mult)
                    B.tt("vector", st8[:, 0:1], st8[:, 0:1], st8[:, 5:6], ALU.add)
                    B.tt("vector", st8[:, 6:7], st8[:, 1:2], st8[:, 2:3], ALU.subtract)
                    B.tt("vector", st8[:, 6:7], st8[:, 6:7], st8[:, 4:5], ALU.mult)
                    B.tt("vector", st8[:, 1:2], st8[:, 2:3], st8[:, 6:7], ALU.add)
                B.ts("vector", mask16[:], idx16[:], st8[:, 0:1], ALU.is_ge,
                     waits=ldw + [P.mark("vector")])
                ones_b = bass.AP(sb_onescf[:].tensor, sb_onescf[:].offset,
                                 [list(sb_onescf[:].ap[0])[:1] + [16], [0, T]])
                B.scan(rank16[:], mask16[:], ones_b, 0.0, ALU.add, ALU.mult)
                B.tt("vector", rank16[:], rank16[:], mask16[:], ALU.mult)
                m_rk = P.mark("vector")
                tr_rel = None
                for tc in range(64):
                    B.tr(ps_tr[:], rank16[0:16, tc * 128:(tc + 1) * 128],
                         sb_identf[0:16, 0:16],
                         waits=[m_rk] + ([tr_rel] if tr_rel else []), inc="pe")
                    B.scopy(rank_tm[:, tc, :], ps_tr[:], waits=[("pe", P.cnt["pe"])])
                    tr_rel = P.mark("scalar")
                P.fence()
            with ExitStack() as ph:
                sbp = lambda nm, shp, dt=F32: ph.enter_context(nc.sbuf_tensor(B.unique(nm), shp, dt))
                kvt = [sbp("abkv", [128, 2, D], BF16) for _ in range(2)]
                Sg = sbp("absg", [128, 4, 64], BF16)
                ps_c = [ph.enter_context(nc.psum_tensor(B.unique("abpc2"), [64, 512], F32))
                        for _ in range(8)]
                ldk = P.sem("ab_ldk")
                ktmv, vtmv = tm(KTM), tm(VTM)
                slot_rel2 = [None, None]
                for pass_ in range(4):
                    heads = [4 * pass_ + z for z in range(4)]
                    for tc in range(64):
                        sl = tc % 2
                        w = list(slot_rel2[sl]) if slot_rel2[sl] else []
                        P.dma("sync", kvt[sl][:, 0, :], ktmv[:, tc, :], ldk, waits=w)
                        P.dma("sync", kvt[sl][:, 1, :], vtmv[:, tc, :], ldk, waits=w)
                        ldw2 = [(ldk, P.cnt[ldk])]
                        for z, h in enumerate(heads):
                            B.ts("vector", Sg[:, z, :], sb_iota64[:],
                                 rank_tm[:, tc, h:h + 1], ALU.is_equal,
                                 waits=(ldw2 + [P.mark("tensor") if False else
                                        ("pe", P.cnt["pe"])]) if z == 0 else ())
                        m_sg = P.mark("vector")
                        for z, h in enumerate(heads):
                            B.mm(ps_c[2 * z][:, 0:64], kvt[sl][:, 0, 64 * h:64 * h + 64],
                                 Sg[:, z, :], tc == 0, tc == 63,
                                 waits=[m_sg] + ldw2,
                                 inc="pe" if tc == 63 else None)
                            B.mm(ps_c[2 * z + 1][:, 0:64], Sg[:, z, :],
                                 kvt[sl][:, 1, 64 * h:64 * h + 64], tc == 0, tc == 63,
                                 inc="pe" if tc == 63 else None)
                        slot_rel2[sl] = [("pe", P.cnt["pe"]), m_sg]
                    pe_f = ("pe", P.cnt["pe"])
                    for z, h in enumerate(heads):
                        B.scopy(ksp[64 * (h % 2):64 * (h % 2) + 64, h // 2, :],
                                ps_c[2 * z][:, 0:64], waits=[pe_f])
                        B.scopy(vsp[0:64, h, :], ps_c[2 * z + 1][:, 0:64], waits=[pe_f])
                    slot_rel2 = [[P.mark("scalar")], [P.mark("scalar")]]
                P.fence()
            with ExitStack() as ph:
                sbp = lambda nm, shp, dt=F32: ph.enter_context(nc.sbuf_tensor(B.unique(nm), shp, dt))
                psp = lambda nm, shp: ph.enter_context(nc.psum_tensor(B.unique(nm), shp, F32))
                lw2 = P.sem("ab_lw2")
                w_oT = sbp("abwo", [128, NCH, D], BF16)
                P.dma("gpsimd", w_oT[:], bWoT.rearrange("(c p) o -> p c o", p=128), lw2)
                LW2 = (lw2, P.cnt[lw2])
                qt = [sbp("abqt", [128, NCH, 512], BF16) for _ in range(2)]
                xr = [sbp("abxr", [128, NCH, 512], F32) for _ in range(2)]
                aout = sbp("abao", [128, NCH, 512], BF16)
                yout = [sbp("abyo", [128, NCH, 512], F32) for _ in range(2)]
                probs_all = sbp("abpb", [128, 64, 64], BF16)
                prT = [sbp("abpt2", [64, 128], BF16) for _ in range(2)]
                sml = sbp("absm", [128, 256])
                ps_s = [psp("abps", [128, 64]) for _ in range(2)]
                ps_t = [ph.enter_context(nc.psum_tensor(B.unique("abptx"), [64, 128], BF16))
                        for _ in range(2)]
                ps_o = psp("abpo", [128, 128])
                ps_mm = [psp("abpm", [128, 512]) for _ in range(2)]
                ldq = P.sem("ab_ldq")
                stx = P.sem("ab_stx")
                qsv = fm(QS)
                xrv, xdv = fm(x_src), fm(x_dst)
                slot_rel3 = [None, None]
                sim_rel = [None, None]
                prT_rel = [None, None]
                tr_rel = [None, None]
                pvo_rel = None
                mm_rel = [None, None]
                tile_pv_guard = None
                for i in range(16):
                    sl = i % 2
                    t0 = i * 512
                    w = list(slot_rel3[sl]) if slot_rel3[sl] else [LW2]
                    P.dma("sync", qt[sl][:], qsv[:, :, t0:t0 + 512], ldq, waits=w)
                    P.dma("sync", xr[sl][:], xrv[:, :, t0:t0 + 512], ldq, waits=w)
                    ldw3 = [(ldq, P.cnt[ldq])]
                    units = [(qsub, c, par) for qsub in range(4)
                             for c in range(NCH) for par in range(2)]
                    NU = len(units)
                    pes_u = [None] * NU
                    mb_u = [None] * NU
                    for step in range(NU + 1):
                        if step < NU:
                            u = step
                            qsub, c, par = units[u]
                            base = 64 * par
                            ps = ps_s[u % 2]
                            w0 = ldw3 + ([sim_rel[u % 2]] if sim_rel[u % 2] else [])
                            B.mm(ps[:], qt[sl][base:base + 64, c, qsub * 128:(qsub + 1) * 128],
                                 ksp[base:base + 64, c, :], True, True, waits=w0, inc="pe")
                            pes_u[u] = ("pe", P.cnt["pe"])
                            B.red(sml[:, 4 * u:4 * u + 1], ps[:], ALU.max, negate=True,
                                  waits=[pes_u[u]] + ([P.mark("scalar")] if u == 0 else []))
                            B.ts("vector", sml[:, 4 * u + 1:4 * u + 2],
                                 sml[:, 4 * u:4 * u + 1], 0.125, ALU.mult)
                            mb_u[u] = P.mark("vector")
                        if step >= 1:
                            v = step - 1
                            qsub, c, par = units[v]
                            base = 64 * par
                            h = 2 * c + par
                            ps = ps_s[v % 2]
                            prv = probs_all[:, v, :]
                            w1 = [mb_u[v], pes_u[v]]
                            if tile_pv_guard is not None:
                                w1.append(tile_pv_guard)
                            B.act(prv, ps[:], AF.Exp, scale=0.125,
                                  bias=sml[:, 4 * v + 1:4 * v + 2],
                                  accum=sml[:, 4 * v + 2:4 * v + 3], waits=w1)
                            sim_rel[v % 2] = P.mark("scalar")
                            B.recip(sml[:, 4 * v + 3:4 * v + 4],
                                    sml[:, 4 * v + 2:4 * v + 3], waits=[sim_rel[v % 2]])
                            B.ts("vector", prv, prv, sml[:, 4 * v + 3:4 * v + 4], ALU.mult)
                            m_sc = P.mark("vector")
                            B.tr(ps_t[v % 2][:], prv, sb_ident[:],
                                 waits=[m_sc] + ([tr_rel[v % 2]] if tr_rel[v % 2] else []),
                                 inc="pe")
                            prb = prT[v % 2]
                            B.scopy(prb[:], ps_t[v % 2][:],
                                    waits=[("pe", P.cnt["pe"])] +
                                    ([prT_rel[v % 2]] if prT_rel[v % 2] else []))
                            tr_rel[v % 2] = P.mark("scalar")
                            m_pt = P.mark("scalar")
                            B.mm(ps_o[base:base + 64, :], vsp[0:64, h, :], prb[:],
                                 True, True,
                                 waits=[m_pt] + ([pvo_rel] if (par == 0 and pvo_rel) else []),
                                 inc="pe")
                            prT_rel[v % 2] = ("pe", P.cnt["pe"])
                            if par == 1:
                                B.act(aout[:, c, qsub * 128:(qsub + 1) * 128], ps_o[:],
                                      AF.Copy, waits=[("pe", P.cnt["pe"])])
                                pvo_rel = P.mark("scalar")
                    tile_pv_guard = ("pe", P.cnt["pe"])
                    m_ao = P.mark("scalar")
                    for co in range(NCH):
                        bk = co % 2
                        ps = ps_mm[bk]
                        w0 = [m_ao, LW2] + ([mm_rel[bk]] if mm_rel[bk] else [])
                        for ci in range(NCH):
                            B.mm(ps[:], w_oT[:, ci, co * 128:(co + 1) * 128],
                                 aout[:, ci, :], ci == 0, ci == NCH - 1,
                                 waits=w0 if ci == 0 else (),
                                 inc="pe" if ci == NCH - 1 else None)
                        B.tt("vector", yout[sl][:, co, :], ps[:],
                             xr[sl][:, co, :], ALU.add, waits=[("pe", P.cnt["pe"])])
                        mm_rel[bk] = P.mark("vector")
                    m_yo = P.mark("vector")
                    P.dma("sync", xdv[:, :, t0:t0 + 512], yout[sl][:], stx, waits=[m_yo])
                    slot_rel3[sl] = [m_yo, (stx, P.cnt[stx])]
                P.fence()

    ln_pass("ln1", xin, dst_bf=XNB)
    if "xnb" in debug_taps:
        P.dma("sync", dbg["dbg_xnb"][:, :], XNB[:, :], "dbgst")
        P.fence()
    if stop_after == "ln1":
        P.dma("sync", out[:, :], xin[:, :], "stout")
        P.fence()
        return nc, finish(B)

    proj_pass("pj1", [(aQw, NCH), (aKw, NCH), (aVw, NCH)], XNB,
              [("fm", QS), ("fm", KS), ("tm", VTM)])
    if "qkv" in debug_taps:
        P.dma("sync", dbg["dbg_q"][:, :], QS[:, :], "dbgst")
        P.dma("sync", dbg["dbg_k"][:, :], KS[:, :], "dbgst")
        P.dma("sync", dbg["dbg_v"][:, :], VTM[:, :], "dbgst")
        P.fence()
    if stop_after == "proj1":
        P.dma("sync", out[:, :], xin[:, :], "stout")
        P.fence()
        return nc, finish(B)

    attn_a_flash(xin, S_a)
    cur = S_a
    if stop_after == "sub1":
        P.dma("sync", out[:, :], cur[:, :], "stout")
        P.fence()
        return nc, finish(B)

    def inner_block(tag, cur):
        ln_pass("ln_" + tag, cur, dst_f32=S0)
        y, alt = S0, S1
        nsub = DEPTH * 2
        if stop_after and stop_after.startswith("inner"):
            nsub = min(nsub, int(stop_after[5:]))
        for l in range(DEPTH):
            if 2 * l >= nsub:
                break
            inner_attn(l, y, alt)
            y, alt = alt, y
            if 2 * l + 1 >= nsub:
                break
            inner_ffn(l, y, alt)
            y, alt = alt, y
        return y

    y = inner_block("i2", cur)
    if stop_after and stop_after.startswith("inner"):
        P.dma("sync", out[:, :], y[:, :], "stout")
        P.fence()
        return nc, finish(B)
    ln_pass("add2", cur, src2=y, dst_sum=S_b, no_norm=True)
    cur = S_b
    if stop_after == "sub2":
        P.dma("sync", out[:, :], cur[:, :], "stout")
        P.fence()
        return nc, finish(B)

    attn_b(cur, S_a)
    cur = S_a
    if stop_after == "sub3":
        P.dma("sync", out[:, :], cur[:, :], "stout")
        P.fence()
        return nc, finish(B)

    y = inner_block("i4", cur)
    ln_pass("add4", cur, src2=y, dst_sum=out, no_norm=True)
    P.fence()
    return nc, finish(B)


def finish(B):
    B.P.build(B.nc)
    B.es.close()
    B.nc.finalize()
    return B



_BUILD_CACHE = {}


def _host_constants():
    import ml_dtypes
    bf = ml_dtypes.bfloat16
    C = {}
    half = np.arange(0, HD, 2, dtype=np.float64)
    inv_freq = 1.0 / 10000.0 ** (half / HD)
    t = np.arange(T, dtype=np.float64)
    freqs = t[:, None] * inv_freq[None, :]
    freqs = np.concatenate([freqs, freqs], -1)
    base_scale = (half + 0.4 * HD) / (1.4 * HD)
    power = (t - T // 2) / (256 // 2)
    scale = base_scale[None, :] ** power[:, None]
    scale = np.concatenate([scale, scale], -1)
    C["tbl_cs"] = np.ascontiguousarray((np.cos(freqs) * scale).T.astype(np.float32))
    C["tbl_ss"] = np.ascontiguousarray((np.sin(freqs) * scale).T.astype(np.float32))
    C["tbl_ci"] = np.ascontiguousarray((np.cos(freqs) / scale).T.astype(np.float32))
    C["tbl_si"] = np.ascontiguousarray((np.sin(freqs) / scale).T.astype(np.float32))
    C["c_ident"] = np.eye(128, dtype=np.float32).astype(bf)
    C["c_identf"] = np.eye(128, dtype=np.float32)
    ii = np.arange(128)[:, None]
    cc = np.arange(1024)[None, :]
    C["c_stair"] = np.where(cc > ii + 384, np.float32(NEG), np.float32(0)).astype(bf)
    il = np.arange(128)[:, None]
    jl = np.arange(384)[None, :]
    band = (jl < il) | (jl > il + 256)
    b0 = band | (jl < 256)
    b1 = band | (jl < 128)
    C["c_band"] = np.stack([
        np.where(band, np.float32(NEG), np.float32(0)),
        np.where(b0, np.float32(NEG), np.float32(0)),
        np.where(b1, np.float32(NEG), np.float32(0))]).astype(bf)
    prot = np.zeros((128, 128), np.float32)
    for b in range(2):
        o = 64 * b
        for m in range(32):
            prot[o + m + 32, o + m] = -1.0
        for m in range(32, 64):
            prot[o + m - 32, o + m] = 1.0
    C["c_prot"] = prot.astype(bf)
    C["c_ones1"] = np.ones((128, 1), np.float32).astype(bf)
    C["c_onesr"] = np.ones((1, 128), np.float32)
    C["c_onescf"] = np.ones((128, 1), np.float32)
    C["c_iota64"] = np.tile(np.arange(1, HD + 1, dtype=np.float32)[None, :], (128, 1))
    hsel = np.zeros((NCH, 128, H), np.float32)
    for c in range(NCH):
        for p in range(128):
            hsel[c, p, 2 * c + p // 64] = 1.0
    C["c_hsel"] = hsel.astype(bf)
    hsel2 = np.zeros((128, 2), np.float32)
    hsel2[np.arange(128), np.arange(128) // 64] = 1.0
    C["c_hsel2"] = hsel2.astype(bf)
    C["c_pairsel"] = np.ascontiguousarray(hsel2.T.astype(np.float32))
    C["c_hselT"] = np.ascontiguousarray(hsel.transpose(2, 0, 1)).astype(bf)
    gsel = np.zeros((128, H), np.float32)
    for p in range(128):
        gsel[p, p // 8] = 1.0
    C["c_gsel"] = gsel.astype(bf)
    C["c_gselT"] = np.ascontiguousarray(gsel.T.astype(np.float32))
    fv = np.zeros(8, np.float32)
    fv[:len(FVALS)] = FVALS
    C["c_fvals"] = np.tile(fv[None, :], (128, 1))
    C["c_onesrb"] = np.ones((1, 512), np.float32).astype(bf)
    return C


def _host_weights(inputs):
    W = {}
    f32 = lambda a: np.ascontiguousarray(np.asarray(a, dtype=np.float32))
    W["aQw"] = f32(inputs["L0_attn_a_Qw"])
    W["aKw"] = f32(inputs["L0_attn_a_Kw"])
    W["aVw"] = f32(inputs["L0_attn_a_Vw"])
    W["bWqT"] = f32(np.asarray(inputs["L0_attn_b_Wq"]).T)
    W["bWkT"] = f32(np.asarray(inputs["L0_attn_b_Wk"]).T)
    W["bWvT"] = f32(np.asarray(inputs["L0_attn_b_Wv"]).T)
    W["bWoT"] = f32(np.asarray(inputs["L0_attn_b_Wo"]).T)
    W["bWk_orig"] = f32(inputs["L0_attn_b_Wk"])
    ws = np.asarray(inputs["L0_attn_b_w_score"], dtype=np.float32)
    wsblk = np.zeros((D, H), np.float32)
    for h in range(H):
        wsblk[64 * h:64 * h + 64, h] = ws
    W["bwsblk"] = wsblk
    W["fWqkv"] = f32(inputs["L0_ffn_Wqkv"])
    W["fWo"] = f32(inputs["L0_ffn_Wo"])
    W1 = f32(inputs["L0_ffn_W1"])
    b1 = f32(inputs["L0_ffn_b1"])
    blocks, sizes = [], []
    for j in range(21):
        blocks.append(np.arange(128 * j, 128 * (j + 1)))
        sizes.append(128)
    blocks.append(np.arange(2688, 2730))
    sizes.append(42)
    perm = []
    for blk in blocks:
        perm.extend(blk.tolist())
        perm.extend((blk + DI).tolist())
    perm = np.asarray(perm)
    W["fW1"] = np.ascontiguousarray(W1[:, :, perm])
    W["fW2"] = f32(inputs["L0_ffn_W2"])
    b1p = b1[:, perm]
    b1cm = np.zeros((DEPTH, 128, 44), np.float32)
    off = 0
    for j, sz in enumerate(sizes):
        b1cm[:, 0:sz, 2 * j] = b1p[:, off:off + sz]
        b1cm[:, 0:sz, 2 * j + 1] = b1p[:, off + sz:off + 2 * sz]
        off += 2 * sz
    W["fb1"] = b1cm
    import ml_dtypes as _md
    _bf = _md.bfloat16
    b1r = np.zeros((DEPTH, 44, 128), np.float32)
    off = 0
    for j, sz in enumerate(sizes):
        b1r[:, 2 * j, 0:sz] = b1p[:, off:off + sz]
        b1r[:, 2 * j + 1, 0:sz] = b1p[:, off + sz:off + 2 * sz]
        off += 2 * sz
    fbr = np.zeros((DEPTH, 2, 30, 128), np.float32)
    fbr[:, 0, 0:22, :] = b1r[:, 0::2, :]
    fbr[:, 1, 0:22, :] = b1r[:, 1::2, :]
    fbr[:, 0, 22:30, :] = np.asarray(inputs["L0_ffn_b2"], np.float32).reshape(DEPTH, NCH, 128)
    W["c_fbr"] = fbr.astype(_bf)
    cm = lambda v: np.ascontiguousarray(
        np.asarray(v, np.float32).reshape(DEPTH, NCH, 128).transpose(0, 2, 1))
    W["fb2"] = cm(inputs["L0_ffn_b2"])
    W["flng"] = cm(inputs["L0_ffn_ln_g"])
    W["flnb"] = cm(inputs["L0_ffn_ln_b"])
    qs = np.asarray(inputs["L0_ffn_q_scale"], np.float32)
    ks = np.asarray(inputs["L0_ffn_k_scale"], np.float32)

    def qcols(v):
        a = np.tile(v, (1, 2))
        b = np.tile(np.concatenate([v[:, 32:], v[:, :32]], 1), (1, 2))
        return np.ascontiguousarray(np.stack([a, b], axis=-1))
    W["fqs"] = qcols(qs)
    W["fks"] = qcols(ks)
    return W


DEV_NCORES = None


def kernel(stop_after="all", debug_taps=(), **inputs):
    key = (stop_after, tuple(sorted(debug_taps)))
    if key not in _BUILD_CACHE:
        _BUILD_CACHE[key] = build_program(stop_after=stop_after, debug_taps=debug_taps)
    nc, _ = _BUILD_CACHE[key]
    C = _host_constants()
    W = _host_weights(inputs)
    x = np.asarray(inputs["L0_attn_a_x"], dtype=np.float32)
    ncores = DEV_NCORES or NCORES
    in_maps = []
    for b in range(ncores):
        m = dict(C)
        m.update(W)
        m["xin"] = np.ascontiguousarray(x[b].T)
        in_maps.append(m)
    res = run_bass_kernel_spmd(nc, in_maps, core_ids=list(range(ncores)))
    outs = [np.asarray(r["xout"]).T for r in res.results]
    full = np.stack(outs, axis=0).astype(np.float32)
    if debug_taps:
        return full, res
    return full
